# version 07
# baseline (speedup 1.0000x reference)
import math
import numpy as np
import concourse.bass as bass
import concourse.mybir as mybir
from concourse.bass_utils import run_bass_kernel_spmd
from concourse.alu_op_type import AluOpType as OP
from contextlib import ExitStack

F32 = mybir.dt.float32
F16 = mybir.dt.float16
AF = mybir.ActivationFunctionType

NT = 32
LN_EPS = 1e-5
C1 = 0.5 / math.sqrt(128.0)
WC2Q_SCALE = 1.0 / C1
LAM = 0.05
SIN_SCALE = math.pi / LAM
PI = math.pi
ENGS = ("pe", "act", "dve", "pool", "syncq")
CSEMS = ("pe", "act", "dve", "pool")


class K:
    def __init__(self, dbg=()):
        self.nc = bass.Bass("TRN2", target_bir_lowering=False)
        self.dbg = dbg
        self.q = {e: [] for e in ENGS}
        self.pe_labels = {}
        self.cnt = {}
        self.sem = {}
        self.lastw = {}

    def op(self, eng, fn, inc=None, n=1):
        if eng in ("act", "dve", "pool"):
            if self.cnt[eng] > 0:
                self.wait(eng, eng, self.cnt[eng])
            if inc is None:
                inc = eng
        elif eng == "pe" and inc is None:
            inc = "pe"
        if inc is None:
            self.q[eng].append(fn)
            return None
        tok = self.cnt[inc] + n
        self.cnt[inc] = tok
        if inc == "pe":
            import traceback as _tb
            for fr in reversed(_tb.extract_stack(limit=8)):
                if fr.name.startswith(("_phase", "_wfuse", "_gf", "_cprep",
                                       "_init", "_load")):
                    self.pe_labels[tok] = (fr.name, fr.lineno)
                    break
        sem = self.sem[inc]
        self.q[eng].append(lambda e, fn=fn, sem=sem, n=n: fn(e).then_inc(sem, n))
        return tok

    def wait(self, eng, semname, tok):
        if not tok or tok <= 0:
            return
        key = (eng, semname)
        if self.lastw.get(key, 0) >= tok:
            return
        self.lastw[key] = tok
        sem = self.sem[semname]
        self.q[eng].append(lambda e, sem=sem, tok=tok: e.wait_ge(sem, tok))

    STSEMS = ("sx0", "sx1", "sg", "sa0", "sa1", "so0", "so1")

    def fence(self):
        snap = {s: self.cnt[s] for s in CSEMS}
        stsnap = {s: self.cnt[s] for s in K.STSEMS}
        for eng in ("pe", "act", "dve", "pool", "syncq"):
            for s in CSEMS:
                if eng != s:
                    self.wait(eng, s, snap[s])
            for s in K.STSEMS:
                self.wait(eng, s, stsnap[s])

    def dld(self, semname, gate, fn, queue="syncq"):
        if gate is not None and gate[1]:
            self.wait(queue, gate[0], gate[1])
        return self.op(queue, fn, inc=semname, n=16)

    def dil(self, fn):
        tok = self.dld("di", self.di_gate, fn)
        return tok

    def di_done(self, eng_sem, tok):
        self.di_gate = (eng_sem, tok)

    @staticmethod
    def mm(psl, lhsT, rhs, start=True, stop=True):
        return lambda e: e.matmul(psl, lhsT, rhs, start=start, stop=stop)

    @staticmethod
    def tp(psl, in_, idn):
        return lambda e: e.transpose(psl, in_, idn)

    @staticmethod
    def act(out, in_, f, bias=0.0, scale=1.0, accum=None):
        return lambda e: e.activation(out, in_, f, bias=bias, scale=scale,
                                      accum_out=accum)

    @staticmethod
    def tt(out, a, c, op):
        return lambda e: e.tensor_tensor(out, a, c, op=op)

    @staticmethod
    def ts(out, a, s1, s2=None, op0=OP.mult, op1=None):
        if s2 is None:
            return lambda e: e.tensor_scalar(out, a, s1, None, op0=op0)
        return lambda e: e.tensor_scalar(out, a, s1, s2, op0=op0, op1=op1)

    @staticmethod
    def stt(out, in0, sc, in1, op0, op1):
        return lambda e: e.scalar_tensor_tensor(out, in0, sc, in1, op0=op0, op1=op1)

    @staticmethod
    def cp(out, in_):
        return lambda e: e.tensor_copy(out, in_)

    @staticmethod
    def acp(out, in_):
        return lambda e: e.copy(out, in_)

    @staticmethod
    def dma(out, in_):
        return lambda e: e.dma_start(out, in_)

    @staticmethod
    def dmas(out, in_):
        return lambda e: e.dma_start(out, in_, allow_slow_non_contiguous=True)

    @staticmethod
    def ap(t, off, dims):
        return bass.AP(t, off, dims)

    @staticmethod
    def f32ap(t16, off32, dims32):
        d16 = [[s * 2, c] for s, c in dims32[:-1]] + [[1, dims32[-1][1] * 2]]
        return bass.AP(t16, off32 * 2, d16).bitcast(F32)

    def build(self):
        nc = self.nc

        def din(name, shape):
            return nc.dram_tensor(name, shape, F32, kind="ExternalInput").ap()

        d = self.d = {}
        d["x"] = din("L0_attn_x", [4096, 1024])
        for nm, sh in [("Wq", [8, 1024, 128]), ("Wk", [8, 1024, 128]),
                       ("Wv", [8, 1024, 128]), ("Wc2q", [8, 128, 8]),
                       ("Wo", [1024, 1024]), ("W1", [1024, 4096]),
                       ("Wm", [4096, 8]), ("Wqo", [8, 4096]), ("W2", [4096, 1024]),
                       ("bq", [8, 128]), ("bk", [8, 128]), ("bv", [8, 128]),
                       ("bc2q", [8, 8]), ("bo", [1024]), ("g1", [1024]),
                       ("beta1", [1024]), ("g2", [1024]), ("beta2", [1024]),
                       ("b1", [4096]), ("bm", [8]), ("bqo", [4096]), ("b2", [1024])]:
            d[nm] = din("L0_attn_" + nm, sh)
        for i, sh in enumerate([[1024, 1024], [1024, 512], [512, 256], [256, 1]]):
            d[f"cnw{i+1}"] = din(f"L0_ffn_cn_w{i+1}", sh)
        for i, sh in enumerate([1024, 512, 256, 1]):
            d[f"cnb{i+1}"] = din(f"L0_ffn_cn_b{i+1}", [sh])
        for i, sh in enumerate([[1024, 512], [512, 256], [256, 1]]):
            d[f"gcw{i+1}"] = din(f"L0_ffn_gc_w{i+1}", sh)
        for i, sh in enumerate([512, 256, 1]):
            d[f"gcb{i+1}"] = din(f"L0_ffn_gc_b{i+1}", [sh])
        d["ar"] = din("L0_ffn_adaptive_range", [1])
        d["ident"] = din("c_ident", [128, 128])
        d["out"] = nc.dram_tensor("out", [4096, 1024], F32, kind="ExternalOutput").ap()
        ks = lambda nm: "ExternalOutput" if nm in self.dbg else "Internal"
        d["xn_sc"] = nc.dram_tensor("xn_sc", [4096, 1024], F32, kind=ks("xn_sc")).ap()
        d["g_sc"] = nc.dram_tensor("g_sc", [4096, 1024], F32, kind=ks("g_sc")).ap()
        d["attn_sc"] = nc.dram_tensor("attn_sc", [128, 8 * 4096], F16,
                                      kind=ks("attn_sc")).ap()
        d["rb_sc"] = nc.dram_tensor("rb_sc", [1, 512], F32, kind="Internal").ap()

        with ExitStack() as st:
            self._alloc(st)
            self._emit()
            with nc.Block() as block:
                @block.tensor
                def _(e):
                    for fn in self.q["pe"]:
                        fn(e)

                @block.scalar
                def _(e):
                    for fn in self.q["act"]:
                        fn(e)

                @block.vector
                def _(e):
                    for fn in self.q["dve"]:
                        fn(e)

                @block.gpsimd
                def _(e):
                    for fn in self.q["pool"]:
                        fn(e)

                @block.sync
                def _(e):
                    for fn in self.q["syncq"]:
                        fn(e)
        return nc

    def _alloc(self, st):
        nc = self.nc

        def sb(name, shape, dt):
            t = st.enter_context(nc.sbuf_tensor(name, shape, dt))
            setattr(self, name, t)
            return t

        sb("arA", [128, 32768], F16)
        sb("arB", [128, 16384], F16)
        sb("arC", [128, 16384], F16)
        sb("arD", [128, 6144], F32)
        sb("arE", [128, 13312], F16)
        sb("arW", [128, 8192], F16)
        sb("rows16", [1, 64], F16)
        sb("cols32", [128, 160], F32)
        sb("cols16", [128, 80], F16)
        sb("small32", [128, 192], F32)
        sb("sqt", [128, 512], F32)
        sb("sc32", [1, 16], F32)
        sb("Wm16", [128, 256], F16)
        sb("Wc2qc16", [128, 64], F16)
        sb("Wc2q16", [128, 64], F16)
        sb("Wf16", [128, 64], F16)
        sb("W2f16", [8, 1024], F16)
        sb("WqoT16", [128, 256], F16)
        sb("ident32", [128, 128], F32)
        sb("ones32", [128, 128], F32)
        sb("ones16", [128, 128], F16)
        sb("ldst", [32, 128], F32)
        sb("w2ftc", [128, 64], F32)
        sb("cc", [128, 4], F32)
        self.PSF = st.enter_context(nc.psum_tensor("psf", [128, 3584], F32))
        self.PST = st.enter_context(nc.psum_tensor("pst", [128, 512], F32))
        allsems = ["pe", "act", "dve", "pool", "di",
                   "dx0", "dx1", "dx2", "dw0", "dw1", "dw2",
                   "dg0", "dg1", "dg2", "da0", "da1", "dn0", "dn1",
                   "dxx0", "dxx1", "dgl0", "dgl1",
                   "sx0", "sx1", "sg", "sa0", "sa1", "so0", "so1",
                   "sr0", "sr1"]
        for s in allsems:
            self.sem[s] = st.enter_context(nc.semaphore("s_" + s))
            self.cnt[s] = 0
        self.di_gate = None

        A, B, C, D, E = self.arA, self.arB, self.arC, self.arD, self.arE
        self.xn_fm = A
        self.ein = B
        self.qc_fm = C[:, 0:4096]
        self.k_fm = C[:, 4096:8192]
        self.v_tm = C[:, 8192:12288]
        self.qh = C[0:8, 12288:16384]
        self.kl = D[0:8, 0:2048].bitcast(F16)
        self.s2bp = [E[:, 0:2048], E[:, 2048:4096], E[:, 4096:6144]]
        self.mq16 = [E[:, 6144:7168], E[:, 9216:10240], E[:, 10496:11520]]
        self.e16 = [E[:, 4096:5120], E[:, 5120:6144], E[:, 6144:7168],
                    E[:, 9216:10240]]
        self.attn_t = [E[:, 7168:7680], E[:, 7680:8192]]
        self.rb_s = E[:, 8192:9216].bitcast(F32)
        self.xn32 = [E[:, 9216:11264].bitcast(F32), E[:, 11264:13312].bitcast(F32)]
        self.qk32 = E[:, 10496:11520].bitcast(F32)
        self.bcqpat = E[0:1, 11520:12032]
        self.gcb1r = B[0:1, 2048:2560]
        self.gcb2r = B[0:1, 2560:2816]
        self.mean16 = B[:, 2816:2824]
        self.ucol16 = B[:, 4608:4640]
        self.nsq = self.small32[:, 64:128]
        self.rs = self.small32[:, 128:192]
        self.Dv = lambda o, n: D[:, o:o + n]
        self.rbuf = D[0:1, 320:1344]
        self.cnw2 = E[:, 9216:13312]
        self.cnw3 = D[:, 5120:5632].bitcast(F16)
        self.cnw4 = self.sqt[:, 0:1].bitcast(F16)
        self.gcw2 = D[:, 5632:6144].bitcast(F16)
        self.gcw3 = self.sqt[:, 1:2].bitcast(F16)
        self.og = B[:, 0:8192].bitcast(F32)
        self.h1g = B[:, 8192:12288]
        self.h2g = B[:, 12288:14336]
        self.h3g = B[:, 14336:15360]
        self.out16 = E[:, 0:4096]
        self.twt = [E[:, 4096:6144].bitcast(F32), E[:, 6144:8192].bitcast(F32)]
        self.th16 = E[:, 4096:8192]
        self.h1row = B[0:1, 0:1024].bitcast(F32)
        self.h2row = B[0:1, 1024:1536].bitcast(F32)
        self.h1T = B[:, 1536:1540]
        self.h2T = B[:, 1540:1542]
        self.g1bc = C[:, 0:2048].bitcast(F32)
        self.g2bc = C[:, 2048:4096].bitcast(F32)
        self.b2bc = C[:, 4096:6144].bitcast(F32)
        self.boC = C[0:1, 6144:7168]
        self.be1C = C[0:1, 7168:8192]
        self.bf2C = C[0:1, 8192:9216]
        self.astg = [C[:, 9216:10240], C[:, 10240:11264]]
        self.qeT = C[0:8, 11264:12288]
        self.x1a = lambda t: self.f32ap(A, t * 1024, [[16384, 128], [1, 1024]])
        self.x1aT = lambda t: A[:, 16384 + t * 1024:16384 + (t + 1) * 1024]
        self.preb = [B[:, 0:2048].bitcast(F32), B[:, 2048:4096].bitcast(F32)]
        self.shb = [B[:, 4096:6144].bitcast(F32), B[:, 6144:8192].bitcast(F32)]
        self.outb = [B[:, 8192:10240].bitcast(F32), B[:, 10240:12288].bitcast(F32)]
        self.trashC = E[:, 0:2048].bitcast(F32)

    def _emit(self):
        self._init_consts()
        self._phase_a()
        self._gf_net()
        self.fence()
        self._wfuse1()
        self.fence()
        self._wfuse2()
        self.fence()
        self._load_cn()
        self._phase_g()
        self.fence()
        self._phase_b()
        self.fence()
        self._cprep()
        self.fence()
        self._phase_c()

    def col(self, nm, j0=0, nj=None):
        o, n = self.CL[nm]
        if nj is None:
            nj = n
        return self.cols32[:, o + j0:o + j0 + nj]

    def _init_consts(self):
        b, d = self, self.d
        self.CL = {}
        offc = 0
        for nm, n in [("bq", 8), ("bk", 8), ("bv", 8), ("g1c", 8), ("be1c", 8),
                      ("b1c", 32), ("bqoc", 32), ("cb1", 8), ("cb2", 4), ("cb3", 2),
                      ("b2c", 8), ("bf2c", 8), ("pat2", 2)]:
            self.CL[nm] = (offc, n)
            offc += n
        assert offc <= 160

        tok = b.dil(K.dma(self.ident32[:, :], d["ident"][:, :]))
        b.wait("dve", "di", tok)
        td = b.op("dve", lambda e: e.memset(self.ones32[:, :], 1.0), inc="dve")
        b.di_done("dve", td)
        self.t_ident = tok
        b.op("dve", lambda e: e.memset(self.ones16[:, :], 1.0))
        b.op("dve", lambda e: e.memset(self.col("pat2", 0, 1), 1.0))
        b.op("dve", lambda e: e.memset(self.col("pat2", 1, 1), LAM))
        b.op("dve", lambda e: e.memset(self.cc[:, 0:1], LN_EPS))
        b.op("dve", lambda e: e.memset(self.cc[:, 1:2], 1e-12))
        b.op("dve", lambda e: e.memset(self.cc[:, 2:3], PI / 2.0), inc="dve")

        for srcn, cell in [("ar", 0), ("gcb3", 1), ("cnb4", 2)]:
            tok = b.dil(K.dma(self.sc32[0:1, cell:cell + 1], d[srcn][None, :]))
            b.wait("dve", "di", tok)
            td = b.op("dve", lambda e: e.nop(), inc="dve")
            b.di_done("dve", td)
        b.op("dve", K.ts(self.sc32[0:1, 4:5], self.sc32[0:1, 0:1], 2.0))
        b.op("dve", K.ts(self.sc32[0:1, 7:8], self.sc32[0:1, 0:1], -1.0, 0.1,
                         op0=OP.mult, op1=OP.add))
        b.op("dve", K.ts(self.sc32[0:1, 5:6], self.sc32[0:1, 2:3], 0.5))
        b.op("dve", K.ts(self.sc32[0:1, 6:7], self.sc32[0:1, 1:2], 0.5), inc="dve")

        for src, n, dst in [(d["bm"], 8, self.rows16[0:1, 8:16]),
                            (d["gcb1"], 512, self.gcb1r),
                            (d["gcb2"], 256, self.gcb2r)]:
            tok = b.dil(K.dma(self.rbuf[0:1, 0:n], src[None, :]))
            b.wait("dve", "di", tok)
            td = b.op("dve", K.cp(dst, self.rbuf[0:1, 0:n]), inc="dve")
            b.di_done("dve", td)
        tok = b.dil(K.dma(self.rbuf[0:1, 0:64],
                          K.ap(d["bc2q"].tensor, 0, [[64, 1], [1, 64]])))
        b.wait("dve", "di", tok)
        td = b.op("dve", K.cp(self.cols16[0:1, 0:64], self.rbuf[0:1, 0:64]),
                  inc="dve")
        b.di_done("dve", td)
        self.bc2q16 = self.cols16[0:1, 0:64]
        self.bqo16 = self.cols16[:, 32:64]

        def load_cols(nm, src_vec, L, two_d=False):
            nchunk = L // 128
            if two_d:
                ap_in = K.ap(src_vec.tensor, 0, [[128, nchunk], [1, 128]])
            else:
                ap_in = K.ap(src_vec.tensor, src_vec.offset,
                             [[128, nchunk], [1, 128]])
            tok = b.dil(K.dma(self.ldst[0:nchunk, :], ap_in))
            b.wait("pe", "di", tok)
            b.wait("pe", "di", self.t_ident)
            b.wait("pe", "dve", self.cnt["dve"])
            tpe = b.op("pe", K.tp(self.PST[:, 0:nchunk], self.ldst[0:nchunk, :],
                                  self.ident32[0:nchunk, 0:nchunk]), inc="pe")
            b.di_done("pe", tpe)
            b.wait("dve", "pe", tpe)
            return b.op("dve", K.cp(self.col(nm), self.PST[:, 0:nchunk]), inc="dve")

        for nm, src, L in [("g1c", d["g1"], 1024), ("be1c", d["beta1"], 1024),
                           ("b1c", d["b1"], 4096), ("bqoc", d["bqo"], 4096),
                           ("cb1", d["cnb1"], 1024), ("cb2", d["cnb2"], 512),
                           ("cb3", d["cnb3"], 256), ("b2c", d["b2"], 1024)]:
            load_cols(nm, src, L)
        for nm, src in [("bq", d["bq"]), ("bk", d["bk"]), ("bv", d["bv"])]:
            load_cols(nm, src, 1024, two_d=True)
        tok = self.cnt["dve"]
        b.wait("pool", "dve", tok)
        b.op("pool", K.cp(self.bqo16, self.col("bqoc")), inc="pool")

        tok = b.dil(K.dma(self.Dv(2048, 256),
                          K.ap(d["Wm"].tensor, 0, [[8, 128], [1024, 32], [1, 8]])))
        b.wait("pool", "di", tok)
        tp_ = b.op("pool", K.cp(self.Wm16[:, :], self.Dv(2048, 256)), inc="pool")
        b.di_done("pool", tp_)
        tok = b.dil(K.dma(self.Dv(2304, 64),
                          K.ap(d["Wc2q"].tensor, 0, [[8, 128], [1024, 8], [1, 8]])))
        b.wait("dve", "di", tok)
        b.op("dve", K.ts(self.Wc2qc16[:, :], self.Dv(2304, 64), WC2Q_SCALE),
             inc="dve")
        b.wait("pool", "di", tok)
        tp_ = b.op("pool", K.cp(self.Wc2q16[:, :], self.Dv(2304, 64)), inc="pool")
        b.di_done("pool", tp_)

        self.gcw1 = self.arW[:, 0:4096]
        b.wait("syncq", "dve", self.cnt["dve"])
        tok = b.dil(K.dma(self.arD[:, 0:4096],
                          K.ap(d["gcw1"].tensor, 0,
                               [[512, 128], [512 * 128, 8], [1, 512]])))
        b.wait("pool", "di", tok)
        tp_ = b.op("pool", K.cp(self.gcw1, self.arD[:, 0:4096]), inc="pool")
        b.di_done("pool", tp_)
        tok = b.dil(K.dma(self.Dv(4224, 1024),
                          K.ap(d["gcw2"].tensor, 0,
                               [[256, 128], [256 * 128, 4], [1, 256]])))
        b.wait("pool", "di", tok)
        tp_ = b.op("pool", K.cp(self.gcw2[:, 0:1024], self.Dv(4224, 1024)),
                   inc="pool")
        b.di_done("pool", tp_)
        tok = b.dil(K.dmas(self.Dv(5378, 2),
                           K.ap(d["gcw3"].tensor, 0, [[1, 128], [128, 2], [1, 1]])))
        b.wait("pool", "di", tok)
        self.t_gcinit = b.op("pool", K.cp(self.gcw3[:, 0:2], self.Dv(5378, 2)),
                             inc="pool")
        b.di_done("pool", self.t_gcinit)

    def _phase_a(self):
        b, d = self, self.d
        PS_MEAN = self.PSF[:, 3072:3080]
        trash = self.Dv(3072, 1024)
        xins = [self.Dv(0, 1024), self.Dv(1024, 1024), self.Dv(2048, 1024)]
        sm = self.small32
        self.tA_st = [0] * NT
        tA_ev = {}
        t_mm = [0] * NT
        tA_xn = [0] * NT
        b.wait("syncq", "pool", self.t_gcinit)
        b.wait("act", "pool", self.t_gcinit)
        b.wait("dve", "pool", self.t_gcinit)
        b.wait("pe", "pool", self.t_gcinit)
        for i in range(NT):
            xb = i % 2
            xin = xins[i % 3]
            dxs = "dx%d" % (i % 3)
            txi = b.dld(dxs, ("dve", tA_xn[i - 3] if i >= 3 else 0),
                        K.dma(xin, d["x"][i * 128:(i + 1) * 128, :]))
            b.wait("act", dxs, txi)
            if i >= 2:
                b.wait("act", "dve", tA_xn[i - 2])
            sq = sm[:, 0 + xb:1 + xb]
            ms = sm[:, 2 + xb:3 + xb]
            b.op("act", K.act(trash, xin, AF.Square, accum=sq))
            ta = b.op("act", K.act(trash, xin, AF.Copy, accum=ms), inc="act")
            b.wait("dve", "act", ta)
            m = sm[:, 4 + xb:5 + xb]
            rstd = sm[:, 9 + xb:10 + xb]
            b.op("dve", K.ts(m, ms, 1.0 / 1024.0))
            b.op("dve", K.ts(sm[:, 6:7], sq, 1.0 / 1024.0))
            b.op("dve", K.tt(sm[:, 7:8], m, m, OP.mult))
            td = b.op("dve", K.tt(sm[:, 8:9], sm[:, 6:7], sm[:, 7:8], OP.subtract),
                      inc="dve")
            b.wait("act", "dve", td)
            b.op("act", K.act(sm[:, 8:9], sm[:, 8:9], AF.Ln, bias=self.cc[:, 0:1]))
            ta2 = b.op("act", K.act(rstd, sm[:, 8:9], AF.Exp, scale=-0.5), inc="act")
            b.wait("dve", "act", ta2)
            if i >= 2:
                b.wait("dve", "pe", t_mm[i - 2])
                b.wait("dve", "sx%d" % xb, self.tA_st[i - 2])
            tA_xn[i] = b.op("dve", K.ts(self.xn32[xb], xin, m, rstd,
                                        op0=OP.subtract, op1=OP.mult), inc="dve")
            b.wait("pool", "dve", tA_xn[i])
            self.tA_st[i] = b.op("pool", K.dma(d["xn_sc"][i * 128:(i + 1) * 128, :],
                                               self.xn32[xb]),
                                 inc="sx%d" % xb, n=16)
            b.wait("pe", "dve", tA_xn[i])
            b.wait("pe", "di", self.t_ident)
            for r in range(2):
                if (i, r) != (0, 0):
                    prev = tA_ev[(i, 0)] if r == 1 else tA_ev[(i - 1, 1)]
                    b.wait("pe", "dve", prev)
                tok = None
                for c4 in range(4):
                    c = 4 * r + c4
                    tok = b.op("pe", K.tp(self.PST[:, c4 * 128:(c4 + 1) * 128],
                                          self.xn32[xb][:, c * 128:(c + 1) * 128],
                                          self.ident32[:, :]),
                               inc="pe" if c4 == 3 else None)
                b.wait("dve", "pe", tok)
                dst = K.ap(self.arA, (4 * r) * 4096 + i * 128,
                           [[32768, 128], [4096, 4], [1, 128]])
                src = K.ap(self.PST, 0, [[512, 128], [128, 4], [1, 128]])
                tA_ev[(i, r)] = b.op("dve", K.cp(dst, src), inc="dve")
            for c in range(8):
                t_mm[i] = b.op("pe", K.mm(PS_MEAN[:, c:c + 1],
                                          self.xn32[xb][:, c * 128:(c + 1) * 128],
                                          self.ones32[:, 0:1],
                                          start=(i == 0 and c == 0),
                                          stop=(i == NT - 1 and c == 7)),
                               inc="pe" if c == 7 else None)
        b.wait("dve", "pe", t_mm[NT - 1])
        self.t_mean16 = b.op("dve", K.ts(self.mean16, PS_MEAN, 1.0 / 4096.0),
                             inc="dve")

    def _gf_net(self):
        b = self
        PSG = self.PSF[0:1, 0:512]
        b.wait("pe", "dve", self.t_mean16)
        b.wait("pe", "pool", self.t_gcinit)
        for c in range(8):
            b.op("pe", K.mm(PSG, self.mean16[:, c:c + 1],
                            self.gcw1[:, c * 512:(c + 1) * 512],
                            start=(c == 0), stop=False))
        tok = b.op("pe", K.mm(PSG, self.ones16[0:1, 0:1], self.gcb1r,
                              start=False, stop=True), inc="pe")
        b.wait("act", "pe", tok)
        ta = b.op("act", K.act(self.h1row, PSG, AF.Gelu), inc="act")
        b.wait("pe", "act", ta)
        tok = None
        for c in range(4):
            tok = b.op("pe", K.tp(self.PST[:, c:c + 1],
                                  self.h1row[0:1, c * 128:(c + 1) * 128],
                                  self.ident32[0:1, 0:1]),
                       inc="pe" if c == 3 else None)
        b.wait("dve", "pe", tok)
        td = b.op("dve", K.cp(self.h1T, self.PST[:, 0:4]), inc="dve")
        PSG2 = self.PSF[0:1, 512:768]
        b.wait("pe", "dve", td)
        for c in range(4):
            b.op("pe", K.mm(PSG2, self.h1T[:, c:c + 1],
                            self.gcw2[:, c * 256:(c + 1) * 256],
                            start=(c == 0), stop=False))
        tok = b.op("pe", K.mm(PSG2, self.ones16[0:1, 0:1], self.gcb2r,
                              start=False, stop=True), inc="pe")
        b.wait("act", "pe", tok)
        ta = b.op("act", K.act(self.h2row, PSG2, AF.Gelu), inc="act")
        b.wait("pe", "act", ta)
        tok = None
        for c in range(2):
            tok = b.op("pe", K.tp(self.PST[:, c:c + 1],
                                  self.h2row[0:1, c * 128:(c + 1) * 128],
                                  self.ident32[0:1, 0:1]),
                       inc="pe" if c == 1 else None)
        b.wait("dve", "pe", tok)
        td = b.op("dve", K.cp(self.h2T, self.PST[:, 0:2]), inc="dve")
        PSG3 = self.PSF[0:1, 768:769]
        b.wait("pe", "dve", td)
        tok = None
        for c in range(2):
            tok = b.op("pe", K.mm(PSG3, self.h2T[:, c:c + 1], self.gcw3[:, c:c + 1],
                                  start=(c == 0), stop=(c == 1)),
                       inc="pe" if c == 1 else None)
        b.wait("act", "pe", tok)
        ta = b.op("act", K.act(self.sc32[0:1, 9:10], PSG3, AF.Tanh,
                               bias=self.sc32[0:1, 6:7], scale=0.5), inc="act")
        b.wait("dve", "act", ta)
        self.t_gfc = b.op("dve", K.ts(self.sc32[0:1, 8:9], self.sc32[0:1, 9:10],
                                      0.35, 0.5, op0=OP.mult, op1=OP.add), inc="dve")

    def _wfuse1(self):
        b, d = self, self.d
        psWf = self.PSF[:, 3080:3144]
        psUT = self.PSF[:, 1024:1056]
        psBF1 = self.PSF[0:1, 1536:1544]
        halves = [self.Dv(0, 2048), self.Dv(2048, 2048)]
        w1g = self.arB[:, 0:4096].bitcast(F32)
        w1t16 = self.arB[:, 4096:4608]
        t_used = [0, 0]
        t_ev_last = 0
        for fc in range(8):
            t_h = [0, 0]
            for h in range(2):
                t_h[h] = b.dld("dw%d" % h, ("pe", t_used[h]),
                               K.dma(halves[h],
                                     d["W1"][fc * 128:(fc + 1) * 128,
                                             h * 2048:(h + 1) * 2048]))
            for h in range(2):
                b.wait("dve", "dw%d" % h, t_h[h])
                tg = b.op("dve", K.ts(w1g, halves[h], self.col("g1c", fc, 1)),
                          inc="dve")
                b.wait("pe", "dw%d" % h, t_h[h])
                for ms in range(16):
                    b.op("pe", K.mm(psUT[:, h * 16 + ms:h * 16 + ms + 1],
                                    halves[h][:, ms * 128:(ms + 1) * 128],
                                    self.col("be1c", fc, 1),
                                    start=(fc == 0 and h == 0 and ms == 0),
                                    stop=(fc == 7 and h == 1 and ms == 15)))
                b.wait("pe", "dve", tg)
                last_mm = 0
                for r in range(4):
                    if t_ev_last:
                        b.wait("pe", "dve", t_ev_last)
                    tok = None
                    for j in range(4):
                        ms = 4 * r + j
                        tok = b.op("pe", K.tp(self.PST[:, j * 128:(j + 1) * 128],
                                              w1g[:, ms * 128:(ms + 1) * 128],
                                              self.ident32[:, :]),
                                   inc="pe" if j == 3 else None)
                    b.wait("dve", "pe", tok)
                    t_ev_last = b.op("dve", K.cp(w1t16, self.PST[:, 0:512]),
                                     inc="dve")
                    b.wait("pe", "dve", t_ev_last)
                    for j in range(4):
                        ms = 4 * r + j
                        kk = h * 16 + ms
                        last_mm = b.op("pe", K.mm(
                            psWf[:, fc * 8:(fc + 1) * 8],
                            w1t16[:, j * 128:(j + 1) * 128],
                            self.Wm16[:, kk * 8:(kk + 1) * 8],
                            start=(kk == 0), stop=(kk == 31)),
                            inc="pe" if j == 3 else None)
                t_used[h] = last_mm
        t_wf = self.cnt["pe"]
        b.wait("dve", "pe", t_wf)
        b.op("dve", K.cp(self.Wf16[:, :], psWf))
        b.op("dve", K.tt(self.qk32[:, 0:32], psUT, self.col("b1c"), OP.add))
        tuc = b.op("dve", K.cp(self.ucol16, self.qk32[:, 0:32]), inc="dve")
        b.wait("pe", "dve", tuc)
        tok = None
        for ms in range(32):
            tok = b.op("pe", K.mm(psBF1, self.ucol16[:, ms:ms + 1],
                                  self.Wm16[:, ms * 8:(ms + 1) * 8],
                                  start=(ms == 0), stop=(ms == 31)),
                       inc="pe" if ms == 31 else None)
        b.wait("dve", "pe", tok)
        self.t_bf1 = b.op("dve", K.tt(self.rows16[0:1, 0:8], psBF1,
                                      self.rows16[0:1, 8:16], OP.add), inc="dve")

    def _wfuse2(self):
        b, d = self, self.d
        psW2 = self.PSF[:, 3200:3264]
        psBF2 = self.PSF[:, 3264:3272]
        wqost = self.arB[0:8, 0:4096].bitcast(F32)
        t_tp = 0
        t_wq = 0
        for h in range(2):
            tok = b.dld("dw2", ("dve", t_wq),
                        K.dma(wqost, d["Wqo"][0:8, h * 2048:(h + 1) * 2048]))
            b.wait("pe", "dw2", tok)
            for ms in range(16):
                t_tp = b.op("pe", K.tp(self.PST[:, ms * 8:(ms + 1) * 8],
                                       wqost[0:8, ms * 128:(ms + 1) * 128],
                                       self.ident32[0:8, 0:8]),
                            inc="pe" if ms == 15 else None)
            b.wait("dve", "pe", t_tp)
            t_wq = b.op("dve", K.cp(self.WqoT16[:, h * 128:(h + 1) * 128],
                                    self.PST[:, 0:128]), inc="dve")
        w2st = [self.Dv(0, 1024), self.Dv(1024, 1024)]
        w2c16 = [self.arB[:, 0:1024], self.arB[:, 1024:2048]]
        t_c = [t_tp, t_tp]
        b.wait("pe", "dve", t_wq)
        for k in range(NT):
            h = k % 2
            tok = b.dld("dw%d" % h, ("pe", t_c[h]),
                        K.dma(w2st[h], d["W2"][k * 128:(k + 1) * 128, :]))
            b.wait("pool", "dw%d" % h, tok)
            if t_c[h]:
                b.wait("pool", "pe", t_c[h])
            tp_ = b.op("pool", K.cp(w2c16[h], w2st[h]), inc="pool")
            b.wait("pe", "pool", tp_)
            tok2 = None
            for fc in range(8):
                b.op("pe", K.mm(psW2[:, fc * 8:(fc + 1) * 8],
                                w2c16[h][:, fc * 128:(fc + 1) * 128],
                                self.WqoT16[:, k * 8:(k + 1) * 8],
                                start=(k == 0 and fc == 0), stop=False))
                tok2 = b.op("pe", K.mm(psBF2[:, fc:fc + 1],
                                       w2c16[h][:, fc * 128:(fc + 1) * 128],
                                       self.bqo16[:, k:k + 1],
                                       start=False,
                                       stop=(k == NT - 1 and fc == 7)),
                            inc="pe" if fc == 7 else None)
            t_c[h] = tok2
        b.wait("dve", "pe", self.cnt["pe"])
        b.op("dve", K.cp(self.w2ftc[:, :], psW2))
        self.t_w2f = b.op("dve", K.tt(self.col("bf2c"), psBF2, self.col("b2c"),
                                      OP.add), inc="dve")

    def _load_cn(self):
        b, d = self, self.d
        self.cnw1 = self.arW
        t_cast = 0
        for c in range(4):
            tok = b.dld("dw0", ("pool", t_cast),
                        K.dma(self.Dv(0, 2048),
                              K.ap(d["cnw1"].tensor, c * 2 * 128 * 1024,
                                   [[1024, 128], [128 * 1024, 2], [1, 1024]])))
            b.wait("pool", "dw0", tok)
            t_cast = b.op("pool", K.cp(self.arW[:, c * 2048:(c + 1) * 2048],
                                       self.Dv(0, 2048)), inc="pool")
        tok = b.dld("dw0", ("pool", t_cast),
                    K.dma(self.Dv(0, 4096),
                          K.ap(d["cnw2"].tensor, 0,
                               [[512, 128], [512 * 128, 8], [1, 512]])))
        b.wait("pool", "dw0", tok)
        t_cast = b.op("pool", K.cp(self.cnw2[:, 0:4096], self.Dv(0, 4096)),
                      inc="pool")
        tok = b.dld("dw0", ("pool", t_cast),
                    K.dma(self.Dv(0, 1024),
                          K.ap(d["cnw3"].tensor, 0,
                               [[256, 128], [256 * 128, 4], [1, 256]])))
        b.wait("pool", "dw0", tok)
        t_cast = b.op("pool", K.cp(self.cnw3[:, 0:1024], self.Dv(0, 1024)),
                      inc="pool")
        tok = b.dld("dw0", ("pool", t_cast),
                    K.dmas(self.Dv(2048, 2),
                           K.ap(d["cnw4"].tensor, 0, [[1, 128], [128, 2], [1, 1]])))
        b.wait("pool", "dw0", tok)
        self.t_cn = b.op("pool", K.cp(self.cnw4[:, 0:2], self.Dv(2048, 2)),
                         inc="pool")

    def _phase_g(self):
        b, d = self, self.d
        xld = [self.Dv(0, 1024), self.Dv(1024, 1024), self.Dv(2048, 1024)]
        gt = self.Dv(3072, 1024)
        PSH = [self.PSF[:, 0:512], self.PSF[:, 512:1024]]
        PSH2 = [self.PSF[:, 1024:1536], self.PSF[:, 1536:2048]]
        PSH3 = [self.PSF[:, 2048:2560], self.PSF[:, 2048:2560]]
        PSH4 = self.PSF[0:1, 2560:3072]
        PSCPL = self.PSF[:, 2560:3072]
        cbc = K.ap(self.arE, 8192, [[13312, 128], [0, 2], [1, 1024]]).bitcast(F32)
        b.wait("pe", "pool", self.t_cn)
        b.wait("pe", "dve", self.t_gfc)
        b.wait("syncq", "pool", self.t_cn)
        b.wait("syncq", "sx0", self.tA_st[30])
        b.wait("syncq", "sx1", self.tA_st[31])
        og2 = self.arC[:, 0:8192].bitcast(F32)
        slots = [dict(ar=self.arB, og=self.og, o16=self.out16),
                 dict(ar=self.arC, og=og2, o16=self.arC[:, 8192:12288])]
        t_st = 0
        t_dg = [0, 0, 0]
        t_pst = 0
        t_g = [0, 0]
        t_g2 = [0, 0]
        t_g3 = [0, 0]
        t_thr = 0
        t_cb = 0
        tc2 = 0
        t_h1mm = [0, 0]
        t_o16 = [0, 0]

        def emit_round(sidx):
            nonlocal t_thr, t_cb, tc2
            sl = slots[sidx]
            b.wait("dve", "pe", t_h1mm[sidx])
            t_o16[sidx] = b.op("dve", K.cp(sl["o16"], sl["og"]), inc="dve")
            b.wait("pe", "dve", t_o16[sidx])
            for fc in range(8):
                if t_g[fc % 2]:
                    b.wait("pe", "act", t_g[fc % 2])
                tok = None
                for kc in range(8):
                    tok = b.op("pe", K.mm(
                        PSH[fc % 2],
                        self.cnw1[:, kc * 1024 + fc * 128:
                                  kc * 1024 + (fc + 1) * 128],
                        sl["o16"][:, kc * 512:(kc + 1) * 512],
                        start=(kc == 0), stop=(kc == 7)),
                        inc="pe" if kc == 7 else None)
                b.wait("act", "pe", tok)
                t_g[fc % 2] = b.op("act", K.act(
                    self.h1g[:, fc * 512:(fc + 1) * 512], PSH[fc % 2],
                    AF.Gelu, bias=self.col("cb1", fc, 1)), inc="act")
            t_h1mm[sidx] = self.cnt["pe"]
            t_h1 = self.cnt["act"]
            for fc in range(4):
                if t_g2[fc % 2]:
                    b.wait("pe", "act", t_g2[fc % 2])
                b.wait("pe", "act", t_h1)
                tok = None
                for kc in range(8):
                    tok = b.op("pe", K.mm(
                        PSH2[fc % 2],
                        self.cnw2[:, kc * 512 + fc * 128:
                                  kc * 512 + (fc + 1) * 128],
                        self.h1g[:, kc * 512:(kc + 1) * 512],
                        start=(kc == 0), stop=(kc == 7)),
                        inc="pe" if kc == 7 else None)
                b.wait("act", "pe", tok)
                t_g2[fc % 2] = b.op("act", K.act(
                    self.h2g[:, fc * 512:(fc + 1) * 512], PSH2[fc % 2],
                    AF.Gelu, bias=self.col("cb2", fc, 1)), inc="act")
            t_h2 = self.cnt["act"]
            for fc in range(2):
                if t_g3[(fc - 1) % 2]:
                    b.wait("pe", "act", t_g3[(fc - 1) % 2])
                b.wait("pe", "act", t_h2)
                tok = None
                for kc in range(4):
                    tok = b.op("pe", K.mm(
                        PSH3[fc % 2],
                        self.cnw3[:, kc * 256 + fc * 128:
                                  kc * 256 + (fc + 1) * 128],
                        self.h2g[:, kc * 512:(kc + 1) * 512],
                        start=(kc == 0), stop=(kc == 3)),
                        inc="pe" if kc == 3 else None)
                b.wait("act", "pe", tok)
                t_g3[fc % 2] = b.op("act", K.act(
                    self.h3g[:, fc * 512:(fc + 1) * 512], PSH3[fc % 2],
                    AF.Gelu, bias=self.col("cb3", fc, 1)), inc="act")
            t_h3 = self.cnt["act"]
            b.wait("pe", "act", t_h3)
            b.wait("pe", "act", t_thr)
            tok = None
            for kc in range(2):
                tok = b.op("pe", K.mm(PSH4, self.cnw4[:, kc:kc + 1],
                                      self.h3g[:, kc * 512:(kc + 1) * 512],
                                      start=(kc == 0), stop=(kc == 1)),
                           inc="pe" if kc == 1 else None)
            b.wait("act", "pe", tok)
            thr = self.rb_s[0:1, 0:512]
            ta = b.op("act", K.act(thr, PSH4, AF.Tanh, bias=self.sc32[0:1, 5:6],
                                   scale=0.5), inc="act")
            t_thr = ta
            b.wait("dve", "act", ta)
            b.op("dve", K.ts(thr, thr, 0.15, self.sc32[0:1, 8:9],
                             op0=OP.mult, op1=OP.add))
            tc = b.op("dve", K.ts(thr, thr, self.sc32[0:1, 4:5],
                                  self.sc32[0:1, 7:8], op0=OP.mult, op1=OP.add),
                      inc="dve")
            b.wait("pe", "dve", tc)
            b.wait("pe", "dve", t_cb)
            t_bc = b.op("pe", K.mm(PSCPL, self.ones32[0:1, 0:128], thr),
                        inc="pe")
            b.wait("dve", "pe", t_bc)
            t_cb = b.op("dve", K.cp(self.rb_s, PSCPL), inc="dve")
            tc2 = t_cb
            og = sl["og"]
            for pair in range(2):
                srcs = (2, 0) if pair == 0 else (3, 1)
                dsts = (0, 2) if pair == 0 else (1, 3)
                ops = (OP.add, OP.subtract) if pair == 0 else \
                    (OP.subtract, OP.add)
                b.wait("pool", "dve", tc2)
                tws = [0, 0]
                tws[0] = b.op("dve", K.tt(
                    self.twt[0], og[:, srcs[0] * 1024:(srcs[0] + 1) * 1024],
                    cbc, OP.mult), inc="dve")
                tws[1] = b.op("pool", K.tt(
                    self.twt[1], og[:, srcs[1] * 1024:(srcs[1] + 1) * 1024],
                    cbc, OP.mult), inc="pool")
                for j in range(2):
                    qq = dsts[j]
                    b.wait("dve", "pool", tws[1])
                    tc2 = b.op("dve", K.tt(
                        og[:, qq * 1024:(qq + 1) * 1024],
                        og[:, qq * 1024:(qq + 1) * 1024],
                        self.twt[j], ops[j]), inc="dve")

        for gp in range(4):
            b.wait("dve", "pe", self.cnt["pe"])
            b.wait("dve", "pool", self.cnt["pool"])
            for sidx in (0, 1):
                g = 2 * gp + sidx
                sl = slots[sidx]
                t_ev = 0
                for i in range(4):
                    T = g * 4 + i
                    slq = T % 3
                    tok = b.dld("dg%d" % slq, ("pe", t_dg[slq]),
                                K.dma(xld[slq],
                                      d["xn_sc"][T * 128:(T + 1) * 128, :]))
                    b.wait("pe", "dg%d" % slq, tok)
                    for r in range(2):
                        if t_pst:
                            b.wait("pe", "dve", t_pst)
                        tok2 = None
                        for c4 in range(4):
                            c = 4 * r + c4
                            tok2 = b.op("pe", K.tp(
                                self.PST[:, c4 * 128:(c4 + 1) * 128],
                                xld[slq][:, c * 128:(c + 1) * 128],
                                self.ident32[:, :]),
                                inc="pe" if c4 == 3 else None)
                        b.wait("dve", "pe", tok2)
                        dst = self.f32ap(sl["ar"], (4 * r) * 512 + i * 128,
                                         [[8192, 128], [512, 4], [1, 128]])
                        srcp = K.ap(self.PST, 0, [[512, 128], [128, 4], [1, 128]])
                        t_ev = b.op("dve", K.cp(dst, srcp), inc="dve")
                        t_pst = t_ev
                    t_dg[slq] = tok2
            for r in range(3):
                for sidx in (0, 1):
                    emit_round(sidx)
            for sidx in (0, 1):
                g = 2 * gp + sidx
                sl = slots[sidx]
                th16 = sl["o16"]
                b.wait("act", "dve", self.cnt["dve"])
                b.wait("act", "pool", self.cnt["pool"])
                b.wait("act", "pe", t_h1mm[sidx])
                ta = b.op("act", K.act(th16, sl["og"], AF.Tanh, scale=0.5),
                          inc="act")
                b.wait("dve", "act", ta)
                b.op("dve", K.ts(th16, th16, 1.0, op0=OP.add))
                t_gg = b.op("dve", K.tt(sl["og"], sl["og"], th16, OP.mult),
                            inc="dve")
                b.wait("pe", "dve", t_gg)
                for i in range(4):
                    T = g * 4 + i
                    for r in range(2):
                        if t_pst:
                            b.wait("pe", "dve", t_pst)
                        tok2 = None
                        for c4 in range(4):
                            c = 4 * r + c4
                            tok2 = b.op("pe", K.tp(
                                self.PST[:, c4 * 128:(c4 + 1) * 128],
                                self.f32ap(sl["ar"], c * 512 + i * 128,
                                           [[8192, 128], [1, 128]]),
                                self.ident32[:, :]), inc="pe" if c4 == 3 else None)
                        b.wait("dve", "pe", tok2)
                        if t_st:
                            b.wait("dve", "sg", t_st)
                        dst = K.ap(self.arD, 3072 + (4 * r) * 128,
                                   [[6144, 128], [128, 4], [1, 128]])
                        srcp = K.ap(self.PST, 0, [[512, 128], [128, 4], [1, 128]])
                        t_ev2 = b.op("dve", K.cp(dst, srcp), inc="dve")
                        t_pst = t_ev2
                    b.wait("pool", "dve", t_ev2)
                    t_st = b.op("pool", K.dma(d["g_sc"][T * 128:(T + 1) * 128, :],
                                              gt), inc="sg", n=16)
        self.tG_gst = t_st

    def _phase_b(self):
        b, d = self, self.d
        psA_ = self.PSF[:, 0:1024]
        psB_ = [self.PSF[:, 1024:2048], self.PSF[:, 2048:3072]]
        psPV = self.PSF[:, 3072:3584]
        psS = self.PST[0:1, 0:512]
        psQK = self.PSF[:, 1024:1536]
        wst = [self.Dv(0, 1024), self.Dv(1024, 1024), self.Dv(2048, 1024)]
        wq16 = self.arW[:, 0:1024]
        wk16 = self.arW[:, 1024:2048]
        wv16 = self.arW[:, 2048:3072]
        vftmp = self.Dv(2048, 4096)
        sm = self.small32

        t_stt = [0] * 16
        t_tt = [0] * 16
        t_exp = [0] * 16
        t_pvs = [0] * 16
        t_sin = [0] * 16
        t_mmA = [0] * 16
        t_mmB = [0] * 16
        t_attn = [0, 0]
        t_recip = 0
        t_norm = 0
        t_sr1_roll = 0
        self.fpend = None

        t_pe_flash = 0
        for n in range(8):
            tw = []
            for wi, wd in enumerate([d["Wq"], d["Wk"], d["Wv"]]):
                tw.append(b.dld("dw%d" % wi, ("pe", t_pe_flash),
                                K.dma(wst[wi], K.ap(wd.tensor, n * 1024 * 128,
                                                    [[128, 128], [128 * 128, 8],
                                                     [1, 128]]))))
            b.wait("pool", "pe", t_pe_flash)
            for wi, w16 in enumerate([wq16, wk16, wv16]):
                b.wait("pool", "dw%d" % wi, tw[wi])
                b.op("pool", K.cp(w16, wst[wi]), inc="pool")
            t_w16 = self.cnt["pool"]
            b.wait("dve", "act", self.cnt["act"])
            b.op("dve", K.cp(self.bcqpat[0:1, 0:8],
                             self.bc2q16[0:1, n * 8:(n + 1) * 8]))
            b.op("dve", K.cp(self.bcqpat[0:1, 8:16],
                             self.bc2q16[0:1, n * 8:(n + 1) * 8]))
            w = 16
            while w < 512:
                b.op("dve", K.cp(self.bcqpat[0:1, w:2 * w], self.bcqpat[0:1, 0:w]))
                w *= 2
            t_pat = b.op("dve", lambda e: e.nop(), inc="dve")
            b.wait("pe", "pool", t_w16)
            b.wait("pe", "dve", self.cnt["dve"])
            t_ev = {}
            for wi in range(3):
                w16 = [wq16, wk16, wv16][wi]
                for sg in range(8):
                    ps = psB_[sg % 2]
                    prev = t_ev.get((wi, sg - 2), t_ev.get((wi - 1, 6 + sg), 0))
                    b.wait("pe", "dve", prev)
                    tok = None
                    for kc in range(8):
                        tok = b.op("pe", K.mm(
                            ps[:, 0:512], w16[:, kc * 128:(kc + 1) * 128],
                            self.xn_fm[:, kc * 4096 + sg * 512:
                                       kc * 4096 + (sg + 1) * 512],
                            start=(kc == 0), stop=(kc == 7)),
                            inc="pe" if kc == 7 else None)
                    b.wait("dve", "pe", tok)
                    if wi == 0:
                        fn = K.ts(self.qc_fm[:, sg * 512:(sg + 1) * 512],
                                  ps[:, 0:512], self.col("bq", n, 1), C1,
                                  op0=OP.add, op1=OP.mult)
                    elif wi == 1:
                        fn = K.ts(self.k_fm[:, sg * 512:(sg + 1) * 512],
                                  ps[:, 0:512], self.col("bk", n, 1), op0=OP.add)
                    else:
                        fn = K.ts(vftmp[:, sg * 512:(sg + 1) * 512], ps[:, 0:512],
                                  self.col("bv", n, 1), op0=OP.add)
                    t_ev[(wi, sg)] = b.op("dve", fn, inc="dve")
            t_qkv = self.cnt["dve"]
            b.wait("pe", "dve", t_qkv)
            t_vev = 0
            for rr in range(8):
                if t_vev:
                    b.wait("pe", "dve", t_vev)
                tok = None
                for j in range(4):
                    tau = rr * 4 + j
                    tok = b.op("pe", K.tp(self.PST[:, j * 128:(j + 1) * 128],
                                          vftmp[:, tau * 128:(tau + 1) * 128],
                                          self.ident32[:, :]),
                               inc="pe" if j == 3 else None)
                b.wait("dve", "pe", tok)
                t_vev = b.op("dve", K.cp(self.v_tm[:, rr * 512:(rr + 1) * 512],
                                         self.PST[:, 0:512]), inc="dve")
            b.wait("pe", "dve", t_vev)
            for tau in range(NT):
                b.op("pe", K.mm(psQK[:, 16 * tau:16 * tau + 8],
                                self.qc_fm[:, tau * 128:(tau + 1) * 128],
                                self.Wc2qc16[:, n * 8:(n + 1) * 8],
                                start=(tau == 0), stop=False))
                b.op("pe", K.mm(psQK[:, 16 * tau + 8:16 * tau + 16],
                                self.k_fm[:, tau * 128:(tau + 1) * 128],
                                self.Wc2q16[:, n * 8:(n + 1) * 8],
                                start=False, stop=False))
            b.wait("pe", "dve", t_pat)
            t_qk = b.op("pe", K.mm(psQK, self.ones16[0:1, 0:128], self.bcqpat,
                                   start=False, stop=True), inc="pe")
            b.wait("act", "pe", t_qk)
            ta = b.op("act", K.act(self.sqt[:, :], psQK, AF.Square), inc="act")
            b.wait("dve", "act", ta)
            td = b.op("dve", lambda e: e.reduce_sum(
                K.ap(self.small32, 64, [[192, 128], [2, 32], [1, 2]]),
                K.ap(self.sqt, 0, [[512, 128], [16, 32], [8, 2], [1, 8]]),
                axis=mybir.AxisListType.X), inc="dve")
            b.wait("act", "dve", td)
            b.op("act", K.act(self.nsq, self.nsq, AF.Ln, bias=self.cc[:, 1:2]))
            ta = b.op("act", K.act(self.nsq, self.nsq, AF.Exp, scale=-0.5),
                      inc="act")
            b.wait("dve", "act", ta)
            b.op("dve", K.ts(self.nsq, self.nsq, 1e4, op0=OP.min))
            b.op("dve", K.tt(self.rs,
                             self.nsq,
                             K.ap(self.cols32, self.CL["pat2"][0],
                                  [[160, 128], [0, 32], [1, 2]]),
                             OP.mult))
            tqk32 = b.op("dve", K.tt(
                self.qk32, psQK,
                K.ap(self.small32, 128, [[192, 128], [2, 32], [1, 2], [0, 8]]),
                OP.mult), inc="dve")
            b.wait("pe", "dve", tqk32)
            t_qev = 0
            for rr in range(16):
                if t_qev:
                    b.wait("pe", "act", t_qev)
                tok = None
                for j in range(4):
                    idx = rr * 4 + j
                    tau, side = idx // 2, idx % 2
                    tok = b.op("pe", K.tp(self.PST[0:8, j * 128:(j + 1) * 128],
                                          self.qk32[:, 16 * tau + 8 * side:
                                                    16 * tau + 8 * side + 8],
                                          self.ident32[:, :]),
                               inc="pe" if j == 3 else None)
                b.wait("act", "pe", tok)
                tau0 = rr * 2
                b.op("act", K.acp(self.qh[0:8, tau0 * 128:(tau0 + 2) * 128],
                                  K.ap(self.PST, 0, [[512, 8], [256, 2], [1, 128]])))
                t_qev = b.op("act", K.acp(
                    self.kl[0:8, tau0 * 128:(tau0 + 2) * 128],
                    K.ap(self.PST, 128, [[512, 8], [256, 2], [1, 128]])),
                    inc="act")
            t_prep_act = t_qev
            t_prep_pool = self.cnt["pool"]

            for i in range(9):
                has_F = i < 8
                has_B = i >= 1
                D0 = self.cnt["dve"]
                A0 = self.cnt["act"]
                has_norm = self.fpend is not None
                pv15_old = t_pvs[15]
                pvs_prev = list(t_pvs)
                tt_prev = list(t_tt)

                def pred_dquad(j):
                    return A0 + (8 if has_F else 0) + (j - 2) + 1

                def pred_stt(p):
                    return D0 + 2 * p + 1 + (2 if has_norm else 0)

                def pred_tt(p):
                    return pred_stt(p) + 1
                cp_tok = D0 + 1
                norm_tok = D0 + 2
                if has_norm:
                    b.wait("pe", "dve", t_recip)
                    b.wait("pe", "dve", t_tt[15])
                    t_bc = b.op("pe", K.mm(psA_[:, 0:512],
                                           self.ones32[0:1, 0:128],
                                           self.rb_s[0:1, 0:512]), inc="pe")
                for p in range(16):
                    if has_B:
                        b.wait("pe", "act",
                               t_exp[p] if p < 8 else pred_dquad(p // 4))
                        if p == 0:
                            b.wait("pe", "dve",
                                   norm_tok if has_norm else t_norm)
                            b.wait("pe", "dve", t_recip)
                        for half in range(2):
                            t_blk = 2 * p + half
                            b.op("pe", K.mm(
                                psPV,
                                self.v_tm[:, t_blk * 128:(t_blk + 1) * 128],
                                self.ein[:, p * 1024 + half * 512:
                                         p * 1024 + (half + 1) * 512],
                                start=(p == 0 and half == 0),
                                stop=(p == 15 and half == 1)))
                            t_pvs[p] = b.op("pe", K.mm(
                                psS, self.ones16[:, 0:1],
                                self.ein[:, p * 1024 + half * 512:
                                         p * 1024 + (half + 1) * 512],
                                start=(p == 0 and half == 0),
                                stop=(p == 15 and half == 1)),
                                inc="pe" if half == 1 else None)
                    if not has_F:
                        continue
                    pb = psB_[p % 2]
                    b.wait("pe", "act", t_prep_act)
                    b.wait("pe", "dve",
                           t_stt[(p - 2) % 16] if p < 2 else pred_stt(p - 2))
                    for half in range(2):
                        t_blk = 2 * p + half
                        t_mmB[p] = b.op("pe", K.mm(
                            pb[:, half * 512:(half + 1) * 512],
                            self.kl[0:8, t_blk * 128:(t_blk + 1) * 128],
                            self.qh[0:8, i * 512:(i + 1) * 512],
                            start=True, stop=True),
                            inc="pe" if half == 1 else None)
                    if p == 0:
                        b.wait("pe", "dve", t_tt[15])
                        b.wait("pe", "dve",
                               cp_tok if has_norm else t_norm)
                    else:
                        b.wait("pe", "dve", pred_tt(p - 1))
                    for half in range(2):
                        t_blk = 2 * p + half
                        t_mmA[p] = b.op("pe", K.mm(
                            psA_[:, half * 512:(half + 1) * 512],
                            self.k_fm[:, t_blk * 128:(t_blk + 1) * 128],
                            self.qc_fm[:, i * 512:(i + 1) * 512],
                            start=True, stop=True),
                            inc="pe" if half == 1 else None)
                def emit_sin(p):
                    b.wait("act", "pe", t_mmB[p])
                    b.wait("act", "dve",
                           t_stt[p + 10] if p < 6 else pred_stt(p - 6))
                    t_sin[p] = b.op("act", K.act(
                        self.s2bp[(p // 2) % 3][:, (p % 2) * 1024:
                                                (p % 2) * 1024 + 1024],
                        psB_[p % 2], AF.Sin, scale=SIN_SCALE), inc="act")

                def emit_equad(sl, j, gate_tt, gate_pe):
                    b.wait("act", "dve", gate_tt)
                    b.wait("act", "pe", gate_pe)
                    tj = b.op("act", K.act(
                        self.ein[:, j * 4096:(j + 1) * 4096],
                        self.ein[:, j * 4096:(j + 1) * 4096],
                        AF.Exp), inc="act")
                    for pp in range(4 * j, 4 * j + 4):
                        t_exp[pp] = tj
                for p in range(8 if has_F else 0):
                    emit_sin(p)
                if has_B:
                    for j in (2, 3):
                        emit_equad(i - 1, j, tt_prev[4 * j + 3],
                                   pvs_prev[4 * j + 3])
                for p in range(8, 16) if has_F else ():
                    emit_sin(p)
                if has_norm:
                    fn2, fss = self.fpend
                    b.wait("dve", "pe", t_bc)
                    t_cp = b.op("dve", K.cp(self.rb_s, psA_[:, 0:512]),
                                inc="dve")
                    assert t_cp == cp_tok
                    b.wait("dve", "pe", pv15_old)
                    b.wait("dve", "sa%d" % (fss % 2), t_attn[fss % 2])
                    t_norm = b.op("dve", K.tt(self.attn_t[fss % 2], psPV,
                                              self.rb_s, OP.mult), inc="dve")
                    assert t_norm == norm_tok
                    b.wait("pool", "dve", t_norm)
                    t_attn[fss % 2] = b.op("pool", K.dma(
                        d["attn_sc"][:, fn2 * 4096 + fss * 512:
                                     fn2 * 4096 + (fss + 1) * 512],
                        self.attn_t[fss % 2]), inc="sa%d" % (fss % 2), n=16)
                    self.fpend = None
                for p in range(16 if has_F else 0):
                    b.wait("dve", "act", t_sin[p])
                    t_stt[p] = b.op("dve", K.stt(
                        self.mq16[p % 3],
                        self.s2bp[(p // 2) % 3][:, (p % 2) * 1024:
                                                (p % 2) * 1024 + 1024],
                        10.0, psB_[p % 2], OP.add, OP.mult), inc="dve")
                    assert t_stt[p] == pred_stt(p)
                    b.wait("dve", "pe", t_mmA[p])
                    b.wait("dve", "pe", t_pvs[p])
                    b.wait("dve", "act", t_exp[p])
                    t_tt[p] = b.op("dve", K.tt(self.ein[:, p * 1024:(p + 1) * 1024],
                                               psA_, self.mq16[p % 3], OP.add),
                                   inc="dve")
                    assert t_tt[p] == pred_tt(p)
                for j in range(2) if has_F else ():
                    emit_equad(i, j, t_tt[4 * j + 3], t_pvs[4 * j + 3])
                if has_B:
                    b.wait("dve", "pe", t_pvs[15])
                    t_recip = b.op("dve",
                                   lambda e, o=self.rb_s[0:1, 0:512], i_=psS:
                                   e.reciprocal(o, i_), inc="dve")
                    self.fpend = (n, i - 1)
            fn2, fss = self.fpend
            b.wait("pe", "dve", t_recip)
            b.wait("pe", "dve", t_tt[15])
            t_bc = b.op("pe", K.mm(psA_[:, 0:512], self.ones32[0:1, 0:128],
                                   self.rb_s[0:1, 0:512]), inc="pe")
            b.wait("dve", "pe", t_bc)
            t_cp = b.op("dve", K.cp(self.rb_s, psA_[:, 0:512]), inc="dve")
            b.wait("dve", "pe", t_pvs[15])
            b.wait("dve", "sa%d" % (fss % 2), t_attn[fss % 2])
            t_norm = b.op("dve", K.tt(self.attn_t[fss % 2], psPV, self.rb_s,
                                      OP.mult), inc="dve")
            b.wait("pool", "dve", t_norm)
            t_attn[fss % 2] = b.op("pool", K.dma(
                d["attn_sc"][:, fn2 * 4096 + fss * 512:
                             fn2 * 4096 + (fss + 1) * 512],
                self.attn_t[fss % 2]), inc="sa%d" % (fss % 2), n=16)
            self.fpend = None
            t_pe_flash = self.cnt["pe"]

    def _cprep(self):
        b, d = self, self.d
        b.wait("syncq", "sa0", self.cnt["sa0"])
        b.wait("syncq", "sa1", self.cnt["sa1"])
        b.wait("syncq", "sg", self.cnt["sg"])
        b.wait("syncq", "sx0", self.cnt["sx0"])
        b.wait("syncq", "sx1", self.cnt["sx1"])
        t_wo = 0
        for c in range(8):
            tok = b.dld("dw0", ("pool", t_wo),
                        K.dma(self.Dv(0, 1024), d["Wo"][c * 128:(c + 1) * 128, :]))
            b.wait("pool", "dw0", tok)
            t_wo = b.op("pool", K.cp(self.arW[:, c * 1024:(c + 1) * 1024],
                                     self.Dv(0, 1024)), inc="pool")
        self.t_wo16 = self.cnt["pool"]
        rb = self.arD[0:1, 2048:3072]
        for nm, dst, half in [("g1", self.g1bc, False), ("g2", self.g2bc, True),
                              ("beta2", self.b2bc, True)]:
            tok = b.dil(K.dma(rb, d[nm][None, :]))
            b.wait("dve", "di", tok)
            if half:
                b.op("dve", K.ts(rb, rb, 0.5))
            trb = b.op("dve", lambda e: e.nop(), inc="dve")
            b.wait("pe", "dve", trb)
            tok2 = None
            for hh in range(2):
                tok2 = b.op("pe", K.mm(self.PSF[:, hh * 512:(hh + 1) * 512],
                                       self.ones32[0:1, 0:128],
                                       rb[0:1, hh * 512:(hh + 1) * 512]),
                            inc="pe" if hh == 1 else None)
            b.di_done("pe", tok2)
            b.wait("dve", "pe", tok2)
            b.op("dve", K.cp(dst, self.PSF[:, 0:1024]), inc="dve")
        for nm, dst in [("bo", self.boC), ("beta1", self.be1C)]:
            tok = b.dil(K.dma(rb, d[nm][None, :]))
            b.wait("dve", "di", tok)
            td = b.op("dve", K.cp(dst, rb), inc="dve")
            b.di_done("dve", td)
        for r in range(2):
            b.wait("pe", "dve", self.cnt["dve"])
            tok = None
            for j in range(4):
                cidx = r * 4 + j
                tok = b.op("pe", K.tp(self.PST[0:1, j * 128:(j + 1) * 128],
                                      self.col("bf2c", cidx, 1),
                                      self.ident32[:, :]),
                           inc="pe" if j == 3 else None)
            b.wait("dve", "pe", tok)
            b.op("dve", K.cp(self.bf2C[0:1, r * 512:(r + 1) * 512],
                             K.ap(self.PST, 0, [[512, 1], [128, 4], [1, 128]])),
                 inc="dve")
        for r in range(2):
            b.wait("pe", "dve", self.cnt["dve"])
            tok = None
            for j in range(4):
                fc = r * 4 + j
                tok = b.op("pe", K.tp(self.PST[0:8, j * 128:(j + 1) * 128],
                                      self.w2ftc[:, fc * 8:(fc + 1) * 8],
                                      self.ident32[:, :]),
                           inc="pe" if j == 3 else None)
            b.wait("dve", "pe", tok)
            b.op("dve", K.cp(self.W2f16[0:8, r * 512:(r + 1) * 512],
                             K.ap(self.PST, 0, [[512, 8], [128, 4], [1, 128]])),
                 inc="dve")
        self.tCP = b.op("dve", lambda e: e.nop(), inc="dve")

    def _phase_c(self):
        b, d = self, self.d
        psWO = [self.PSF[:, 0:1024], self.PSF[:, 1024:2048]]
        psQM = self.PSF[:, 2048:2056]
        psFF = self.PSF[:, 2560:3584]
        xnld = [self.Dv(3072, 1024), self.Dv(4096, 1024)]
        xld = [self.Dv(0, 1024), self.Dv(1024, 1024)]
        gld = [self.Dv(2048, 512), self.Dv(5120, 1024)]
        gld = [self.arD[:, 2048:3072], self.arD[:, 5120:6144]]
        sm = self.small32
        t_pre = [0, 0]
        t_sh = [0, 0]
        t_out = [0, 0]
        t_ld = [0, 0]
        t_wo = [0, 0]
        t_xn = [0, 0]
        t_x1 = [0] * 8
        t_x1T = [0] * 8
        t_qeT = [0] * 8
        self.t_ff_last = 0
        for gi in range(4):
            b.wait("dve", "pool", self.cnt["pool"])
            for t in range(8):
                T = gi * 8 + t
                xb = t % 2
                tok_a = b.dld("da%d" % xb, ("pe", t_wo[xb]),
                              K.dma(self.astg[xb],
                                    K.ap(d["attn_sc"].tensor, T * 128,
                                         [[32768, 128], [4096, 8], [1, 128]])))
                tok_x = b.dld("dn%d" % xb, ("dve", t_xn[xb]),
                              K.dma(xnld[xb],
                                    d["xn_sc"][T * 128:(T + 1) * 128, :]))
                b.wait("pe", "da%d" % xb, tok_a)
                if t_pre[xb]:
                    b.wait("pe", "dve", t_pre[xb])
                tok = None
                for c in range(8):
                    for hh in range(2):
                        b.op("pe", K.mm(
                            psWO[xb][:, hh * 512:(hh + 1) * 512],
                            self.astg[xb][:, c * 128:(c + 1) * 128],
                            self.arW[:, c * 1024 + hh * 512:
                                     c * 1024 + (hh + 1) * 512],
                            start=(c == 0), stop=False))
                for hh in range(2):
                    tok = b.op("pe", K.mm(psWO[xb][:, hh * 512:(hh + 1) * 512],
                                          self.ones16[0:1, 0:128],
                                          self.boC[0:1, hh * 512:(hh + 1) * 512],
                                          start=False, stop=True),
                               inc="pe" if hh == 1 else None)
                t_wo[xb] = tok
                b.wait("dve", "pe", tok)
                b.wait("dve", "dn%d" % xb, tok_x)
                t_xn[xb] = b.op("dve", K.tt(self.preb[xb], psWO[xb], xnld[xb],
                                            OP.add), inc="dve")
                b.wait("act", "dve", t_xn[xb])
                b.op("act", K.act(self.trashC, self.preb[xb], AF.Square,
                                  accum=sm[:, 24 + xb:25 + xb]))
                ta = b.op("act", K.act(self.trashC, self.preb[xb], AF.Copy,
                                       accum=sm[:, 26 + xb:27 + xb]), inc="act")
                b.wait("dve", "act", ta)
                b.op("dve", K.ts(sm[:, 28:29], sm[:, 26 + xb:27 + xb], 1.0 / 1024.0))
                b.op("dve", K.ts(sm[:, 29:30], sm[:, 24 + xb:25 + xb], 1.0 / 1024.0))
                b.op("dve", K.tt(sm[:, 30:31], sm[:, 28:29], sm[:, 28:29], OP.mult))
                td = b.op("dve", K.tt(sm[:, 31:32], sm[:, 29:30], sm[:, 30:31],
                                      OP.subtract), inc="dve")
                b.wait("act", "dve", td)
                b.op("act", K.act(sm[:, 31:32], sm[:, 31:32], AF.Ln, bias=self.cc[:, 0:1]))
                ta2 = b.op("act", K.act(sm[:, 32:33], sm[:, 31:32], AF.Exp,
                                        scale=-0.5), inc="act")
                b.wait("dve", "act", ta2)
                t_x1[t] = b.op("dve", K.ts(self.x1a(t), self.preb[xb], sm[:, 28:29],
                                           sm[:, 32:33], op0=OP.subtract,
                                           op1=OP.mult), inc="dve")
                t_pre[xb] = t_x1[t]
                b.wait("pe", "dve", t_x1[t])
                for r in range(2):
                    tok = None
                    for c4 in range(4):
                        c = 4 * r + c4
                        tok = b.op("pe", K.tp(
                            self.PST[:, c4 * 128:(c4 + 1) * 128],
                            self.f32ap(self.arA, t * 1024 + c * 128,
                                       [[16384, 128], [1, 128]]),
                            self.ident32[:, :]), inc="pe" if c4 == 3 else None)
                    b.wait("dve", "pe", tok)
                    t_x1T[t] = b.op("dve", K.cp(
                        self.x1aT(t)[:, r * 512:(r + 1) * 512],
                        self.PST[:, 0:512]), inc="dve")
                    if r == 0:
                        b.wait("pe", "dve", t_x1T[t])
            b.wait("act", "pe", self.cnt["pe"])
            for t in range(8):
                b.wait("pe", "dve", t_x1T[t])
                for c in range(8):
                    b.op("pe", K.mm(psQM, self.x1aT(t)[:, c * 128:(c + 1) * 128],
                                    self.Wf16[:, c * 8:(c + 1) * 8],
                                    start=(c == 0), stop=False))
                tok = b.op("pe", K.mm(psQM, self.ones16[0:1, 0:128],
                                      self.rows16[0:1, 0:8],
                                      start=False, stop=True), inc="pe")
                b.wait("act", "pe", tok)
                b.op("act", K.act(sm[:, 33:41], psQM, AF.Tanh))
                ta = b.op("act", K.act(sm[:, 41:49], psQM, AF.Abs,
                                       scale=PI / 4.0), inc="act")
                b.wait("dve", "act", ta)
                td = b.op("dve", K.ts(sm[:, 41:49], sm[:, 41:49], 1.5 * PI,
                                      op0=OP.min), inc="dve")
                b.wait("act", "dve", td)
                ta = b.op("act", K.act(sm[:, 41:49], sm[:, 41:49], AF.Sin,
                                       bias=self.cc[:, 2:3], scale=-1.0), inc="act")
                b.wait("dve", "act", ta)
                tq = b.op("dve", K.tt(sm[:, 41:49], sm[:, 33:41], sm[:, 41:49],
                                      OP.mult), inc="dve")
                b.wait("pe", "dve", tq)
                if t > 0:
                    b.wait("pe", "act", t_qeT[t - 1])
                tok = b.op("pe", K.tp(self.PST[0:8, 0:128], sm[:, 41:49],
                                      self.ident32[:, :]), inc="pe")
                b.wait("act", "pe", tok)
                t_qeT[t] = b.op("act", K.acp(self.qeT[0:8, t * 128:(t + 1) * 128],
                                             self.PST[0:8, 0:128]), inc="act")
            for t in range(8):
                T = gi * 8 + t
                xb = t % 2
                b.wait("pe", "act", t_qeT[t])
                if self.t_ff_last:
                    b.wait("pe", "dve", self.t_ff_last)
                tok = None
                for hh in range(2):
                    b.op("pe", K.mm(psFF[:, hh * 512:(hh + 1) * 512],
                                    self.qeT[0:8, t * 128:(t + 1) * 128],
                                    self.W2f16[0:8, hh * 512:(hh + 1) * 512],
                                    start=True, stop=False))
                    b.op("pe", K.mm(psFF[:, hh * 512:(hh + 1) * 512],
                                    self.ones16[0:1, 0:128],
                                    self.be1C[0:1, hh * 512:(hh + 1) * 512],
                                    start=False, stop=False))
                    tok = b.op("pe", K.mm(psFF[:, hh * 512:(hh + 1) * 512],
                                          self.ones16[0:1, 0:128],
                                          self.bf2C[0:1, hh * 512:(hh + 1) * 512],
                                          start=False, stop=True),
                               inc="pe" if hh == 1 else None)
                b.op("dve", K.tt(self.x1a(t), self.x1a(t), self.g1bc, OP.mult))
                b.wait("dve", "pe", tok)
                t_pre[xb] = b.op("dve", K.tt(self.preb[xb], psFF, self.x1a(t),
                                             OP.add), inc="dve")
                self.t_ff_last = t_pre[xb]
                b.wait("act", "dve", t_pre[xb])
                b.op("act", K.act(self.trashC, self.preb[xb], AF.Square,
                                  accum=sm[:, 24 + xb:25 + xb]))
                ta = b.op("act", K.act(self.trashC, self.preb[xb], AF.Copy,
                                       accum=sm[:, 26 + xb:27 + xb]), inc="act")
                b.wait("dve", "act", ta)
                b.op("dve", K.ts(sm[:, 28:29], sm[:, 26 + xb:27 + xb], 1.0 / 1024.0))
                b.op("dve", K.ts(sm[:, 29:30], sm[:, 24 + xb:25 + xb], 1.0 / 1024.0))
                b.op("dve", K.tt(sm[:, 30:31], sm[:, 28:29], sm[:, 28:29], OP.mult))
                td = b.op("dve", K.tt(sm[:, 31:32], sm[:, 29:30], sm[:, 30:31],
                                      OP.subtract), inc="dve")
                b.wait("act", "dve", td)
                b.op("act", K.act(sm[:, 31:32], sm[:, 31:32], AF.Ln, bias=self.cc[:, 0:1]))
                ta2 = b.op("act", K.act(sm[:, 32:33], sm[:, 31:32], AF.Exp,
                                        scale=-0.5), inc="act")
                b.wait("dve", "act", ta2)
                ts2 = b.op("dve", K.ts(self.x1a(t), self.preb[xb], sm[:, 28:29],
                                       sm[:, 32:33], op0=OP.subtract, op1=OP.mult),
                           inc="dve")
                b.wait("pool", "dve", ts2)
                if t_sh[xb]:
                    b.wait("pool", "dve", t_ld[xb])
                b.op("pool", K.tt(self.shb[xb], self.x1a(t), self.g2bc, OP.mult))
                t_sh[xb] = b.op("pool", K.tt(self.shb[xb], self.shb[xb], self.b2bc,
                                             OP.add), inc="pool")
                tok_x = b.dld("dxx%d" % xb, ("dve", t_ld[xb]),
                              K.dma(xld[xb], d["x"][T * 128:(T + 1) * 128, :]))
                tok_g = b.dld("dgl%d" % xb, ("dve", t_ld[xb]),
                              K.dma(gld[xb], d["g_sc"][T * 128:(T + 1) * 128, :]))
                b.wait("dve", "pool", t_sh[xb])
                b.wait("dve", "dgl%d" % xb, tok_g)
                if t_out[xb]:
                    b.wait("dve", "so%d" % xb, t_out[xb])
                b.op("dve", K.tt(self.outb[xb], self.shb[xb], gld[xb], OP.mult))
                b.wait("dve", "dxx%d" % xb, tok_x)
                t_ld[xb] = b.op("dve", K.tt(self.outb[xb], self.outb[xb], xld[xb],
                                            OP.add), inc="dve")
                b.wait("pool", "dve", t_ld[xb])
                t_out[xb] = b.op("pool", K.dma(d["out"][T * 128:(T + 1) * 128, :],
                                               self.outb[xb]),
                                 inc="so%d" % xb, n=16)
        b.wait("pool", "so0", self.cnt["so0"])
        b.wait("pool", "so1", self.cnt["so1"])


def build_kernel(dbg=()):
    k = K(dbg)
    nc = k.build()
    n = sum(len(k.q[e]) for e in ENGS)
    return nc, n


def kernel(**kw):
    x = np.ascontiguousarray(kw["L0_attn_x"], dtype=np.float32)
    nc, _ = build_kernel()
    const = {"c_ident": np.eye(128, dtype=np.float32)}
    in_maps = []
    for core in range(8):
        m = {k2: np.ascontiguousarray(v, dtype=np.float32)
             for k2, v in kw.items() if k2 != "L0_attn_x"}
        m["L0_attn_x"] = x[core]
        m.update(const)
        in_maps.append(m)
    res = run_bass_kernel_spmd(nc, in_maps, core_ids=list(range(8)))
    return np.stack([res.results[c]["out"] for c in range(8)], axis=0)


# version 08
# speedup vs baseline: 1.0377x; 1.0377x over previous
import math
import numpy as np
import concourse.bass as bass
import concourse.mybir as mybir
from concourse.bass_utils import run_bass_kernel_spmd
from concourse.alu_op_type import AluOpType as OP
from contextlib import ExitStack

F32 = mybir.dt.float32
F16 = mybir.dt.float16
AF = mybir.ActivationFunctionType

NT = 32
LN_EPS = 1e-5
C1 = 0.5 / math.sqrt(128.0)
WC2Q_SCALE = 1.0 / C1
LAM = 0.05
SIN_SCALE = math.pi / LAM
PI = math.pi
ENGS = ("pe", "act", "dve", "pool", "syncq")
CSEMS = ("pe", "act", "dve", "pool")


class K:
    def __init__(self, dbg=()):
        self.nc = bass.Bass("TRN2", target_bir_lowering=False)
        self.dbg = dbg
        self.q = {e: [] for e in ENGS}
        self.pe_labels = {}
        self.cnt = {}
        self.sem = {}
        self.lastw = {}

    def op(self, eng, fn, inc=None, n=1):
        if eng in ("act", "dve", "pool"):
            if self.cnt[eng] > 0:
                self.wait(eng, eng, self.cnt[eng])
            if inc is None:
                inc = eng
        elif eng == "pe" and inc is None:
            inc = "pe"
        if inc is None:
            self.q[eng].append(fn)
            return None
        tok = self.cnt[inc] + n
        self.cnt[inc] = tok
        if inc == "pe":
            import traceback as _tb
            for fr in reversed(_tb.extract_stack(limit=8)):
                if fr.name.startswith(("_phase", "_wfuse", "_gf", "_cprep",
                                       "_init", "_load")):
                    self.pe_labels[tok] = (fr.name, fr.lineno)
                    break
        sem = self.sem[inc]
        self.q[eng].append(lambda e, fn=fn, sem=sem, n=n: fn(e).then_inc(sem, n))
        return tok

    def wait(self, eng, semname, tok):
        if not tok or tok <= 0:
            return
        key = (eng, semname)
        if self.lastw.get(key, 0) >= tok:
            return
        self.lastw[key] = tok
        sem = self.sem[semname]
        self.q[eng].append(lambda e, sem=sem, tok=tok: e.wait_ge(sem, tok))

    STSEMS = ("sx0", "sx1", "sg", "sa0", "sa1", "so0", "so1")

    def fence(self):
        snap = {s: self.cnt[s] for s in CSEMS}
        stsnap = {s: self.cnt[s] for s in K.STSEMS}
        for eng in ("pe", "act", "dve", "pool", "syncq"):
            for s in CSEMS:
                if eng != s:
                    self.wait(eng, s, snap[s])
            for s in K.STSEMS:
                self.wait(eng, s, stsnap[s])

    def dld(self, semname, gate, fn, queue="syncq"):
        if gate is not None and gate[1]:
            self.wait(queue, gate[0], gate[1])
        return self.op(queue, fn, inc=semname, n=16)

    def dil(self, fn):
        tok = self.dld("di", self.di_gate, fn)
        return tok

    def di_done(self, eng_sem, tok):
        self.di_gate = (eng_sem, tok)

    @staticmethod
    def mm(psl, lhsT, rhs, start=True, stop=True):
        return lambda e: e.matmul(psl, lhsT, rhs, start=start, stop=stop)

    @staticmethod
    def tp(psl, in_, idn):
        return lambda e: e.transpose(psl, in_, idn)

    @staticmethod
    def act(out, in_, f, bias=0.0, scale=1.0, accum=None):
        return lambda e: e.activation(out, in_, f, bias=bias, scale=scale,
                                      accum_out=accum)

    @staticmethod
    def tt(out, a, c, op):
        return lambda e: e.tensor_tensor(out, a, c, op=op)

    @staticmethod
    def ts(out, a, s1, s2=None, op0=OP.mult, op1=None):
        if s2 is None:
            return lambda e: e.tensor_scalar(out, a, s1, None, op0=op0)
        return lambda e: e.tensor_scalar(out, a, s1, s2, op0=op0, op1=op1)

    @staticmethod
    def stt(out, in0, sc, in1, op0, op1):
        return lambda e: e.scalar_tensor_tensor(out, in0, sc, in1, op0=op0, op1=op1)

    @staticmethod
    def cp(out, in_):
        return lambda e: e.tensor_copy(out, in_)

    @staticmethod
    def acp(out, in_):
        return lambda e: e.copy(out, in_)

    @staticmethod
    def dma(out, in_):
        return lambda e: e.dma_start(out, in_)

    @staticmethod
    def dmas(out, in_):
        return lambda e: e.dma_start(out, in_, allow_slow_non_contiguous=True)

    @staticmethod
    def ap(t, off, dims):
        return bass.AP(t, off, dims)

    @staticmethod
    def f32ap(t16, off32, dims32):
        d16 = [[s * 2, c] for s, c in dims32[:-1]] + [[1, dims32[-1][1] * 2]]
        return bass.AP(t16, off32 * 2, d16).bitcast(F32)

    def build(self):
        nc = self.nc

        def din(name, shape):
            return nc.dram_tensor(name, shape, F32, kind="ExternalInput").ap()

        d = self.d = {}
        d["x"] = din("L0_attn_x", [4096, 1024])
        for nm, sh in [("Wq", [8, 1024, 128]), ("Wk", [8, 1024, 128]),
                       ("Wv", [8, 1024, 128]), ("Wc2q", [8, 128, 8]),
                       ("Wo", [1024, 1024]), ("W1", [1024, 4096]),
                       ("Wm", [4096, 8]), ("Wqo", [8, 4096]), ("W2", [4096, 1024]),
                       ("bq", [8, 128]), ("bk", [8, 128]), ("bv", [8, 128]),
                       ("bc2q", [8, 8]), ("bo", [1024]), ("g1", [1024]),
                       ("beta1", [1024]), ("g2", [1024]), ("beta2", [1024]),
                       ("b1", [4096]), ("bm", [8]), ("bqo", [4096]), ("b2", [1024])]:
            d[nm] = din("L0_attn_" + nm, sh)
        for i, sh in enumerate([[1024, 1024], [1024, 512], [512, 256], [256, 1]]):
            d[f"cnw{i+1}"] = din(f"L0_ffn_cn_w{i+1}", sh)
        for i, sh in enumerate([1024, 512, 256, 1]):
            d[f"cnb{i+1}"] = din(f"L0_ffn_cn_b{i+1}", [sh])
        for i, sh in enumerate([[1024, 512], [512, 256], [256, 1]]):
            d[f"gcw{i+1}"] = din(f"L0_ffn_gc_w{i+1}", sh)
        for i, sh in enumerate([512, 256, 1]):
            d[f"gcb{i+1}"] = din(f"L0_ffn_gc_b{i+1}", [sh])
        d["ar"] = din("L0_ffn_adaptive_range", [1])
        d["ident"] = din("c_ident", [128, 128])
        d["out"] = nc.dram_tensor("out", [4096, 1024], F32, kind="ExternalOutput").ap()
        ks = lambda nm: "ExternalOutput" if nm in self.dbg else "Internal"
        d["xn_sc"] = nc.dram_tensor("xn_sc", [4096, 1024], F32, kind=ks("xn_sc")).ap()
        d["g_sc"] = nc.dram_tensor("g_sc", [4096, 1024], F32, kind=ks("g_sc")).ap()
        d["attn_sc"] = nc.dram_tensor("attn_sc", [128, 8 * 4096], F16,
                                      kind=ks("attn_sc")).ap()
        d["rb_sc"] = nc.dram_tensor("rb_sc", [1, 512], F32, kind="Internal").ap()

        with ExitStack() as st:
            self._alloc(st)
            self._emit()
            with nc.Block() as block:
                @block.tensor
                def _(e):
                    for fn in self.q["pe"]:
                        fn(e)

                @block.scalar
                def _(e):
                    for fn in self.q["act"]:
                        fn(e)

                @block.vector
                def _(e):
                    for fn in self.q["dve"]:
                        fn(e)

                @block.gpsimd
                def _(e):
                    for fn in self.q["pool"]:
                        fn(e)

                @block.sync
                def _(e):
                    for fn in self.q["syncq"]:
                        fn(e)
        return nc

    def _alloc(self, st):
        nc = self.nc

        def sb(name, shape, dt):
            t = st.enter_context(nc.sbuf_tensor(name, shape, dt))
            setattr(self, name, t)
            return t

        sb("arA", [128, 32768], F16)
        sb("arB", [128, 16384], F16)
        sb("arC", [128, 16384], F16)
        sb("arD", [128, 6144], F32)
        sb("arE", [128, 13312], F16)
        sb("arW", [128, 8192], F16)
        sb("rows16", [1, 64], F16)
        sb("cols32", [128, 160], F32)
        sb("cols16", [128, 80], F16)
        sb("small32", [128, 192], F32)
        sb("sqt", [128, 512], F32)
        sb("sc32", [1, 16], F32)
        sb("Wm16", [128, 256], F16)
        sb("Wc2qc16", [128, 64], F16)
        sb("Wc2q16", [128, 64], F16)
        sb("Wf16", [128, 64], F16)
        sb("W2f16", [8, 1024], F16)
        sb("WqoT16", [128, 256], F16)
        sb("ident32", [128, 128], F32)
        sb("ones32", [128, 128], F32)
        sb("ones16", [128, 128], F16)
        sb("ldst", [32, 128], F32)
        sb("w2ftc", [128, 64], F32)
        sb("cc", [128, 4], F32)
        self.PSF = st.enter_context(nc.psum_tensor("psf", [128, 3584], F32))
        self.PST = st.enter_context(nc.psum_tensor("pst", [128, 512], F32))
        allsems = ["pe", "act", "dve", "pool", "di",
                   "dx0", "dx1", "dx2", "dw0", "dw1", "dw2",
                   "dg0", "dg1", "dg2", "da0", "da1", "dn0", "dn1",
                   "dxx0", "dxx1", "dgl0", "dgl1",
                   "sx0", "sx1", "sg", "sa0", "sa1", "so0", "so1",
                   "sr0", "sr1"]
        for s in allsems:
            self.sem[s] = st.enter_context(nc.semaphore("s_" + s))
            self.cnt[s] = 0
        self.di_gate = None

        A, B, C, D, E = self.arA, self.arB, self.arC, self.arD, self.arE
        self.xn_fm = A
        self.ein = B
        self.qc_fm = C[:, 0:4096]
        self.k_fm = C[:, 4096:8192]
        self.v_tm = C[:, 8192:12288]
        self.qh = C[0:8, 12288:16384]
        self.kl = D[0:8, 0:2048].bitcast(F16)
        self.s2bp = [E[:, 0:2048], E[:, 2048:4096], E[:, 4096:6144]]
        self.mq16 = [E[:, 6144:7168], E[:, 9216:10240], E[:, 10496:11520]]
        self.e16 = [E[:, 4096:5120], E[:, 5120:6144], E[:, 6144:7168],
                    E[:, 9216:10240]]
        self.attn_t = [E[:, 7168:7680], E[:, 7680:8192]]
        self.rb_s = E[:, 8192:9216].bitcast(F32)
        self.xn32 = [E[:, 9216:11264].bitcast(F32), E[:, 11264:13312].bitcast(F32)]
        self.qk32 = E[:, 10496:11520].bitcast(F32)
        self.bcqpat = E[0:1, 11520:12032]
        self.gcb1r = B[0:1, 2048:2560]
        self.gcb2r = B[0:1, 2560:2816]
        self.mean16 = B[:, 2816:2824]
        self.ucol16 = B[:, 4608:4640]
        self.nsq = self.small32[:, 64:128]
        self.rs = self.small32[:, 128:192]
        self.Dv = lambda o, n: D[:, o:o + n]
        self.rbuf = D[0:1, 320:1344]
        self.cnw2 = E[:, 9216:13312]
        self.cnw3 = D[:, 5120:5632].bitcast(F16)
        self.cnw4 = self.sqt[:, 0:1].bitcast(F16)
        self.gcw2 = D[:, 5632:6144].bitcast(F16)
        self.gcw3 = self.sqt[:, 1:2].bitcast(F16)
        self.og = B[:, 0:8192].bitcast(F32)
        self.h1g = B[:, 8192:12288]
        self.h2g = B[:, 12288:14336]
        self.h3g = B[:, 14336:15360]
        self.out16 = E[:, 0:4096]
        self.twt = [E[:, 4096:6144].bitcast(F32), E[:, 6144:8192].bitcast(F32)]
        self.th16 = E[:, 4096:8192]
        self.h1row = B[0:1, 0:1024].bitcast(F32)
        self.h2row = B[0:1, 1024:1536].bitcast(F32)
        self.h1T = B[:, 1536:1540]
        self.h2T = B[:, 1540:1542]
        self.g1bc = C[:, 0:2048].bitcast(F32)
        self.g2bc = C[:, 2048:4096].bitcast(F32)
        self.b2bc = C[:, 4096:6144].bitcast(F32)
        self.boC = C[0:1, 6144:7168]
        self.be1C = C[0:1, 7168:8192]
        self.bf2C = C[0:1, 8192:9216]
        self.astg = [C[:, 9216:10240], C[:, 10240:11264]]
        self.qeT = C[0:8, 11264:12288]
        self.x1a = lambda t: self.f32ap(A, t * 1024, [[16384, 128], [1, 1024]])
        self.x1aT = lambda t: A[:, 16384 + t * 1024:16384 + (t + 1) * 1024]
        self.preb = [B[:, 0:2048].bitcast(F32), B[:, 2048:4096].bitcast(F32)]
        self.shb = [B[:, 4096:6144].bitcast(F32), B[:, 6144:8192].bitcast(F32)]
        self.outb = [B[:, 8192:10240].bitcast(F32), B[:, 10240:12288].bitcast(F32)]
        self.trashC = E[:, 0:2048].bitcast(F32)

    def _emit(self):
        self._init_consts()
        self._phase_a()
        self._gf_net()
        self.fence()
        self._wfuse1()
        self.fence()
        self._wfuse2()
        self.fence()
        self._load_cn()
        self._phase_g()
        self.fence()
        self._phase_b()
        self.fence()
        self._cprep()
        self.fence()
        self._phase_c()

    def col(self, nm, j0=0, nj=None):
        o, n = self.CL[nm]
        if nj is None:
            nj = n
        return self.cols32[:, o + j0:o + j0 + nj]

    def _init_consts(self):
        b, d = self, self.d
        self.CL = {}
        offc = 0
        for nm, n in [("bq", 8), ("bk", 8), ("bv", 8), ("g1c", 8), ("be1c", 8),
                      ("b1c", 32), ("bqoc", 32), ("cb1", 8), ("cb2", 4), ("cb3", 2),
                      ("b2c", 8), ("bf2c", 8), ("pat2", 2)]:
            self.CL[nm] = (offc, n)
            offc += n
        assert offc <= 160

        tok = b.dil(K.dma(self.ident32[:, :], d["ident"][:, :]))
        b.wait("dve", "di", tok)
        td = b.op("dve", lambda e: e.memset(self.ones32[:, :], 1.0), inc="dve")
        b.di_done("dve", td)
        self.t_ident = tok
        b.op("dve", lambda e: e.memset(self.ones16[:, :], 1.0))
        b.op("dve", lambda e: e.memset(self.col("pat2", 0, 1), 1.0))
        b.op("dve", lambda e: e.memset(self.col("pat2", 1, 1), LAM))
        b.op("dve", lambda e: e.memset(self.cc[:, 0:1], LN_EPS))
        b.op("dve", lambda e: e.memset(self.cc[:, 1:2], 1e-12))
        b.op("dve", lambda e: e.memset(self.cc[:, 2:3], PI / 2.0), inc="dve")

        for srcn, cell in [("ar", 0), ("gcb3", 1), ("cnb4", 2)]:
            tok = b.dil(K.dma(self.sc32[0:1, cell:cell + 1], d[srcn][None, :]))
            b.wait("dve", "di", tok)
            td = b.op("dve", lambda e: e.nop(), inc="dve")
            b.di_done("dve", td)
        b.op("dve", K.ts(self.sc32[0:1, 4:5], self.sc32[0:1, 0:1], 2.0))
        b.op("dve", K.ts(self.sc32[0:1, 7:8], self.sc32[0:1, 0:1], -1.0, 0.1,
                         op0=OP.mult, op1=OP.add))
        b.op("dve", K.ts(self.sc32[0:1, 5:6], self.sc32[0:1, 2:3], 0.5))
        b.op("dve", K.ts(self.sc32[0:1, 6:7], self.sc32[0:1, 1:2], 0.5), inc="dve")

        for src, n, dst in [(d["bm"], 8, self.rows16[0:1, 8:16]),
                            (d["gcb1"], 512, self.gcb1r),
                            (d["gcb2"], 256, self.gcb2r)]:
            tok = b.dil(K.dma(self.rbuf[0:1, 0:n], src[None, :]))
            b.wait("dve", "di", tok)
            td = b.op("dve", K.cp(dst, self.rbuf[0:1, 0:n]), inc="dve")
            b.di_done("dve", td)
        tok = b.dil(K.dma(self.rbuf[0:1, 0:64],
                          K.ap(d["bc2q"].tensor, 0, [[64, 1], [1, 64]])))
        b.wait("dve", "di", tok)
        td = b.op("dve", K.cp(self.cols16[0:1, 0:64], self.rbuf[0:1, 0:64]),
                  inc="dve")
        b.di_done("dve", td)
        self.bc2q16 = self.cols16[0:1, 0:64]
        self.bqo16 = self.cols16[:, 32:64]

        def load_cols(nm, src_vec, L, two_d=False):
            nchunk = L // 128
            if two_d:
                ap_in = K.ap(src_vec.tensor, 0, [[128, nchunk], [1, 128]])
            else:
                ap_in = K.ap(src_vec.tensor, src_vec.offset,
                             [[128, nchunk], [1, 128]])
            tok = b.dil(K.dma(self.ldst[0:nchunk, :], ap_in))
            b.wait("pe", "di", tok)
            b.wait("pe", "di", self.t_ident)
            b.wait("pe", "dve", self.cnt["dve"])
            tpe = b.op("pe", K.tp(self.PST[:, 0:nchunk], self.ldst[0:nchunk, :],
                                  self.ident32[0:nchunk, 0:nchunk]), inc="pe")
            b.di_done("pe", tpe)
            b.wait("dve", "pe", tpe)
            return b.op("dve", K.cp(self.col(nm), self.PST[:, 0:nchunk]), inc="dve")

        for nm, src, L in [("g1c", d["g1"], 1024), ("be1c", d["beta1"], 1024),
                           ("b1c", d["b1"], 4096), ("bqoc", d["bqo"], 4096),
                           ("cb1", d["cnb1"], 1024), ("cb2", d["cnb2"], 512),
                           ("cb3", d["cnb3"], 256), ("b2c", d["b2"], 1024)]:
            load_cols(nm, src, L)
        for nm, src in [("bq", d["bq"]), ("bk", d["bk"]), ("bv", d["bv"])]:
            load_cols(nm, src, 1024, two_d=True)
        tok = self.cnt["dve"]
        b.wait("pool", "dve", tok)
        b.op("pool", K.cp(self.bqo16, self.col("bqoc")), inc="pool")

        tok = b.dil(K.dma(self.Dv(2048, 256),
                          K.ap(d["Wm"].tensor, 0, [[8, 128], [1024, 32], [1, 8]])))
        b.wait("pool", "di", tok)
        tp_ = b.op("pool", K.cp(self.Wm16[:, :], self.Dv(2048, 256)), inc="pool")
        b.di_done("pool", tp_)
        tok = b.dil(K.dma(self.Dv(2304, 64),
                          K.ap(d["Wc2q"].tensor, 0, [[8, 128], [1024, 8], [1, 8]])))
        b.wait("dve", "di", tok)
        b.op("dve", K.ts(self.Wc2qc16[:, :], self.Dv(2304, 64), WC2Q_SCALE),
             inc="dve")
        b.wait("pool", "di", tok)
        tp_ = b.op("pool", K.cp(self.Wc2q16[:, :], self.Dv(2304, 64)), inc="pool")
        b.di_done("pool", tp_)

        self.gcw1 = self.arW[:, 0:4096]
        b.wait("syncq", "dve", self.cnt["dve"])
        tok = b.dil(K.dma(self.arD[:, 0:4096],
                          K.ap(d["gcw1"].tensor, 0,
                               [[512, 128], [512 * 128, 8], [1, 512]])))
        b.wait("pool", "di", tok)
        tp_ = b.op("pool", K.cp(self.gcw1, self.arD[:, 0:4096]), inc="pool")
        b.di_done("pool", tp_)
        tok = b.dil(K.dma(self.Dv(4224, 1024),
                          K.ap(d["gcw2"].tensor, 0,
                               [[256, 128], [256 * 128, 4], [1, 256]])))
        b.wait("pool", "di", tok)
        tp_ = b.op("pool", K.cp(self.gcw2[:, 0:1024], self.Dv(4224, 1024)),
                   inc="pool")
        b.di_done("pool", tp_)
        tok = b.dil(K.dmas(self.Dv(5378, 2),
                           K.ap(d["gcw3"].tensor, 0, [[1, 128], [128, 2], [1, 1]])))
        b.wait("pool", "di", tok)
        self.t_gcinit = b.op("pool", K.cp(self.gcw3[:, 0:2], self.Dv(5378, 2)),
                             inc="pool")
        b.di_done("pool", self.t_gcinit)

    def _phase_a(self):
        b, d = self, self.d
        PS_MEAN = self.PSF[:, 3072:3080]
        trash = self.Dv(3072, 1024)
        xins = [self.Dv(0, 1024), self.Dv(1024, 1024), self.Dv(2048, 1024)]
        sm = self.small32
        self.tA_st = [0] * NT
        tA_ev = {}
        t_mm = [0] * NT
        tA_xn = [0] * NT
        b.wait("syncq", "pool", self.t_gcinit)
        b.wait("act", "pool", self.t_gcinit)
        b.wait("dve", "pool", self.t_gcinit)
        b.wait("pe", "pool", self.t_gcinit)
        for i in range(NT):
            xb = i % 2
            xin = xins[i % 3]
            dxs = "dx%d" % (i % 3)
            txi = b.dld(dxs, ("dve", tA_xn[i - 3] if i >= 3 else 0),
                        K.dma(xin, d["x"][i * 128:(i + 1) * 128, :]))
            b.wait("act", dxs, txi)
            if i >= 2:
                b.wait("act", "dve", tA_xn[i - 2])
            sq = sm[:, 0 + xb:1 + xb]
            ms = sm[:, 2 + xb:3 + xb]
            b.op("act", K.act(trash, xin, AF.Square, accum=sq))
            ta = b.op("act", K.act(trash, xin, AF.Copy, accum=ms), inc="act")
            b.wait("dve", "act", ta)
            m = sm[:, 4 + xb:5 + xb]
            rstd = sm[:, 9 + xb:10 + xb]
            b.op("dve", K.ts(m, ms, 1.0 / 1024.0))
            b.op("dve", K.ts(sm[:, 6:7], sq, 1.0 / 1024.0))
            b.op("dve", K.tt(sm[:, 7:8], m, m, OP.mult))
            td = b.op("dve", K.tt(sm[:, 8:9], sm[:, 6:7], sm[:, 7:8], OP.subtract),
                      inc="dve")
            b.wait("act", "dve", td)
            b.op("act", K.act(sm[:, 8:9], sm[:, 8:9], AF.Ln, bias=self.cc[:, 0:1]))
            ta2 = b.op("act", K.act(rstd, sm[:, 8:9], AF.Exp, scale=-0.5), inc="act")
            b.wait("dve", "act", ta2)
            if i >= 2:
                b.wait("dve", "pe", t_mm[i - 2])
                b.wait("dve", "sx%d" % xb, self.tA_st[i - 2])
            tA_xn[i] = b.op("dve", K.ts(self.xn32[xb], xin, m, rstd,
                                        op0=OP.subtract, op1=OP.mult), inc="dve")
            b.wait("pool", "dve", tA_xn[i])
            self.tA_st[i] = b.op("pool", K.dma(d["xn_sc"][i * 128:(i + 1) * 128, :],
                                               self.xn32[xb]),
                                 inc="sx%d" % xb, n=16)
            b.wait("pe", "dve", tA_xn[i])
            b.wait("pe", "di", self.t_ident)
            for r in range(2):
                if (i, r) != (0, 0):
                    prev = tA_ev[(i, 0)] if r == 1 else tA_ev[(i - 1, 1)]
                    b.wait("pe", "dve", prev)
                tok = None
                for c4 in range(4):
                    c = 4 * r + c4
                    tok = b.op("pe", K.tp(self.PST[:, c4 * 128:(c4 + 1) * 128],
                                          self.xn32[xb][:, c * 128:(c + 1) * 128],
                                          self.ident32[:, :]),
                               inc="pe" if c4 == 3 else None)
                b.wait("dve", "pe", tok)
                dst = K.ap(self.arA, (4 * r) * 4096 + i * 128,
                           [[32768, 128], [4096, 4], [1, 128]])
                src = K.ap(self.PST, 0, [[512, 128], [128, 4], [1, 128]])
                tA_ev[(i, r)] = b.op("dve", K.cp(dst, src), inc="dve")
            for c in range(8):
                t_mm[i] = b.op("pe", K.mm(PS_MEAN[:, c:c + 1],
                                          self.xn32[xb][:, c * 128:(c + 1) * 128],
                                          self.ones32[:, 0:1],
                                          start=(i == 0 and c == 0),
                                          stop=(i == NT - 1 and c == 7)),
                               inc="pe" if c == 7 else None)
        b.wait("dve", "pe", t_mm[NT - 1])
        self.t_mean16 = b.op("dve", K.ts(self.mean16, PS_MEAN, 1.0 / 4096.0),
                             inc="dve")

    def _gf_net(self):
        b = self
        PSG = self.PSF[0:1, 0:512]
        b.wait("pe", "dve", self.t_mean16)
        b.wait("pe", "pool", self.t_gcinit)
        for c in range(8):
            b.op("pe", K.mm(PSG, self.mean16[:, c:c + 1],
                            self.gcw1[:, c * 512:(c + 1) * 512],
                            start=(c == 0), stop=False))
        tok = b.op("pe", K.mm(PSG, self.ones16[0:1, 0:1], self.gcb1r,
                              start=False, stop=True), inc="pe")
        b.wait("act", "pe", tok)
        ta = b.op("act", K.act(self.h1row, PSG, AF.Gelu), inc="act")
        b.wait("pe", "act", ta)
        tok = None
        for c in range(4):
            tok = b.op("pe", K.tp(self.PST[:, c:c + 1],
                                  self.h1row[0:1, c * 128:(c + 1) * 128],
                                  self.ident32[0:1, 0:1]),
                       inc="pe" if c == 3 else None)
        b.wait("dve", "pe", tok)
        td = b.op("dve", K.cp(self.h1T, self.PST[:, 0:4]), inc="dve")
        PSG2 = self.PSF[0:1, 512:768]
        b.wait("pe", "dve", td)
        for c in range(4):
            b.op("pe", K.mm(PSG2, self.h1T[:, c:c + 1],
                            self.gcw2[:, c * 256:(c + 1) * 256],
                            start=(c == 0), stop=False))
        tok = b.op("pe", K.mm(PSG2, self.ones16[0:1, 0:1], self.gcb2r,
                              start=False, stop=True), inc="pe")
        b.wait("act", "pe", tok)
        ta = b.op("act", K.act(self.h2row, PSG2, AF.Gelu), inc="act")
        b.wait("pe", "act", ta)
        tok = None
        for c in range(2):
            tok = b.op("pe", K.tp(self.PST[:, c:c + 1],
                                  self.h2row[0:1, c * 128:(c + 1) * 128],
                                  self.ident32[0:1, 0:1]),
                       inc="pe" if c == 1 else None)
        b.wait("dve", "pe", tok)
        td = b.op("dve", K.cp(self.h2T, self.PST[:, 0:2]), inc="dve")
        PSG3 = self.PSF[0:1, 768:769]
        b.wait("pe", "dve", td)
        tok = None
        for c in range(2):
            tok = b.op("pe", K.mm(PSG3, self.h2T[:, c:c + 1], self.gcw3[:, c:c + 1],
                                  start=(c == 0), stop=(c == 1)),
                       inc="pe" if c == 1 else None)
        b.wait("act", "pe", tok)
        ta = b.op("act", K.act(self.sc32[0:1, 9:10], PSG3, AF.Tanh,
                               bias=self.sc32[0:1, 6:7], scale=0.5), inc="act")
        b.wait("dve", "act", ta)
        self.t_gfc = b.op("dve", K.ts(self.sc32[0:1, 8:9], self.sc32[0:1, 9:10],
                                      0.35, 0.5, op0=OP.mult, op1=OP.add), inc="dve")

    def _wfuse1(self):
        b, d = self, self.d
        psWf = self.PSF[:, 3080:3144]
        psUT = self.PSF[:, 1024:1056]
        psBF1 = self.PSF[0:1, 1536:1544]
        halves = [self.Dv(0, 2048), self.Dv(2048, 2048)]
        w1g = self.arB[:, 0:4096].bitcast(F32)
        w1t16 = self.arB[:, 4096:4608]
        t_used = [0, 0]
        t_ev_last = 0
        for fc in range(8):
            t_h = [0, 0]
            for h in range(2):
                t_h[h] = b.dld("dw%d" % h, ("pe", t_used[h]),
                               K.dma(halves[h],
                                     d["W1"][fc * 128:(fc + 1) * 128,
                                             h * 2048:(h + 1) * 2048]))
            for h in range(2):
                b.wait("dve", "dw%d" % h, t_h[h])
                tg = b.op("dve", K.ts(w1g, halves[h], self.col("g1c", fc, 1)),
                          inc="dve")
                b.wait("pe", "dw%d" % h, t_h[h])
                for ms in range(16):
                    b.op("pe", K.mm(psUT[:, h * 16 + ms:h * 16 + ms + 1],
                                    halves[h][:, ms * 128:(ms + 1) * 128],
                                    self.col("be1c", fc, 1),
                                    start=(fc == 0 and h == 0 and ms == 0),
                                    stop=(fc == 7 and h == 1 and ms == 15)))
                b.wait("pe", "dve", tg)
                last_mm = 0
                for r in range(4):
                    if t_ev_last:
                        b.wait("pe", "dve", t_ev_last)
                    tok = None
                    for j in range(4):
                        ms = 4 * r + j
                        tok = b.op("pe", K.tp(self.PST[:, j * 128:(j + 1) * 128],
                                              w1g[:, ms * 128:(ms + 1) * 128],
                                              self.ident32[:, :]),
                                   inc="pe" if j == 3 else None)
                    b.wait("dve", "pe", tok)
                    t_ev_last = b.op("dve", K.cp(w1t16, self.PST[:, 0:512]),
                                     inc="dve")
                    b.wait("pe", "dve", t_ev_last)
                    for j in range(4):
                        ms = 4 * r + j
                        kk = h * 16 + ms
                        last_mm = b.op("pe", K.mm(
                            psWf[:, fc * 8:(fc + 1) * 8],
                            w1t16[:, j * 128:(j + 1) * 128],
                            self.Wm16[:, kk * 8:(kk + 1) * 8],
                            start=(kk == 0), stop=(kk == 31)),
                            inc="pe" if j == 3 else None)
                t_used[h] = last_mm
        t_wf = self.cnt["pe"]
        b.wait("dve", "pe", t_wf)
        b.op("dve", K.cp(self.Wf16[:, :], psWf))
        b.op("dve", K.tt(self.qk32[:, 0:32], psUT, self.col("b1c"), OP.add))
        tuc = b.op("dve", K.cp(self.ucol16, self.qk32[:, 0:32]), inc="dve")
        b.wait("pe", "dve", tuc)
        tok = None
        for ms in range(32):
            tok = b.op("pe", K.mm(psBF1, self.ucol16[:, ms:ms + 1],
                                  self.Wm16[:, ms * 8:(ms + 1) * 8],
                                  start=(ms == 0), stop=(ms == 31)),
                       inc="pe" if ms == 31 else None)
        b.wait("dve", "pe", tok)
        self.t_bf1 = b.op("dve", K.tt(self.rows16[0:1, 0:8], psBF1,
                                      self.rows16[0:1, 8:16], OP.add), inc="dve")

    def _wfuse2(self):
        b, d = self, self.d
        psW2 = self.PSF[:, 3200:3264]
        psBF2 = self.PSF[:, 3264:3272]
        wqost = self.arB[0:8, 0:4096].bitcast(F32)
        t_tp = 0
        t_wq = 0
        for h in range(2):
            tok = b.dld("dw2", ("dve", t_wq),
                        K.dma(wqost, d["Wqo"][0:8, h * 2048:(h + 1) * 2048]))
            b.wait("pe", "dw2", tok)
            for ms in range(16):
                t_tp = b.op("pe", K.tp(self.PST[:, ms * 8:(ms + 1) * 8],
                                       wqost[0:8, ms * 128:(ms + 1) * 128],
                                       self.ident32[0:8, 0:8]),
                            inc="pe" if ms == 15 else None)
            b.wait("dve", "pe", t_tp)
            t_wq = b.op("dve", K.cp(self.WqoT16[:, h * 128:(h + 1) * 128],
                                    self.PST[:, 0:128]), inc="dve")
        w2st = [self.Dv(0, 1024), self.Dv(1024, 1024)]
        w2c16 = [self.arB[:, 0:1024], self.arB[:, 1024:2048]]
        t_c = [t_tp, t_tp]
        b.wait("pe", "dve", t_wq)
        for k in range(NT):
            h = k % 2
            tok = b.dld("dw%d" % h, ("pe", t_c[h]),
                        K.dma(w2st[h], d["W2"][k * 128:(k + 1) * 128, :]))
            b.wait("pool", "dw%d" % h, tok)
            if t_c[h]:
                b.wait("pool", "pe", t_c[h])
            tp_ = b.op("pool", K.cp(w2c16[h], w2st[h]), inc="pool")
            b.wait("pe", "pool", tp_)
            tok2 = None
            for fc in range(8):
                b.op("pe", K.mm(psW2[:, fc * 8:(fc + 1) * 8],
                                w2c16[h][:, fc * 128:(fc + 1) * 128],
                                self.WqoT16[:, k * 8:(k + 1) * 8],
                                start=(k == 0 and fc == 0), stop=False))
                tok2 = b.op("pe", K.mm(psBF2[:, fc:fc + 1],
                                       w2c16[h][:, fc * 128:(fc + 1) * 128],
                                       self.bqo16[:, k:k + 1],
                                       start=False,
                                       stop=(k == NT - 1 and fc == 7)),
                            inc="pe" if fc == 7 else None)
            t_c[h] = tok2
        b.wait("dve", "pe", self.cnt["pe"])
        b.op("dve", K.cp(self.w2ftc[:, :], psW2))
        self.t_w2f = b.op("dve", K.tt(self.col("bf2c"), psBF2, self.col("b2c"),
                                      OP.add), inc="dve")

    def _load_cn(self):
        b, d = self, self.d
        self.cnw1 = self.arW
        t_cast = 0
        for c in range(4):
            tok = b.dld("dw0", ("pool", t_cast),
                        K.dma(self.Dv(0, 2048),
                              K.ap(d["cnw1"].tensor, c * 2 * 128 * 1024,
                                   [[1024, 128], [128 * 1024, 2], [1, 1024]])))
            b.wait("pool", "dw0", tok)
            t_cast = b.op("pool", K.cp(self.arW[:, c * 2048:(c + 1) * 2048],
                                       self.Dv(0, 2048)), inc="pool")
        tok = b.dld("dw0", ("pool", t_cast),
                    K.dma(self.Dv(0, 4096),
                          K.ap(d["cnw2"].tensor, 0,
                               [[512, 128], [512 * 128, 8], [1, 512]])))
        b.wait("pool", "dw0", tok)
        t_cast = b.op("pool", K.cp(self.cnw2[:, 0:4096], self.Dv(0, 4096)),
                      inc="pool")
        tok = b.dld("dw0", ("pool", t_cast),
                    K.dma(self.Dv(0, 1024),
                          K.ap(d["cnw3"].tensor, 0,
                               [[256, 128], [256 * 128, 4], [1, 256]])))
        b.wait("pool", "dw0", tok)
        t_cast = b.op("pool", K.cp(self.cnw3[:, 0:1024], self.Dv(0, 1024)),
                      inc="pool")
        tok = b.dld("dw0", ("pool", t_cast),
                    K.dmas(self.Dv(2048, 2),
                           K.ap(d["cnw4"].tensor, 0, [[1, 128], [128, 2], [1, 1]])))
        b.wait("pool", "dw0", tok)
        self.t_cn = b.op("pool", K.cp(self.cnw4[:, 0:2], self.Dv(2048, 2)),
                         inc="pool")

    def _phase_g(self):
        b, d = self, self.d
        xld = [self.Dv(0, 1024), self.Dv(1024, 1024), self.Dv(2048, 1024)]
        gt = self.Dv(3072, 1024)
        PSH = [self.PSF[:, 0:512], self.PSF[:, 512:1024]]
        PSH2 = [self.PSF[:, 1024:1536], self.PSF[:, 1536:2048]]
        PSH3 = [self.PSF[:, 2048:2560], self.PSF[:, 2048:2560]]
        PSH4 = self.PSF[0:1, 2560:3072]
        PSCPL = self.PSF[:, 2560:3072]
        cbc = K.ap(self.arE, 8192, [[13312, 128], [0, 2], [1, 1024]]).bitcast(F32)
        b.wait("pe", "pool", self.t_cn)
        b.wait("pe", "dve", self.t_gfc)
        b.wait("syncq", "pool", self.t_cn)
        b.wait("syncq", "sx0", self.tA_st[30])
        b.wait("syncq", "sx1", self.tA_st[31])
        og2 = self.arC[:, 0:8192].bitcast(F32)
        slots = [dict(ar=self.arB, og=self.og, o16=self.out16),
                 dict(ar=self.arC, og=og2, o16=self.arC[:, 8192:12288])]
        t_st = 0
        t_dg = [0, 0, 0]
        t_pst = 0
        t_g = [0, 0]
        t_g2 = [0, 0]
        t_g3 = [0, 0]
        t_thr = 0
        t_cb = 0
        tc2 = 0
        t_tws1p = 0
        t_h1mm = [0, 0]
        t_o16 = [0, 0]

        def emit_cast(sidx):
            sl = slots[sidx]
            b.wait("dve", "pe", t_h1mm[sidx])
            t_o16[sidx] = b.op("dve", K.cp(sl["o16"], sl["og"]), inc="dve")

        def emit_round(sidx):
            nonlocal t_thr, t_cb, tc2, t_tws1p
            sl = slots[sidx]
            b.wait("pe", "dve", t_o16[sidx])
            for fc in range(8):
                if t_g[fc % 2]:
                    b.wait("pe", "act", t_g[fc % 2])
                tok = None
                for kc in range(8):
                    tok = b.op("pe", K.mm(
                        PSH[fc % 2],
                        self.cnw1[:, kc * 1024 + fc * 128:
                                  kc * 1024 + (fc + 1) * 128],
                        sl["o16"][:, kc * 512:(kc + 1) * 512],
                        start=(kc == 0), stop=(kc == 7)),
                        inc="pe" if kc == 7 else None)
                b.wait("act", "pe", tok)
                t_g[fc % 2] = b.op("act", K.act(
                    self.h1g[:, fc * 512:(fc + 1) * 512], PSH[fc % 2],
                    AF.Gelu, bias=self.col("cb1", fc, 1)), inc="act")
            t_h1mm[sidx] = self.cnt["pe"]
            t_h1 = self.cnt["act"]
            for fc in range(4):
                if t_g2[fc % 2]:
                    b.wait("pe", "act", t_g2[fc % 2])
                b.wait("pe", "act", t_h1)
                tok = None
                for kc in range(8):
                    tok = b.op("pe", K.mm(
                        PSH2[fc % 2],
                        self.cnw2[:, kc * 512 + fc * 128:
                                  kc * 512 + (fc + 1) * 128],
                        self.h1g[:, kc * 512:(kc + 1) * 512],
                        start=(kc == 0), stop=(kc == 7)),
                        inc="pe" if kc == 7 else None)
                b.wait("act", "pe", tok)
                t_g2[fc % 2] = b.op("act", K.act(
                    self.h2g[:, fc * 512:(fc + 1) * 512], PSH2[fc % 2],
                    AF.Gelu, bias=self.col("cb2", fc, 1)), inc="act")
            t_h2 = self.cnt["act"]
            for fc in range(2):
                if t_g3[(fc - 1) % 2]:
                    b.wait("pe", "act", t_g3[(fc - 1) % 2])
                b.wait("pe", "act", t_h2)
                tok = None
                for kc in range(4):
                    tok = b.op("pe", K.mm(
                        PSH3[fc % 2],
                        self.cnw3[:, kc * 256 + fc * 128:
                                  kc * 256 + (fc + 1) * 128],
                        self.h2g[:, kc * 512:(kc + 1) * 512],
                        start=(kc == 0), stop=(kc == 3)),
                        inc="pe" if kc == 3 else None)
                b.wait("act", "pe", tok)
                t_g3[fc % 2] = b.op("act", K.act(
                    self.h3g[:, fc * 512:(fc + 1) * 512], PSH3[fc % 2],
                    AF.Gelu, bias=self.col("cb3", fc, 1)), inc="act")
            t_h3 = self.cnt["act"]
            b.wait("pe", "act", t_h3)
            b.wait("pe", "act", t_thr)
            b.wait("pe", "dve", t_cb)
            tok = None
            for kc in range(2):
                tok = b.op("pe", K.mm(PSH4, self.cnw4[:, kc:kc + 1],
                                      self.h3g[:, kc * 512:(kc + 1) * 512],
                                      start=(kc == 0), stop=(kc == 1)),
                           inc="pe" if kc == 1 else None)
            b.wait("act", "pe", tok)
            b.wait("act", "dve", tc2)
            b.wait("act", "pool", t_tws1p)
            thr = self.rb_s[0:1, 0:512]
            ta = b.op("act", K.act(thr, PSH4, AF.Tanh, bias=self.sc32[0:1, 5:6],
                                   scale=0.5), inc="act")
            t_thr = ta
            b.wait("dve", "act", ta)
            b.op("dve", K.ts(thr, thr, 0.15, self.sc32[0:1, 8:9],
                             op0=OP.mult, op1=OP.add))
            tc = b.op("dve", K.ts(thr, thr, self.sc32[0:1, 4:5],
                                  self.sc32[0:1, 7:8], op0=OP.mult, op1=OP.add),
                      inc="dve")
            b.wait("pe", "dve", tc)
            b.wait("pe", "dve", t_cb)
            t_bc = b.op("pe", K.mm(PSCPL, self.ones32[0:1, 0:128], thr),
                        inc="pe")
            b.wait("dve", "pe", t_bc)
            t_cb = b.op("dve", K.cp(self.rb_s, PSCPL), inc="dve")
            tc2 = t_cb
            og = sl["og"]
            for pair in range(2):
                srcs = (2, 0) if pair == 0 else (3, 1)
                dsts = (0, 2) if pair == 0 else (1, 3)
                ops = (OP.add, OP.subtract) if pair == 0 else \
                    (OP.subtract, OP.add)
                b.wait("pool", "dve", tc2)
                tws = [0, 0]
                tws[0] = b.op("dve", K.tt(
                    self.twt[0], og[:, srcs[0] * 1024:(srcs[0] + 1) * 1024],
                    cbc, OP.mult), inc="dve")
                tws[1] = b.op("pool", K.tt(
                    self.twt[1], og[:, srcs[1] * 1024:(srcs[1] + 1) * 1024],
                    cbc, OP.mult), inc="pool")
                t_tws1p = tws[1]
                for j in range(2):
                    qq = dsts[j]
                    b.wait("dve", "pool", tws[1])
                    tc2 = b.op("dve", K.tt(
                        og[:, qq * 1024:(qq + 1) * 1024],
                        og[:, qq * 1024:(qq + 1) * 1024],
                        self.twt[j], ops[j]), inc="dve")

        for gp in range(4):
            b.wait("dve", "pe", self.cnt["pe"])
            b.wait("dve", "pool", self.cnt["pool"])
            for sidx in (0, 1):
                g = 2 * gp + sidx
                sl = slots[sidx]
                t_ev = 0
                for i in range(4):
                    T = g * 4 + i
                    slq = T % 3
                    tok = b.dld("dg%d" % slq, ("pe", t_dg[slq]),
                                K.dma(xld[slq],
                                      d["xn_sc"][T * 128:(T + 1) * 128, :]))
                    b.wait("pe", "dg%d" % slq, tok)
                    for r in range(2):
                        if t_pst:
                            b.wait("pe", "dve", t_pst)
                        tok2 = None
                        for c4 in range(4):
                            c = 4 * r + c4
                            tok2 = b.op("pe", K.tp(
                                self.PST[:, c4 * 128:(c4 + 1) * 128],
                                xld[slq][:, c * 128:(c + 1) * 128],
                                self.ident32[:, :]),
                                inc="pe" if c4 == 3 else None)
                        b.wait("dve", "pe", tok2)
                        dst = self.f32ap(sl["ar"], (4 * r) * 512 + i * 128,
                                         [[8192, 128], [512, 4], [1, 128]])
                        srcp = K.ap(self.PST, 0, [[512, 128], [128, 4], [1, 128]])
                        t_ev = b.op("dve", K.cp(dst, srcp), inc="dve")
                        t_pst = t_ev
                    t_dg[slq] = tok2
            for sidx in (0, 1):
                emit_cast(sidx)
            for r in range(3):
                for sidx in (0, 1):
                    emit_round(sidx)
                    if r < 2:
                        emit_cast(sidx)
            for sidx in (0, 1):
                g = 2 * gp + sidx
                sl = slots[sidx]
                th16 = sl["o16"]
                b.wait("act", "dve", self.cnt["dve"])
                b.wait("act", "pool", self.cnt["pool"])
                b.wait("act", "pe", t_h1mm[sidx])
                ta = b.op("act", K.act(th16, sl["og"], AF.Tanh, scale=0.5),
                          inc="act")
                b.wait("dve", "act", ta)
                b.op("dve", K.ts(th16, th16, 1.0, op0=OP.add))
                t_gg = b.op("dve", K.tt(sl["og"], sl["og"], th16, OP.mult),
                            inc="dve")
                b.wait("pe", "dve", t_gg)
                for i in range(4):
                    T = g * 4 + i
                    for r in range(2):
                        if t_pst:
                            b.wait("pe", "dve", t_pst)
                        tok2 = None
                        for c4 in range(4):
                            c = 4 * r + c4
                            tok2 = b.op("pe", K.tp(
                                self.PST[:, c4 * 128:(c4 + 1) * 128],
                                self.f32ap(sl["ar"], c * 512 + i * 128,
                                           [[8192, 128], [1, 128]]),
                                self.ident32[:, :]), inc="pe" if c4 == 3 else None)
                        b.wait("dve", "pe", tok2)
                        if t_st:
                            b.wait("dve", "sg", t_st)
                        dst = K.ap(self.arD, 3072 + (4 * r) * 128,
                                   [[6144, 128], [128, 4], [1, 128]])
                        srcp = K.ap(self.PST, 0, [[512, 128], [128, 4], [1, 128]])
                        t_ev2 = b.op("dve", K.cp(dst, srcp), inc="dve")
                        t_pst = t_ev2
                    b.wait("pool", "dve", t_ev2)
                    t_st = b.op("pool", K.dma(d["g_sc"][T * 128:(T + 1) * 128, :],
                                              gt), inc="sg", n=16)
        self.tG_gst = t_st

    def _phase_b(self):
        b, d = self, self.d
        psA_ = self.PSF[:, 0:1024]
        psB_ = [self.PSF[:, 1024:2048], self.PSF[:, 2048:3072]]
        psPV = self.PSF[:, 3072:3584]
        psS = self.PST[0:1, 0:512]
        psQK = self.PSF[:, 1024:1536]
        wst = [self.Dv(0, 1024), self.Dv(1024, 1024), self.Dv(2048, 1024)]
        wq16 = self.arW[:, 0:1024]
        wk16 = self.arW[:, 1024:2048]
        wv16 = self.arW[:, 2048:3072]
        vftmp = self.Dv(2048, 4096)
        sm = self.small32

        t_stt = [0] * 16
        t_tt = [0] * 16
        t_exp = [0] * 16
        t_pvs = [0] * 16
        t_sin = [0] * 16
        t_mmA = [0] * 16
        t_mmB = [0] * 16
        t_attn = [0, 0]
        t_recip = 0
        t_norm = 0
        t_sr1_roll = 0
        self.fpend = None

        t_pe_flash = 0
        for n in range(8):
            tw = []
            for wi, wd in enumerate([d["Wq"], d["Wk"], d["Wv"]]):
                tw.append(b.dld("dw%d" % wi, ("pe", t_pe_flash),
                                K.dma(wst[wi], K.ap(wd.tensor, n * 1024 * 128,
                                                    [[128, 128], [128 * 128, 8],
                                                     [1, 128]]))))
            b.wait("pool", "pe", t_pe_flash)
            for wi, w16 in enumerate([wq16, wk16, wv16]):
                b.wait("pool", "dw%d" % wi, tw[wi])
                b.op("pool", K.cp(w16, wst[wi]), inc="pool")
            t_w16 = self.cnt["pool"]
            b.wait("dve", "act", self.cnt["act"])
            b.op("dve", K.cp(self.bcqpat[0:1, 0:8],
                             self.bc2q16[0:1, n * 8:(n + 1) * 8]))
            b.op("dve", K.cp(self.bcqpat[0:1, 8:16],
                             self.bc2q16[0:1, n * 8:(n + 1) * 8]))
            w = 16
            while w < 512:
                b.op("dve", K.cp(self.bcqpat[0:1, w:2 * w], self.bcqpat[0:1, 0:w]))
                w *= 2
            t_pat = b.op("dve", lambda e: e.nop(), inc="dve")
            b.wait("pe", "pool", t_w16)
            b.wait("pe", "dve", self.cnt["dve"])
            t_ev = {}
            for wi in range(3):
                w16 = [wq16, wk16, wv16][wi]
                for sg in range(8):
                    ps = psB_[sg % 2]
                    prev = t_ev.get((wi, sg - 2), t_ev.get((wi - 1, 6 + sg), 0))
                    b.wait("pe", "dve", prev)
                    tok = None
                    for kc in range(8):
                        tok = b.op("pe", K.mm(
                            ps[:, 0:512], w16[:, kc * 128:(kc + 1) * 128],
                            self.xn_fm[:, kc * 4096 + sg * 512:
                                       kc * 4096 + (sg + 1) * 512],
                            start=(kc == 0), stop=(kc == 7)),
                            inc="pe" if kc == 7 else None)
                    b.wait("dve", "pe", tok)
                    if wi == 0:
                        fn = K.ts(self.qc_fm[:, sg * 512:(sg + 1) * 512],
                                  ps[:, 0:512], self.col("bq", n, 1), C1,
                                  op0=OP.add, op1=OP.mult)
                    elif wi == 1:
                        fn = K.ts(self.k_fm[:, sg * 512:(sg + 1) * 512],
                                  ps[:, 0:512], self.col("bk", n, 1), op0=OP.add)
                    else:
                        fn = K.ts(vftmp[:, sg * 512:(sg + 1) * 512], ps[:, 0:512],
                                  self.col("bv", n, 1), op0=OP.add)
                    t_ev[(wi, sg)] = b.op("dve", fn, inc="dve")
            t_qkv = self.cnt["dve"]
            b.wait("pe", "dve", t_qkv)
            t_vev = 0
            for rr in range(8):
                if t_vev:
                    b.wait("pe", "dve", t_vev)
                tok = None
                for j in range(4):
                    tau = rr * 4 + j
                    tok = b.op("pe", K.tp(self.PST[:, j * 128:(j + 1) * 128],
                                          vftmp[:, tau * 128:(tau + 1) * 128],
                                          self.ident32[:, :]),
                               inc="pe" if j == 3 else None)
                b.wait("dve", "pe", tok)
                t_vev = b.op("dve", K.cp(self.v_tm[:, rr * 512:(rr + 1) * 512],
                                         self.PST[:, 0:512]), inc="dve")
            b.wait("pe", "dve", t_vev)
            for tau in range(NT):
                b.op("pe", K.mm(psQK[:, 16 * tau:16 * tau + 8],
                                self.qc_fm[:, tau * 128:(tau + 1) * 128],
                                self.Wc2qc16[:, n * 8:(n + 1) * 8],
                                start=(tau == 0), stop=False))
                b.op("pe", K.mm(psQK[:, 16 * tau + 8:16 * tau + 16],
                                self.k_fm[:, tau * 128:(tau + 1) * 128],
                                self.Wc2q16[:, n * 8:(n + 1) * 8],
                                start=False, stop=False))
            b.wait("pe", "dve", t_pat)
            t_qk = b.op("pe", K.mm(psQK, self.ones16[0:1, 0:128], self.bcqpat,
                                   start=False, stop=True), inc="pe")
            b.wait("act", "pe", t_qk)
            ta = b.op("act", K.act(self.sqt[:, :], psQK, AF.Square), inc="act")
            b.wait("dve", "act", ta)
            td = b.op("dve", lambda e: e.reduce_sum(
                K.ap(self.small32, 64, [[192, 128], [2, 32], [1, 2]]),
                K.ap(self.sqt, 0, [[512, 128], [16, 32], [8, 2], [1, 8]]),
                axis=mybir.AxisListType.X), inc="dve")
            b.wait("act", "dve", td)
            b.op("act", K.act(self.nsq, self.nsq, AF.Ln, bias=self.cc[:, 1:2]))
            ta = b.op("act", K.act(self.nsq, self.nsq, AF.Exp, scale=-0.5),
                      inc="act")
            b.wait("dve", "act", ta)
            b.op("dve", K.ts(self.nsq, self.nsq, 1e4, op0=OP.min))
            b.op("dve", K.tt(self.rs,
                             self.nsq,
                             K.ap(self.cols32, self.CL["pat2"][0],
                                  [[160, 128], [0, 32], [1, 2]]),
                             OP.mult))
            tqk32 = b.op("dve", K.tt(
                self.qk32, psQK,
                K.ap(self.small32, 128, [[192, 128], [2, 32], [1, 2], [0, 8]]),
                OP.mult), inc="dve")
            b.wait("pe", "dve", tqk32)
            t_qev = 0
            for rr in range(16):
                if t_qev:
                    b.wait("pe", "act", t_qev)
                tok = None
                for j in range(4):
                    idx = rr * 4 + j
                    tau, side = idx // 2, idx % 2
                    tok = b.op("pe", K.tp(self.PST[0:8, j * 128:(j + 1) * 128],
                                          self.qk32[:, 16 * tau + 8 * side:
                                                    16 * tau + 8 * side + 8],
                                          self.ident32[:, :]),
                               inc="pe" if j == 3 else None)
                b.wait("act", "pe", tok)
                tau0 = rr * 2
                b.op("act", K.acp(self.qh[0:8, tau0 * 128:(tau0 + 2) * 128],
                                  K.ap(self.PST, 0, [[512, 8], [256, 2], [1, 128]])))
                t_qev = b.op("act", K.acp(
                    self.kl[0:8, tau0 * 128:(tau0 + 2) * 128],
                    K.ap(self.PST, 128, [[512, 8], [256, 2], [1, 128]])),
                    inc="act")
            t_prep_act = t_qev
            t_prep_pool = self.cnt["pool"]

            for i in range(9):
                has_F = i < 8
                has_B = i >= 1
                D0 = self.cnt["dve"]
                A0 = self.cnt["act"]
                has_norm = self.fpend is not None
                pv15_old = t_pvs[15]
                pvs_prev = list(t_pvs)
                tt_prev = list(t_tt)

                def pred_dquad(j):
                    return A0 + (8 if has_F else 0) + (j - 2) + 1

                def pred_stt(p):
                    return D0 + 2 * p + 1 + (2 if has_norm else 0)

                def pred_tt(p):
                    return pred_stt(p) + 1
                cp_tok = D0 + 1
                norm_tok = D0 + 2
                if has_norm:
                    b.wait("pe", "dve", t_recip)
                    b.wait("pe", "dve", t_tt[15])
                    t_bc = b.op("pe", K.mm(psA_[:, 0:512],
                                           self.ones32[0:1, 0:128],
                                           self.rb_s[0:1, 0:512]), inc="pe")
                for p in range(16):
                    if has_B:
                        b.wait("pe", "act",
                               t_exp[p] if p < 8 else pred_dquad(p // 4))
                        if p == 0:
                            b.wait("pe", "dve",
                                   norm_tok if has_norm else t_norm)
                            b.wait("pe", "dve", t_recip)
                        for half in range(2):
                            t_blk = 2 * p + half
                            b.op("pe", K.mm(
                                psPV,
                                self.v_tm[:, t_blk * 128:(t_blk + 1) * 128],
                                self.ein[:, p * 1024 + half * 512:
                                         p * 1024 + (half + 1) * 512],
                                start=(p == 0 and half == 0),
                                stop=(p == 15 and half == 1)))
                            t_pvs[p] = b.op("pe", K.mm(
                                psS, self.ones16[:, 0:1],
                                self.ein[:, p * 1024 + half * 512:
                                         p * 1024 + (half + 1) * 512],
                                start=(p == 0 and half == 0),
                                stop=(p == 15 and half == 1)),
                                inc="pe" if half == 1 else None)
                    if not has_F:
                        continue
                    pb = psB_[p % 2]
                    b.wait("pe", "act", t_prep_act)
                    b.wait("pe", "dve",
                           t_stt[(p - 2) % 16] if p < 2 else pred_stt(p - 2))
                    for half in range(2):
                        t_blk = 2 * p + half
                        t_mmB[p] = b.op("pe", K.mm(
                            pb[:, half * 512:(half + 1) * 512],
                            self.kl[0:8, t_blk * 128:(t_blk + 1) * 128],
                            self.qh[0:8, i * 512:(i + 1) * 512],
                            start=True, stop=True),
                            inc="pe" if half == 1 else None)
                    if p == 0:
                        b.wait("pe", "dve", t_tt[15])
                        b.wait("pe", "dve",
                               cp_tok if has_norm else t_norm)
                    else:
                        b.wait("pe", "dve", pred_tt(p - 1))
                    for half in range(2):
                        t_blk = 2 * p + half
                        t_mmA[p] = b.op("pe", K.mm(
                            psA_[:, half * 512:(half + 1) * 512],
                            self.k_fm[:, t_blk * 128:(t_blk + 1) * 128],
                            self.qc_fm[:, i * 512:(i + 1) * 512],
                            start=True, stop=True),
                            inc="pe" if half == 1 else None)
                def emit_sin(p):
                    b.wait("act", "pe", t_mmB[p])
                    b.wait("act", "dve",
                           t_stt[p + 10] if p < 6 else pred_stt(p - 6))
                    t_sin[p] = b.op("act", K.act(
                        self.s2bp[(p // 2) % 3][:, (p % 2) * 1024:
                                                (p % 2) * 1024 + 1024],
                        psB_[p % 2], AF.Sin, scale=SIN_SCALE), inc="act")

                def emit_equad(sl, j, gate_tt, gate_pe):
                    b.wait("act", "dve", gate_tt)
                    b.wait("act", "pe", gate_pe)
                    tj = b.op("act", K.act(
                        self.ein[:, j * 4096:(j + 1) * 4096],
                        self.ein[:, j * 4096:(j + 1) * 4096],
                        AF.Exp), inc="act")
                    for pp in range(4 * j, 4 * j + 4):
                        t_exp[pp] = tj
                for p in range(8 if has_F else 0):
                    emit_sin(p)
                if has_B:
                    for j in (2, 3):
                        emit_equad(i - 1, j, tt_prev[4 * j + 3],
                                   pvs_prev[4 * j + 3])
                for p in range(8, 16) if has_F else ():
                    emit_sin(p)
                if has_norm:
                    fn2, fss = self.fpend
                    b.wait("dve", "pe", t_bc)
                    t_cp = b.op("dve", K.cp(self.rb_s, psA_[:, 0:512]),
                                inc="dve")
                    assert t_cp == cp_tok
                    b.wait("dve", "pe", pv15_old)
                    b.wait("dve", "sa%d" % (fss % 2), t_attn[fss % 2])
                    t_norm = b.op("dve", K.tt(self.attn_t[fss % 2], psPV,
                                              self.rb_s, OP.mult), inc="dve")
                    assert t_norm == norm_tok
                    b.wait("pool", "dve", t_norm)
                    t_attn[fss % 2] = b.op("pool", K.dma(
                        d["attn_sc"][:, fn2 * 4096 + fss * 512:
                                     fn2 * 4096 + (fss + 1) * 512],
                        self.attn_t[fss % 2]), inc="sa%d" % (fss % 2), n=16)
                    self.fpend = None
                for p in range(16 if has_F else 0):
                    b.wait("dve", "act", t_sin[p])
                    t_stt[p] = b.op("dve", K.stt(
                        self.mq16[p % 3],
                        self.s2bp[(p // 2) % 3][:, (p % 2) * 1024:
                                                (p % 2) * 1024 + 1024],
                        10.0, psB_[p % 2], OP.add, OP.mult), inc="dve")
                    assert t_stt[p] == pred_stt(p)
                    b.wait("dve", "pe", t_mmA[p])
                    b.wait("dve", "pe", t_pvs[p])
                    b.wait("dve", "act", t_exp[p])
                    t_tt[p] = b.op("dve", K.tt(self.ein[:, p * 1024:(p + 1) * 1024],
                                               psA_, self.mq16[p % 3], OP.add),
                                   inc="dve")
                    assert t_tt[p] == pred_tt(p)
                for j in range(2) if has_F else ():
                    emit_equad(i, j, t_tt[4 * j + 3], t_pvs[4 * j + 3])
                if has_B:
                    b.wait("dve", "pe", t_pvs[15])
                    t_recip = b.op("dve",
                                   lambda e, o=self.rb_s[0:1, 0:512], i_=psS:
                                   e.reciprocal(o, i_), inc="dve")
                    self.fpend = (n, i - 1)
            fn2, fss = self.fpend
            b.wait("pe", "dve", t_recip)
            b.wait("pe", "dve", t_tt[15])
            t_bc = b.op("pe", K.mm(psA_[:, 0:512], self.ones32[0:1, 0:128],
                                   self.rb_s[0:1, 0:512]), inc="pe")
            b.wait("dve", "pe", t_bc)
            t_cp = b.op("dve", K.cp(self.rb_s, psA_[:, 0:512]), inc="dve")
            b.wait("dve", "pe", t_pvs[15])
            b.wait("dve", "sa%d" % (fss % 2), t_attn[fss % 2])
            t_norm = b.op("dve", K.tt(self.attn_t[fss % 2], psPV, self.rb_s,
                                      OP.mult), inc="dve")
            b.wait("pool", "dve", t_norm)
            t_attn[fss % 2] = b.op("pool", K.dma(
                d["attn_sc"][:, fn2 * 4096 + fss * 512:
                             fn2 * 4096 + (fss + 1) * 512],
                self.attn_t[fss % 2]), inc="sa%d" % (fss % 2), n=16)
            self.fpend = None
            t_pe_flash = self.cnt["pe"]

    def _cprep(self):
        b, d = self, self.d
        b.wait("syncq", "sa0", self.cnt["sa0"])
        b.wait("syncq", "sa1", self.cnt["sa1"])
        b.wait("syncq", "sg", self.cnt["sg"])
        b.wait("syncq", "sx0", self.cnt["sx0"])
        b.wait("syncq", "sx1", self.cnt["sx1"])
        t_wo = 0
        for c in range(8):
            tok = b.dld("dw0", ("pool", t_wo),
                        K.dma(self.Dv(0, 1024), d["Wo"][c * 128:(c + 1) * 128, :]))
            b.wait("pool", "dw0", tok)
            t_wo = b.op("pool", K.cp(self.arW[:, c * 1024:(c + 1) * 1024],
                                     self.Dv(0, 1024)), inc="pool")
        self.t_wo16 = self.cnt["pool"]
        rb = self.arD[0:1, 2048:3072]
        for nm, dst, half in [("g1", self.g1bc, False), ("g2", self.g2bc, True),
                              ("beta2", self.b2bc, True)]:
            tok = b.dil(K.dma(rb, d[nm][None, :]))
            b.wait("dve", "di", tok)
            if half:
                b.op("dve", K.ts(rb, rb, 0.5))
            trb = b.op("dve", lambda e: e.nop(), inc="dve")
            b.wait("pe", "dve", trb)
            tok2 = None
            for hh in range(2):
                tok2 = b.op("pe", K.mm(self.PSF[:, hh * 512:(hh + 1) * 512],
                                       self.ones32[0:1, 0:128],
                                       rb[0:1, hh * 512:(hh + 1) * 512]),
                            inc="pe" if hh == 1 else None)
            b.di_done("pe", tok2)
            b.wait("dve", "pe", tok2)
            b.op("dve", K.cp(dst, self.PSF[:, 0:1024]), inc="dve")
        for nm, dst in [("bo", self.boC), ("beta1", self.be1C)]:
            tok = b.dil(K.dma(rb, d[nm][None, :]))
            b.wait("dve", "di", tok)
            td = b.op("dve", K.cp(dst, rb), inc="dve")
            b.di_done("dve", td)
        for r in range(2):
            b.wait("pe", "dve", self.cnt["dve"])
            tok = None
            for j in range(4):
                cidx = r * 4 + j
                tok = b.op("pe", K.tp(self.PST[0:1, j * 128:(j + 1) * 128],
                                      self.col("bf2c", cidx, 1),
                                      self.ident32[:, :]),
                           inc="pe" if j == 3 else None)
            b.wait("dve", "pe", tok)
            b.op("dve", K.cp(self.bf2C[0:1, r * 512:(r + 1) * 512],
                             K.ap(self.PST, 0, [[512, 1], [128, 4], [1, 128]])),
                 inc="dve")
        for r in range(2):
            b.wait("pe", "dve", self.cnt["dve"])
            tok = None
            for j in range(4):
                fc = r * 4 + j
                tok = b.op("pe", K.tp(self.PST[0:8, j * 128:(j + 1) * 128],
                                      self.w2ftc[:, fc * 8:(fc + 1) * 8],
                                      self.ident32[:, :]),
                           inc="pe" if j == 3 else None)
            b.wait("dve", "pe", tok)
            b.op("dve", K.cp(self.W2f16[0:8, r * 512:(r + 1) * 512],
                             K.ap(self.PST, 0, [[512, 8], [128, 4], [1, 128]])),
                 inc="dve")
        self.tCP = b.op("dve", lambda e: e.nop(), inc="dve")

    def _phase_c(self):
        b, d = self, self.d
        psWO = [self.PSF[:, 0:1024], self.PSF[:, 1024:2048]]
        psQM = self.PSF[:, 2048:2056]
        psFF = self.PSF[:, 2560:3584]
        xnld = [self.Dv(3072, 1024), self.Dv(4096, 1024)]
        xld = [self.Dv(0, 1024), self.Dv(1024, 1024)]
        gld = [self.Dv(2048, 512), self.Dv(5120, 1024)]
        gld = [self.arD[:, 2048:3072], self.arD[:, 5120:6144]]
        sm = self.small32
        t_pre = [0, 0]
        t_sh = [0, 0]
        t_out = [0, 0]
        t_ld = [0, 0]
        t_wo = [0, 0]
        t_xn = [0, 0]
        t_x1 = [0] * 8
        t_x1T = [0] * 8
        t_qeT = [0] * 8
        self.t_ff_last = 0
        for gi in range(4):
            b.wait("dve", "pool", self.cnt["pool"])
            for t in range(8):
                T = gi * 8 + t
                xb = t % 2
                tok_a = b.dld("da%d" % xb, ("pe", t_wo[xb]),
                              K.dma(self.astg[xb],
                                    K.ap(d["attn_sc"].tensor, T * 128,
                                         [[32768, 128], [4096, 8], [1, 128]])))
                tok_x = b.dld("dn%d" % xb, ("dve", t_xn[xb]),
                              K.dma(xnld[xb],
                                    d["xn_sc"][T * 128:(T + 1) * 128, :]))
                b.wait("pe", "da%d" % xb, tok_a)
                if t_pre[xb]:
                    b.wait("pe", "dve", t_pre[xb])
                tok = None
                for c in range(8):
                    for hh in range(2):
                        b.op("pe", K.mm(
                            psWO[xb][:, hh * 512:(hh + 1) * 512],
                            self.astg[xb][:, c * 128:(c + 1) * 128],
                            self.arW[:, c * 1024 + hh * 512:
                                     c * 1024 + (hh + 1) * 512],
                            start=(c == 0), stop=False))
                for hh in range(2):
                    tok = b.op("pe", K.mm(psWO[xb][:, hh * 512:(hh + 1) * 512],
                                          self.ones16[0:1, 0:128],
                                          self.boC[0:1, hh * 512:(hh + 1) * 512],
                                          start=False, stop=True),
                               inc="pe" if hh == 1 else None)
                t_wo[xb] = tok
                b.wait("dve", "pe", tok)
                b.wait("dve", "dn%d" % xb, tok_x)
                t_xn[xb] = b.op("dve", K.tt(self.preb[xb], psWO[xb], xnld[xb],
                                            OP.add), inc="dve")
                b.wait("act", "dve", t_xn[xb])
                b.op("act", K.act(self.trashC, self.preb[xb], AF.Square,
                                  accum=sm[:, 24 + xb:25 + xb]))
                ta = b.op("act", K.act(self.trashC, self.preb[xb], AF.Copy,
                                       accum=sm[:, 26 + xb:27 + xb]), inc="act")
                b.wait("dve", "act", ta)
                b.op("dve", K.ts(sm[:, 28:29], sm[:, 26 + xb:27 + xb], 1.0 / 1024.0))
                b.op("dve", K.ts(sm[:, 29:30], sm[:, 24 + xb:25 + xb], 1.0 / 1024.0))
                b.op("dve", K.tt(sm[:, 30:31], sm[:, 28:29], sm[:, 28:29], OP.mult))
                td = b.op("dve", K.tt(sm[:, 31:32], sm[:, 29:30], sm[:, 30:31],
                                      OP.subtract), inc="dve")
                b.wait("act", "dve", td)
                b.op("act", K.act(sm[:, 31:32], sm[:, 31:32], AF.Ln, bias=self.cc[:, 0:1]))
                ta2 = b.op("act", K.act(sm[:, 32:33], sm[:, 31:32], AF.Exp,
                                        scale=-0.5), inc="act")
                b.wait("dve", "act", ta2)
                t_x1[t] = b.op("dve", K.ts(self.x1a(t), self.preb[xb], sm[:, 28:29],
                                           sm[:, 32:33], op0=OP.subtract,
                                           op1=OP.mult), inc="dve")
                t_pre[xb] = t_x1[t]
                b.wait("pe", "dve", t_x1[t])
                for r in range(2):
                    tok = None
                    for c4 in range(4):
                        c = 4 * r + c4
                        tok = b.op("pe", K.tp(
                            self.PST[:, c4 * 128:(c4 + 1) * 128],
                            self.f32ap(self.arA, t * 1024 + c * 128,
                                       [[16384, 128], [1, 128]]),
                            self.ident32[:, :]), inc="pe" if c4 == 3 else None)
                    b.wait("dve", "pe", tok)
                    t_x1T[t] = b.op("dve", K.cp(
                        self.x1aT(t)[:, r * 512:(r + 1) * 512],
                        self.PST[:, 0:512]), inc="dve")
                    if r == 0:
                        b.wait("pe", "dve", t_x1T[t])
            b.wait("act", "pe", self.cnt["pe"])
            for t in range(8):
                b.wait("pe", "dve", t_x1T[t])
                for c in range(8):
                    b.op("pe", K.mm(psQM, self.x1aT(t)[:, c * 128:(c + 1) * 128],
                                    self.Wf16[:, c * 8:(c + 1) * 8],
                                    start=(c == 0), stop=False))
                tok = b.op("pe", K.mm(psQM, self.ones16[0:1, 0:128],
                                      self.rows16[0:1, 0:8],
                                      start=False, stop=True), inc="pe")
                b.wait("act", "pe", tok)
                b.op("act", K.act(sm[:, 33:41], psQM, AF.Tanh))
                ta = b.op("act", K.act(sm[:, 41:49], psQM, AF.Abs,
                                       scale=PI / 4.0), inc="act")
                b.wait("dve", "act", ta)
                td = b.op("dve", K.ts(sm[:, 41:49], sm[:, 41:49], 1.5 * PI,
                                      op0=OP.min), inc="dve")
                b.wait("act", "dve", td)
                ta = b.op("act", K.act(sm[:, 41:49], sm[:, 41:49], AF.Sin,
                                       bias=self.cc[:, 2:3], scale=-1.0), inc="act")
                b.wait("dve", "act", ta)
                tq = b.op("dve", K.tt(sm[:, 41:49], sm[:, 33:41], sm[:, 41:49],
                                      OP.mult), inc="dve")
                b.wait("pe", "dve", tq)
                if t > 0:
                    b.wait("pe", "act", t_qeT[t - 1])
                tok = b.op("pe", K.tp(self.PST[0:8, 0:128], sm[:, 41:49],
                                      self.ident32[:, :]), inc="pe")
                b.wait("act", "pe", tok)
                t_qeT[t] = b.op("act", K.acp(self.qeT[0:8, t * 128:(t + 1) * 128],
                                             self.PST[0:8, 0:128]), inc="act")
            for t in range(8):
                T = gi * 8 + t
                xb = t % 2
                b.wait("pe", "act", t_qeT[t])
                if self.t_ff_last:
                    b.wait("pe", "dve", self.t_ff_last)
                tok = None
                for hh in range(2):
                    b.op("pe", K.mm(psFF[:, hh * 512:(hh + 1) * 512],
                                    self.qeT[0:8, t * 128:(t + 1) * 128],
                                    self.W2f16[0:8, hh * 512:(hh + 1) * 512],
                                    start=True, stop=False))
                    b.op("pe", K.mm(psFF[:, hh * 512:(hh + 1) * 512],
                                    self.ones16[0:1, 0:128],
                                    self.be1C[0:1, hh * 512:(hh + 1) * 512],
                                    start=False, stop=False))
                    tok = b.op("pe", K.mm(psFF[:, hh * 512:(hh + 1) * 512],
                                          self.ones16[0:1, 0:128],
                                          self.bf2C[0:1, hh * 512:(hh + 1) * 512],
                                          start=False, stop=True),
                               inc="pe" if hh == 1 else None)
                b.op("dve", K.tt(self.x1a(t), self.x1a(t), self.g1bc, OP.mult))
                b.wait("dve", "pe", tok)
                t_pre[xb] = b.op("dve", K.tt(self.preb[xb], psFF, self.x1a(t),
                                             OP.add), inc="dve")
                self.t_ff_last = t_pre[xb]
                b.wait("act", "dve", t_pre[xb])
                b.op("act", K.act(self.trashC, self.preb[xb], AF.Square,
                                  accum=sm[:, 24 + xb:25 + xb]))
                ta = b.op("act", K.act(self.trashC, self.preb[xb], AF.Copy,
                                       accum=sm[:, 26 + xb:27 + xb]), inc="act")
                b.wait("dve", "act", ta)
                b.op("dve", K.ts(sm[:, 28:29], sm[:, 26 + xb:27 + xb], 1.0 / 1024.0))
                b.op("dve", K.ts(sm[:, 29:30], sm[:, 24 + xb:25 + xb], 1.0 / 1024.0))
                b.op("dve", K.tt(sm[:, 30:31], sm[:, 28:29], sm[:, 28:29], OP.mult))
                td = b.op("dve", K.tt(sm[:, 31:32], sm[:, 29:30], sm[:, 30:31],
                                      OP.subtract), inc="dve")
                b.wait("act", "dve", td)
                b.op("act", K.act(sm[:, 31:32], sm[:, 31:32], AF.Ln, bias=self.cc[:, 0:1]))
                ta2 = b.op("act", K.act(sm[:, 32:33], sm[:, 31:32], AF.Exp,
                                        scale=-0.5), inc="act")
                b.wait("dve", "act", ta2)
                ts2 = b.op("dve", K.ts(self.x1a(t), self.preb[xb], sm[:, 28:29],
                                       sm[:, 32:33], op0=OP.subtract, op1=OP.mult),
                           inc="dve")
                b.wait("pool", "dve", ts2)
                if t_sh[xb]:
                    b.wait("pool", "dve", t_ld[xb])
                b.op("pool", K.tt(self.shb[xb], self.x1a(t), self.g2bc, OP.mult))
                t_sh[xb] = b.op("pool", K.tt(self.shb[xb], self.shb[xb], self.b2bc,
                                             OP.add), inc="pool")
                tok_x = b.dld("dxx%d" % xb, ("dve", t_ld[xb]),
                              K.dma(xld[xb], d["x"][T * 128:(T + 1) * 128, :]))
                tok_g = b.dld("dgl%d" % xb, ("dve", t_ld[xb]),
                              K.dma(gld[xb], d["g_sc"][T * 128:(T + 1) * 128, :]))
                b.wait("dve", "pool", t_sh[xb])
                b.wait("dve", "dgl%d" % xb, tok_g)
                if t_out[xb]:
                    b.wait("dve", "so%d" % xb, t_out[xb])
                b.op("dve", K.tt(self.outb[xb], self.shb[xb], gld[xb], OP.mult))
                b.wait("dve", "dxx%d" % xb, tok_x)
                t_ld[xb] = b.op("dve", K.tt(self.outb[xb], self.outb[xb], xld[xb],
                                            OP.add), inc="dve")
                b.wait("pool", "dve", t_ld[xb])
                t_out[xb] = b.op("pool", K.dma(d["out"][T * 128:(T + 1) * 128, :],
                                               self.outb[xb]),
                                 inc="so%d" % xb, n=16)
        b.wait("pool", "so0", self.cnt["so0"])
        b.wait("pool", "so1", self.cnt["so1"])


def build_kernel(dbg=()):
    k = K(dbg)
    nc = k.build()
    n = sum(len(k.q[e]) for e in ENGS)
    return nc, n


def kernel(**kw):
    x = np.ascontiguousarray(kw["L0_attn_x"], dtype=np.float32)
    nc, _ = build_kernel()
    const = {"c_ident": np.eye(128, dtype=np.float32)}
    in_maps = []
    for core in range(8):
        m = {k2: np.ascontiguousarray(v, dtype=np.float32)
             for k2, v in kw.items() if k2 != "L0_attn_x"}
        m["L0_attn_x"] = x[core]
        m.update(const)
        in_maps.append(m)
    res = run_bass_kernel_spmd(nc, in_maps, core_ids=list(range(8)))
    return np.stack([res.results[c]["out"] for c in range(8)], axis=0)


# version 09 (final)
# speedup vs baseline: 1.0402x; 1.0024x over previous
import math
import numpy as np
import concourse.bass as bass
import concourse.mybir as mybir
from concourse.bass_utils import run_bass_kernel_spmd
from concourse.alu_op_type import AluOpType as OP
from contextlib import ExitStack

F32 = mybir.dt.float32
F16 = mybir.dt.float16
AF = mybir.ActivationFunctionType

NT = 32
LN_EPS = 1e-5
C1 = 0.5 / math.sqrt(128.0)
WC2Q_SCALE = 1.0 / C1
LAM = 0.05
SIN_SCALE = math.pi / LAM
PI = math.pi
ENGS = ("pe", "act", "dve", "pool", "syncq")
CSEMS = ("pe", "act", "dve", "pool")


class K:
    def __init__(self, dbg=()):
        self.nc = bass.Bass("TRN2", target_bir_lowering=False)
        self.dbg = dbg
        self.q = {e: [] for e in ENGS}
        self.pe_labels = {}
        self.cnt = {}
        self.sem = {}
        self.lastw = {}

    def op(self, eng, fn, inc=None, n=1):
        if eng in ("act", "dve", "pool"):
            if self.cnt[eng] > 0:
                self.wait(eng, eng, self.cnt[eng])
            if inc is None:
                inc = eng
        elif eng == "pe" and inc is None:
            inc = "pe"
        if inc is None:
            self.q[eng].append(fn)
            return None
        tok = self.cnt[inc] + n
        self.cnt[inc] = tok
        if inc == "pe":
            import traceback as _tb
            for fr in reversed(_tb.extract_stack(limit=8)):
                if fr.name.startswith(("_phase", "_wfuse", "_gf", "_cprep",
                                       "_init", "_load")):
                    self.pe_labels[tok] = (fr.name, fr.lineno)
                    break
        sem = self.sem[inc]
        self.q[eng].append(lambda e, fn=fn, sem=sem, n=n: fn(e).then_inc(sem, n))
        return tok

    def wait(self, eng, semname, tok):
        if not tok or tok <= 0:
            return
        key = (eng, semname)
        if self.lastw.get(key, 0) >= tok:
            return
        self.lastw[key] = tok
        sem = self.sem[semname]
        self.q[eng].append(lambda e, sem=sem, tok=tok: e.wait_ge(sem, tok))

    STSEMS = ("sx0", "sx1", "sg", "sa0", "sa1", "so0", "so1")

    def fence(self):
        snap = {s: self.cnt[s] for s in CSEMS}
        stsnap = {s: self.cnt[s] for s in K.STSEMS}
        for eng in ("pe", "act", "dve", "pool", "syncq"):
            for s in CSEMS:
                if eng != s:
                    self.wait(eng, s, snap[s])
            for s in K.STSEMS:
                self.wait(eng, s, stsnap[s])

    def dld(self, semname, gate, fn, queue="syncq"):
        if gate is not None and gate[1]:
            self.wait(queue, gate[0], gate[1])
        return self.op(queue, fn, inc=semname, n=16)

    def dil(self, fn):
        tok = self.dld("di", self.di_gate, fn)
        return tok

    def di_done(self, eng_sem, tok):
        self.di_gate = (eng_sem, tok)

    @staticmethod
    def mm(psl, lhsT, rhs, start=True, stop=True):
        return lambda e: e.matmul(psl, lhsT, rhs, start=start, stop=stop)

    @staticmethod
    def tp(psl, in_, idn):
        return lambda e: e.transpose(psl, in_, idn)

    @staticmethod
    def act(out, in_, f, bias=0.0, scale=1.0, accum=None):
        return lambda e: e.activation(out, in_, f, bias=bias, scale=scale,
                                      accum_out=accum)

    @staticmethod
    def tt(out, a, c, op):
        return lambda e: e.tensor_tensor(out, a, c, op=op)

    @staticmethod
    def ts(out, a, s1, s2=None, op0=OP.mult, op1=None):
        if s2 is None:
            return lambda e: e.tensor_scalar(out, a, s1, None, op0=op0)
        return lambda e: e.tensor_scalar(out, a, s1, s2, op0=op0, op1=op1)

    @staticmethod
    def stt(out, in0, sc, in1, op0, op1):
        return lambda e: e.scalar_tensor_tensor(out, in0, sc, in1, op0=op0, op1=op1)

    @staticmethod
    def cp(out, in_):
        return lambda e: e.tensor_copy(out, in_)

    @staticmethod
    def acp(out, in_):
        return lambda e: e.copy(out, in_)

    @staticmethod
    def dma(out, in_):
        return lambda e: e.dma_start(out, in_)

    @staticmethod
    def dmas(out, in_):
        return lambda e: e.dma_start(out, in_, allow_slow_non_contiguous=True)

    @staticmethod
    def ap(t, off, dims):
        return bass.AP(t, off, dims)

    @staticmethod
    def f32ap(t16, off32, dims32):
        d16 = [[s * 2, c] for s, c in dims32[:-1]] + [[1, dims32[-1][1] * 2]]
        return bass.AP(t16, off32 * 2, d16).bitcast(F32)

    def build(self):
        nc = self.nc

        def din(name, shape):
            return nc.dram_tensor(name, shape, F32, kind="ExternalInput").ap()

        d = self.d = {}
        d["x"] = din("L0_attn_x", [4096, 1024])
        for nm, sh in [("Wq", [8, 1024, 128]), ("Wk", [8, 1024, 128]),
                       ("Wv", [8, 1024, 128]), ("Wc2q", [8, 128, 8]),
                       ("Wo", [1024, 1024]), ("W1", [1024, 4096]),
                       ("Wm", [4096, 8]), ("Wqo", [8, 4096]), ("W2", [4096, 1024]),
                       ("bq", [8, 128]), ("bk", [8, 128]), ("bv", [8, 128]),
                       ("bc2q", [8, 8]), ("bo", [1024]), ("g1", [1024]),
                       ("beta1", [1024]), ("g2", [1024]), ("beta2", [1024]),
                       ("b1", [4096]), ("bm", [8]), ("bqo", [4096]), ("b2", [1024])]:
            d[nm] = din("L0_attn_" + nm, sh)
        for i, sh in enumerate([[1024, 1024], [1024, 512], [512, 256], [256, 1]]):
            d[f"cnw{i+1}"] = din(f"L0_ffn_cn_w{i+1}", sh)
        for i, sh in enumerate([1024, 512, 256, 1]):
            d[f"cnb{i+1}"] = din(f"L0_ffn_cn_b{i+1}", [sh])
        for i, sh in enumerate([[1024, 512], [512, 256], [256, 1]]):
            d[f"gcw{i+1}"] = din(f"L0_ffn_gc_w{i+1}", sh)
        for i, sh in enumerate([512, 256, 1]):
            d[f"gcb{i+1}"] = din(f"L0_ffn_gc_b{i+1}", [sh])
        d["ar"] = din("L0_ffn_adaptive_range", [1])
        d["ident"] = din("c_ident", [128, 128])
        d["out"] = nc.dram_tensor("out", [4096, 1024], F32, kind="ExternalOutput").ap()
        ks = lambda nm: "ExternalOutput" if nm in self.dbg else "Internal"
        d["xn_sc"] = nc.dram_tensor("xn_sc", [4096, 1024], F32, kind=ks("xn_sc")).ap()
        d["g_sc"] = nc.dram_tensor("g_sc", [4096, 1024], F32, kind=ks("g_sc")).ap()
        d["attn_sc"] = nc.dram_tensor("attn_sc", [128, 8 * 4096], F16,
                                      kind=ks("attn_sc")).ap()
        d["rb_sc"] = nc.dram_tensor("rb_sc", [1, 512], F32, kind="Internal").ap()

        with ExitStack() as st:
            self._alloc(st)
            self._emit()
            with nc.Block() as block:
                @block.tensor
                def _(e):
                    for fn in self.q["pe"]:
                        fn(e)

                @block.scalar
                def _(e):
                    for fn in self.q["act"]:
                        fn(e)

                @block.vector
                def _(e):
                    for fn in self.q["dve"]:
                        fn(e)

                @block.gpsimd
                def _(e):
                    for fn in self.q["pool"]:
                        fn(e)

                @block.sync
                def _(e):
                    for fn in self.q["syncq"]:
                        fn(e)
        return nc

    def _alloc(self, st):
        nc = self.nc

        def sb(name, shape, dt):
            t = st.enter_context(nc.sbuf_tensor(name, shape, dt))
            setattr(self, name, t)
            return t

        sb("arA", [128, 32768], F16)
        sb("arB", [128, 16384], F16)
        sb("arC", [128, 16384], F16)
        sb("arD", [128, 6144], F32)
        sb("arE", [128, 13312], F16)
        sb("arW", [128, 8192], F16)
        sb("rows16", [1, 64], F16)
        sb("cols32", [128, 160], F32)
        sb("cols16", [128, 80], F16)
        sb("small32", [128, 192], F32)
        sb("sqt", [128, 512], F32)
        sb("sc32", [1, 16], F32)
        sb("Wm16", [128, 256], F16)
        sb("Wc2qc16", [128, 64], F16)
        sb("Wc2q16", [128, 64], F16)
        sb("Wf16", [128, 64], F16)
        sb("W2f16", [8, 1024], F16)
        sb("WqoT16", [128, 256], F16)
        sb("ident32", [128, 128], F32)
        sb("ones32", [128, 128], F32)
        sb("ones16", [128, 128], F16)
        sb("ldst", [32, 128], F32)
        sb("w2ftc", [128, 64], F32)
        sb("cc", [128, 4], F32)
        self.PSF = st.enter_context(nc.psum_tensor("psf", [128, 3584], F32))
        self.PST = st.enter_context(nc.psum_tensor("pst", [128, 512], F32))
        allsems = ["pe", "act", "dve", "pool", "di",
                   "dx0", "dx1", "dx2", "dw0", "dw1", "dw2",
                   "dg0", "dg1", "dg2", "da0", "da1", "dn0", "dn1",
                   "dxx0", "dxx1", "dgl0", "dgl1",
                   "sx0", "sx1", "sg", "sa0", "sa1", "so0", "so1",
                   "sr0", "sr1"]
        for s in allsems:
            self.sem[s] = st.enter_context(nc.semaphore("s_" + s))
            self.cnt[s] = 0
        self.di_gate = None

        A, B, C, D, E = self.arA, self.arB, self.arC, self.arD, self.arE
        self.xn_fm = A
        self.ein = B
        self.qc_fm = C[:, 0:4096]
        self.k_fm = C[:, 4096:8192]
        self.v_tm = C[:, 8192:12288]
        self.qh = C[0:8, 12288:16384]
        self.kl = D[0:8, 0:2048].bitcast(F16)
        self.s2bp = [E[:, 0:2048], E[:, 2048:4096], E[:, 4096:6144]]
        self.mq16 = [E[:, 6144:7168], E[:, 9216:10240], E[:, 10496:11520]]
        self.e16 = [E[:, 4096:5120], E[:, 5120:6144], E[:, 6144:7168],
                    E[:, 9216:10240]]
        self.attn_t = [E[:, 7168:7680], E[:, 7680:8192]]
        self.rb_s = E[:, 8192:9216].bitcast(F32)
        self.xn32 = [E[:, 9216:11264].bitcast(F32), E[:, 11264:13312].bitcast(F32)]
        self.qk32 = E[:, 10496:11520].bitcast(F32)
        self.bcqpat = E[0:1, 11520:12032]
        self.gcb1r = B[0:1, 2048:2560]
        self.gcb2r = B[0:1, 2560:2816]
        self.mean16 = B[:, 2816:2824]
        self.ucol16 = B[:, 4608:4640]
        self.nsq = self.small32[:, 64:128]
        self.rs = self.small32[:, 128:192]
        self.Dv = lambda o, n: D[:, o:o + n]
        self.rbuf = D[0:1, 320:1344]
        self.cnw2 = E[:, 9216:13312]
        self.cnw3 = D[:, 5120:5632].bitcast(F16)
        self.cnw4 = self.sqt[:, 0:1].bitcast(F16)
        self.gcw2 = D[:, 5632:6144].bitcast(F16)
        self.gcw3 = self.sqt[:, 1:2].bitcast(F16)
        self.og = B[:, 0:8192].bitcast(F32)
        self.h1g = B[:, 8192:12288]
        self.h2g = B[:, 12288:14336]
        self.h3g = B[:, 14336:15360]
        self.out16 = E[:, 0:4096]
        self.twt = [E[:, 4096:6144].bitcast(F32), E[:, 6144:8192].bitcast(F32)]
        self.th16 = E[:, 4096:8192]
        self.h1row = B[0:1, 0:1024].bitcast(F32)
        self.h2row = B[0:1, 1024:1536].bitcast(F32)
        self.h1T = B[:, 1536:1540]
        self.h2T = B[:, 1540:1542]
        self.g1bc = C[:, 0:2048].bitcast(F32)
        self.g2bc = C[:, 2048:4096].bitcast(F32)
        self.b2bc = C[:, 4096:6144].bitcast(F32)
        self.boC = C[0:1, 6144:7168]
        self.be1C = C[0:1, 7168:8192]
        self.bf2C = C[0:1, 8192:9216]
        self.astg = [C[:, 9216:10240], C[:, 10240:11264]]
        self.qeT = C[0:8, 11264:12288]
        self.x1a = lambda t: self.f32ap(A, t * 1024, [[16384, 128], [1, 1024]])
        self.x1aT = lambda t: A[:, 16384 + t * 1024:16384 + (t + 1) * 1024]
        self.preb = [B[:, 0:2048].bitcast(F32), B[:, 2048:4096].bitcast(F32)]
        self.shb = [B[:, 4096:6144].bitcast(F32), B[:, 6144:8192].bitcast(F32)]
        self.outb = [B[:, 8192:10240].bitcast(F32), B[:, 10240:12288].bitcast(F32)]
        self.trashC = E[:, 0:2048].bitcast(F32)

    def _emit(self):
        self._init_consts()
        self._phase_a()
        self._gf_net()
        self.fence()
        self._wfuse1()
        self.fence()
        self._wfuse2()
        self.fence()
        self._load_cn()
        self._phase_g()
        self.fence()
        self._phase_b()
        self.fence()
        self._cprep()
        self.fence()
        self._phase_c()

    def col(self, nm, j0=0, nj=None):
        o, n = self.CL[nm]
        if nj is None:
            nj = n
        return self.cols32[:, o + j0:o + j0 + nj]

    def _init_consts(self):
        b, d = self, self.d
        self.CL = {}
        offc = 0
        for nm, n in [("bq", 8), ("bk", 8), ("bv", 8), ("g1c", 8), ("be1c", 8),
                      ("b1c", 32), ("bqoc", 32), ("cb1", 8), ("cb2", 4), ("cb3", 2),
                      ("b2c", 8), ("bf2c", 8), ("pat2", 2)]:
            self.CL[nm] = (offc, n)
            offc += n
        assert offc <= 160

        tok = b.dil(K.dma(self.ident32[:, :], d["ident"][:, :]))
        b.wait("dve", "di", tok)
        td = b.op("dve", lambda e: e.memset(self.ones32[:, :], 1.0), inc="dve")
        b.di_done("dve", td)
        self.t_ident = tok
        b.op("dve", lambda e: e.memset(self.ones16[:, :], 1.0))
        b.op("dve", lambda e: e.memset(self.col("pat2", 0, 1), 1.0))
        b.op("dve", lambda e: e.memset(self.col("pat2", 1, 1), LAM))
        b.op("dve", lambda e: e.memset(self.cc[:, 0:1], LN_EPS))
        b.op("dve", lambda e: e.memset(self.cc[:, 1:2], 1e-12))
        b.op("dve", lambda e: e.memset(self.cc[:, 2:3], PI / 2.0), inc="dve")

        for srcn, cell in [("ar", 0), ("gcb3", 1), ("cnb4", 2)]:
            tok = b.dil(K.dma(self.sc32[0:1, cell:cell + 1], d[srcn][None, :]))
            b.wait("dve", "di", tok)
            td = b.op("dve", lambda e: e.nop(), inc="dve")
            b.di_done("dve", td)
        b.op("dve", K.ts(self.sc32[0:1, 4:5], self.sc32[0:1, 0:1], 2.0))
        b.op("dve", K.ts(self.sc32[0:1, 7:8], self.sc32[0:1, 0:1], -1.0, 0.1,
                         op0=OP.mult, op1=OP.add))
        b.op("dve", K.ts(self.sc32[0:1, 5:6], self.sc32[0:1, 2:3], 0.5))
        b.op("dve", K.ts(self.sc32[0:1, 6:7], self.sc32[0:1, 1:2], 0.5), inc="dve")

        for src, n, dst in [(d["bm"], 8, self.rows16[0:1, 8:16]),
                            (d["gcb1"], 512, self.gcb1r),
                            (d["gcb2"], 256, self.gcb2r)]:
            tok = b.dil(K.dma(self.rbuf[0:1, 0:n], src[None, :]))
            b.wait("dve", "di", tok)
            td = b.op("dve", K.cp(dst, self.rbuf[0:1, 0:n]), inc="dve")
            b.di_done("dve", td)
        tok = b.dil(K.dma(self.rbuf[0:1, 0:64],
                          K.ap(d["bc2q"].tensor, 0, [[64, 1], [1, 64]])))
        b.wait("dve", "di", tok)
        td = b.op("dve", K.cp(self.cols16[0:1, 0:64], self.rbuf[0:1, 0:64]),
                  inc="dve")
        b.di_done("dve", td)
        self.bc2q16 = self.cols16[0:1, 0:64]
        self.bqo16 = self.cols16[:, 32:64]

        def load_cols(nm, src_vec, L, two_d=False):
            nchunk = L // 128
            if two_d:
                ap_in = K.ap(src_vec.tensor, 0, [[128, nchunk], [1, 128]])
            else:
                ap_in = K.ap(src_vec.tensor, src_vec.offset,
                             [[128, nchunk], [1, 128]])
            tok = b.dil(K.dma(self.ldst[0:nchunk, :], ap_in))
            b.wait("pe", "di", tok)
            b.wait("pe", "di", self.t_ident)
            b.wait("pe", "dve", self.cnt["dve"])
            tpe = b.op("pe", K.tp(self.PST[:, 0:nchunk], self.ldst[0:nchunk, :],
                                  self.ident32[0:nchunk, 0:nchunk]), inc="pe")
            b.di_done("pe", tpe)
            b.wait("dve", "pe", tpe)
            return b.op("dve", K.cp(self.col(nm), self.PST[:, 0:nchunk]), inc="dve")

        for nm, src, L in [("g1c", d["g1"], 1024), ("be1c", d["beta1"], 1024),
                           ("b1c", d["b1"], 4096), ("bqoc", d["bqo"], 4096),
                           ("cb1", d["cnb1"], 1024), ("cb2", d["cnb2"], 512),
                           ("cb3", d["cnb3"], 256), ("b2c", d["b2"], 1024)]:
            load_cols(nm, src, L)
        for nm, src in [("bq", d["bq"]), ("bk", d["bk"]), ("bv", d["bv"])]:
            load_cols(nm, src, 1024, two_d=True)
        tok = self.cnt["dve"]
        b.wait("pool", "dve", tok)
        b.op("pool", K.cp(self.bqo16, self.col("bqoc")), inc="pool")

        tok = b.dil(K.dma(self.Dv(2048, 256),
                          K.ap(d["Wm"].tensor, 0, [[8, 128], [1024, 32], [1, 8]])))
        b.wait("pool", "di", tok)
        tp_ = b.op("pool", K.cp(self.Wm16[:, :], self.Dv(2048, 256)), inc="pool")
        b.di_done("pool", tp_)
        tok = b.dil(K.dma(self.Dv(2304, 64),
                          K.ap(d["Wc2q"].tensor, 0, [[8, 128], [1024, 8], [1, 8]])))
        b.wait("dve", "di", tok)
        b.op("dve", K.ts(self.Wc2qc16[:, :], self.Dv(2304, 64), WC2Q_SCALE),
             inc="dve")
        b.wait("pool", "di", tok)
        tp_ = b.op("pool", K.cp(self.Wc2q16[:, :], self.Dv(2304, 64)), inc="pool")
        b.di_done("pool", tp_)

        self.gcw1 = self.arW[:, 0:4096]
        b.wait("syncq", "dve", self.cnt["dve"])
        tok = b.dil(K.dma(self.arD[:, 0:4096],
                          K.ap(d["gcw1"].tensor, 0,
                               [[512, 128], [512 * 128, 8], [1, 512]])))
        b.wait("pool", "di", tok)
        tp_ = b.op("pool", K.cp(self.gcw1, self.arD[:, 0:4096]), inc="pool")
        b.di_done("pool", tp_)
        tok = b.dil(K.dma(self.Dv(4224, 1024),
                          K.ap(d["gcw2"].tensor, 0,
                               [[256, 128], [256 * 128, 4], [1, 256]])))
        b.wait("pool", "di", tok)
        tp_ = b.op("pool", K.cp(self.gcw2[:, 0:1024], self.Dv(4224, 1024)),
                   inc="pool")
        b.di_done("pool", tp_)
        tok = b.dil(K.dmas(self.Dv(5378, 2),
                           K.ap(d["gcw3"].tensor, 0, [[1, 128], [128, 2], [1, 1]])))
        b.wait("pool", "di", tok)
        self.t_gcinit = b.op("pool", K.cp(self.gcw3[:, 0:2], self.Dv(5378, 2)),
                             inc="pool")
        b.di_done("pool", self.t_gcinit)

    def _phase_a(self):
        b, d = self, self.d
        PS_MEAN = self.PSF[:, 3072:3080]
        trash = self.Dv(3072, 1024)
        xins = [self.Dv(0, 1024), self.Dv(1024, 1024), self.Dv(2048, 1024)]
        sm = self.small32
        self.tA_st = [0] * NT
        tA_ev = {}
        t_mm = [0] * NT
        tA_xn = [0] * NT
        b.wait("syncq", "pool", self.t_gcinit)
        b.wait("act", "pool", self.t_gcinit)
        b.wait("dve", "pool", self.t_gcinit)
        b.wait("pe", "pool", self.t_gcinit)
        for i in range(NT):
            xb = i % 2
            xin = xins[i % 3]
            dxs = "dx%d" % (i % 3)
            txi = b.dld(dxs, ("dve", tA_xn[i - 3] if i >= 3 else 0),
                        K.dma(xin, d["x"][i * 128:(i + 1) * 128, :]))
            b.wait("act", dxs, txi)
            if i >= 2:
                b.wait("act", "dve", tA_xn[i - 2])
            sq = sm[:, 0 + xb:1 + xb]
            ms = sm[:, 2 + xb:3 + xb]
            b.op("act", K.act(trash, xin, AF.Square, accum=sq))
            ta = b.op("act", K.act(trash, xin, AF.Copy, accum=ms), inc="act")
            b.wait("dve", "act", ta)
            m = sm[:, 4 + xb:5 + xb]
            rstd = sm[:, 9 + xb:10 + xb]
            b.op("dve", K.ts(m, ms, 1.0 / 1024.0))
            b.op("dve", K.ts(sm[:, 6:7], sq, 1.0 / 1024.0))
            b.op("dve", K.tt(sm[:, 7:8], m, m, OP.mult))
            td = b.op("dve", K.tt(sm[:, 8:9], sm[:, 6:7], sm[:, 7:8], OP.subtract),
                      inc="dve")
            b.wait("act", "dve", td)
            b.op("act", K.act(sm[:, 8:9], sm[:, 8:9], AF.Ln, bias=self.cc[:, 0:1]))
            ta2 = b.op("act", K.act(rstd, sm[:, 8:9], AF.Exp, scale=-0.5), inc="act")
            b.wait("dve", "act", ta2)
            if i >= 2:
                b.wait("dve", "pe", t_mm[i - 2])
                b.wait("dve", "sx%d" % xb, self.tA_st[i - 2])
            tA_xn[i] = b.op("dve", K.ts(self.xn32[xb], xin, m, rstd,
                                        op0=OP.subtract, op1=OP.mult), inc="dve")
            b.wait("pool", "dve", tA_xn[i])
            self.tA_st[i] = b.op("pool", K.dma(d["xn_sc"][i * 128:(i + 1) * 128, :],
                                               self.xn32[xb]),
                                 inc="sx%d" % xb, n=16)
            b.wait("pe", "dve", tA_xn[i])
            b.wait("pe", "di", self.t_ident)
            for r in range(2):
                if (i, r) != (0, 0):
                    prev = tA_ev[(i, 0)] if r == 1 else tA_ev[(i - 1, 1)]
                    b.wait("pe", "dve", prev)
                tok = None
                for c4 in range(4):
                    c = 4 * r + c4
                    tok = b.op("pe", K.tp(self.PST[:, c4 * 128:(c4 + 1) * 128],
                                          self.xn32[xb][:, c * 128:(c + 1) * 128],
                                          self.ident32[:, :]),
                               inc="pe" if c4 == 3 else None)
                b.wait("dve", "pe", tok)
                dst = K.ap(self.arA, (4 * r) * 4096 + i * 128,
                           [[32768, 128], [4096, 4], [1, 128]])
                src = K.ap(self.PST, 0, [[512, 128], [128, 4], [1, 128]])
                tA_ev[(i, r)] = b.op("dve", K.cp(dst, src), inc="dve")
            for c in range(8):
                t_mm[i] = b.op("pe", K.mm(PS_MEAN[:, c:c + 1],
                                          self.xn32[xb][:, c * 128:(c + 1) * 128],
                                          self.ones32[:, 0:1],
                                          start=(i == 0 and c == 0),
                                          stop=(i == NT - 1 and c == 7)),
                               inc="pe" if c == 7 else None)
        b.wait("dve", "pe", t_mm[NT - 1])
        self.t_mean16 = b.op("dve", K.ts(self.mean16, PS_MEAN, 1.0 / 4096.0),
                             inc="dve")

    def _gf_net(self):
        b = self
        PSG = self.PSF[0:1, 0:512]
        b.wait("pe", "dve", self.t_mean16)
        b.wait("pe", "pool", self.t_gcinit)
        for c in range(8):
            b.op("pe", K.mm(PSG, self.mean16[:, c:c + 1],
                            self.gcw1[:, c * 512:(c + 1) * 512],
                            start=(c == 0), stop=False))
        tok = b.op("pe", K.mm(PSG, self.ones16[0:1, 0:1], self.gcb1r,
                              start=False, stop=True), inc="pe")
        b.wait("act", "pe", tok)
        ta = b.op("act", K.act(self.h1row, PSG, AF.Gelu), inc="act")
        b.wait("pe", "act", ta)
        tok = None
        for c in range(4):
            tok = b.op("pe", K.tp(self.PST[:, c:c + 1],
                                  self.h1row[0:1, c * 128:(c + 1) * 128],
                                  self.ident32[0:1, 0:1]),
                       inc="pe" if c == 3 else None)
        b.wait("dve", "pe", tok)
        td = b.op("dve", K.cp(self.h1T, self.PST[:, 0:4]), inc="dve")
        PSG2 = self.PSF[0:1, 512:768]
        b.wait("pe", "dve", td)
        for c in range(4):
            b.op("pe", K.mm(PSG2, self.h1T[:, c:c + 1],
                            self.gcw2[:, c * 256:(c + 1) * 256],
                            start=(c == 0), stop=False))
        tok = b.op("pe", K.mm(PSG2, self.ones16[0:1, 0:1], self.gcb2r,
                              start=False, stop=True), inc="pe")
        b.wait("act", "pe", tok)
        ta = b.op("act", K.act(self.h2row, PSG2, AF.Gelu), inc="act")
        b.wait("pe", "act", ta)
        tok = None
        for c in range(2):
            tok = b.op("pe", K.tp(self.PST[:, c:c + 1],
                                  self.h2row[0:1, c * 128:(c + 1) * 128],
                                  self.ident32[0:1, 0:1]),
                       inc="pe" if c == 1 else None)
        b.wait("dve", "pe", tok)
        td = b.op("dve", K.cp(self.h2T, self.PST[:, 0:2]), inc="dve")
        PSG3 = self.PSF[0:1, 768:769]
        b.wait("pe", "dve", td)
        tok = None
        for c in range(2):
            tok = b.op("pe", K.mm(PSG3, self.h2T[:, c:c + 1], self.gcw3[:, c:c + 1],
                                  start=(c == 0), stop=(c == 1)),
                       inc="pe" if c == 1 else None)
        b.wait("act", "pe", tok)
        ta = b.op("act", K.act(self.sc32[0:1, 9:10], PSG3, AF.Tanh,
                               bias=self.sc32[0:1, 6:7], scale=0.5), inc="act")
        b.wait("dve", "act", ta)
        self.t_gfc = b.op("dve", K.ts(self.sc32[0:1, 8:9], self.sc32[0:1, 9:10],
                                      0.35, 0.5, op0=OP.mult, op1=OP.add), inc="dve")

    def _wfuse1(self):
        b, d = self, self.d
        psWf = self.PSF[:, 3080:3144]
        psUT = self.PSF[:, 1024:1056]
        psBF1 = self.PSF[0:1, 1536:1544]
        halves = [self.Dv(0, 2048), self.Dv(2048, 2048)]
        w1g = self.arB[:, 0:4096].bitcast(F32)
        w1t16 = self.arB[:, 4096:4608]
        t_used = [0, 0]
        t_ev_last = 0
        for fc in range(8):
            t_h = [0, 0]
            for h in range(2):
                t_h[h] = b.dld("dw%d" % h, ("pe", t_used[h]),
                               K.dma(halves[h],
                                     d["W1"][fc * 128:(fc + 1) * 128,
                                             h * 2048:(h + 1) * 2048]))
            for h in range(2):
                b.wait("dve", "dw%d" % h, t_h[h])
                tg = b.op("dve", K.ts(w1g, halves[h], self.col("g1c", fc, 1)),
                          inc="dve")
                b.wait("pe", "dw%d" % h, t_h[h])
                for ms in range(16):
                    b.op("pe", K.mm(psUT[:, h * 16 + ms:h * 16 + ms + 1],
                                    halves[h][:, ms * 128:(ms + 1) * 128],
                                    self.col("be1c", fc, 1),
                                    start=(fc == 0 and h == 0 and ms == 0),
                                    stop=(fc == 7 and h == 1 and ms == 15)))
                b.wait("pe", "dve", tg)
                last_mm = 0
                for r in range(4):
                    if t_ev_last:
                        b.wait("pe", "dve", t_ev_last)
                    tok = None
                    for j in range(4):
                        ms = 4 * r + j
                        tok = b.op("pe", K.tp(self.PST[:, j * 128:(j + 1) * 128],
                                              w1g[:, ms * 128:(ms + 1) * 128],
                                              self.ident32[:, :]),
                                   inc="pe" if j == 3 else None)
                    b.wait("dve", "pe", tok)
                    t_ev_last = b.op("dve", K.cp(w1t16, self.PST[:, 0:512]),
                                     inc="dve")
                    b.wait("pe", "dve", t_ev_last)
                    for j in range(4):
                        ms = 4 * r + j
                        kk = h * 16 + ms
                        last_mm = b.op("pe", K.mm(
                            psWf[:, fc * 8:(fc + 1) * 8],
                            w1t16[:, j * 128:(j + 1) * 128],
                            self.Wm16[:, kk * 8:(kk + 1) * 8],
                            start=(kk == 0), stop=(kk == 31)),
                            inc="pe" if j == 3 else None)
                t_used[h] = last_mm
        t_wf = self.cnt["pe"]
        b.wait("dve", "pe", t_wf)
        b.op("dve", K.cp(self.Wf16[:, :], psWf))
        b.op("dve", K.tt(self.qk32[:, 0:32], psUT, self.col("b1c"), OP.add))
        tuc = b.op("dve", K.cp(self.ucol16, self.qk32[:, 0:32]), inc="dve")
        b.wait("pe", "dve", tuc)
        tok = None
        for ms in range(32):
            tok = b.op("pe", K.mm(psBF1, self.ucol16[:, ms:ms + 1],
                                  self.Wm16[:, ms * 8:(ms + 1) * 8],
                                  start=(ms == 0), stop=(ms == 31)),
                       inc="pe" if ms == 31 else None)
        b.wait("dve", "pe", tok)
        self.t_bf1 = b.op("dve", K.tt(self.rows16[0:1, 0:8], psBF1,
                                      self.rows16[0:1, 8:16], OP.add), inc="dve")

    def _wfuse2(self):
        b, d = self, self.d
        psW2 = self.PSF[:, 3200:3264]
        psBF2 = self.PSF[:, 3264:3272]
        wqost = self.arB[0:8, 0:4096].bitcast(F32)
        t_tp = 0
        t_wq = 0
        for h in range(2):
            tok = b.dld("dw2", ("dve", t_wq),
                        K.dma(wqost, d["Wqo"][0:8, h * 2048:(h + 1) * 2048]))
            b.wait("pe", "dw2", tok)
            for ms in range(16):
                t_tp = b.op("pe", K.tp(self.PST[:, ms * 8:(ms + 1) * 8],
                                       wqost[0:8, ms * 128:(ms + 1) * 128],
                                       self.ident32[0:8, 0:8]),
                            inc="pe" if ms == 15 else None)
            b.wait("dve", "pe", t_tp)
            t_wq = b.op("dve", K.cp(self.WqoT16[:, h * 128:(h + 1) * 128],
                                    self.PST[:, 0:128]), inc="dve")
        w2st = [self.Dv(0, 1024), self.Dv(1024, 1024)]
        w2c16 = [self.arB[:, 0:1024], self.arB[:, 1024:2048]]
        t_c = [t_tp, t_tp]
        b.wait("pe", "dve", t_wq)
        for k in range(NT):
            h = k % 2
            tok = b.dld("dw%d" % h, ("pe", t_c[h]),
                        K.dma(w2st[h], d["W2"][k * 128:(k + 1) * 128, :]))
            b.wait("pool", "dw%d" % h, tok)
            if t_c[h]:
                b.wait("pool", "pe", t_c[h])
            tp_ = b.op("pool", K.cp(w2c16[h], w2st[h]), inc="pool")
            b.wait("pe", "pool", tp_)
            tok2 = None
            for fc in range(8):
                b.op("pe", K.mm(psW2[:, fc * 8:(fc + 1) * 8],
                                w2c16[h][:, fc * 128:(fc + 1) * 128],
                                self.WqoT16[:, k * 8:(k + 1) * 8],
                                start=(k == 0 and fc == 0), stop=False))
                tok2 = b.op("pe", K.mm(psBF2[:, fc:fc + 1],
                                       w2c16[h][:, fc * 128:(fc + 1) * 128],
                                       self.bqo16[:, k:k + 1],
                                       start=False,
                                       stop=(k == NT - 1 and fc == 7)),
                            inc="pe" if fc == 7 else None)
            t_c[h] = tok2
        b.wait("dve", "pe", self.cnt["pe"])
        b.op("dve", K.cp(self.w2ftc[:, :], psW2))
        self.t_w2f = b.op("dve", K.tt(self.col("bf2c"), psBF2, self.col("b2c"),
                                      OP.add), inc="dve")

    def _load_cn(self):
        b, d = self, self.d
        self.cnw1 = self.arW
        st = [self.Dv(0, 2048), self.Dv(2048, 2048)]
        jobs = []
        for c in range(4):
            jobs.append((2048, self.arW[:, c * 2048:(c + 1) * 2048],
                         K.ap(d["cnw1"].tensor, c * 2 * 128 * 1024,
                              [[1024, 128], [128 * 1024, 2], [1, 1024]])))
        for h in range(2):
            jobs.append((2048, self.cnw2[:, h * 2048:(h + 1) * 2048],
                         K.ap(d["cnw2"].tensor, h * 4 * 512 * 128,
                              [[512, 128], [512 * 128, 4], [1, 512]])))
        jobs.append((1024, self.cnw3[:, 0:1024],
                     K.ap(d["cnw3"].tensor, 0,
                          [[256, 128], [256 * 128, 4], [1, 256]])))
        t_cast = [0, 0]
        for j, (n, dstv, ap) in enumerate(jobs):
            sl = j % 2
            tok = b.dld("dw%d" % sl, ("dve", t_cast[sl]),
                        K.dma(st[sl][:, 0:n], ap))
            b.wait("dve", "dw%d" % sl, tok)
            t_cast[sl] = b.op("dve", K.cp(dstv, st[sl][:, 0:n]), inc="dve")
        tok = b.dld("dw2", ("dve", t_cast[0]),
                    K.dmas(self.Dv(4096, 2),
                           K.ap(d["cnw4"].tensor, 0, [[1, 128], [128, 2], [1, 1]])))
        b.wait("dve", "dw2", tok)
        self.t_cn = b.op("dve", K.cp(self.cnw4[:, 0:2], self.Dv(4096, 2)),
                         inc="dve")

    def _phase_g(self):
        b, d = self, self.d
        xld = [self.Dv(0, 1024), self.Dv(1024, 1024), self.Dv(2048, 1024)]
        gt = self.Dv(3072, 1024)
        PSH = [self.PSF[:, 0:512], self.PSF[:, 512:1024]]
        PSH2 = [self.PSF[:, 1024:1536], self.PSF[:, 1536:2048]]
        PSH3 = [self.PSF[:, 2048:2560], self.PSF[:, 2048:2560]]
        PSH4 = self.PSF[0:1, 2560:3072]
        PSCPL = self.PSF[:, 2560:3072]
        cbc = K.ap(self.arE, 8192, [[13312, 128], [0, 2], [1, 1024]]).bitcast(F32)
        b.wait("pe", "dve", self.t_cn)
        b.wait("pe", "dve", self.t_gfc)
        b.wait("syncq", "dve", self.t_cn)
        b.wait("syncq", "sx0", self.tA_st[30])
        b.wait("syncq", "sx1", self.tA_st[31])
        og2 = self.arC[:, 0:8192].bitcast(F32)
        slots = [dict(ar=self.arB, og=self.og, o16=self.out16),
                 dict(ar=self.arC, og=og2, o16=self.arC[:, 8192:12288])]
        t_st = 0
        t_dg = [0, 0, 0]
        t_pst = 0
        t_g = [0, 0]
        t_g2 = [0, 0]
        t_g3 = [0, 0]
        t_thr = 0
        t_cb = 0
        tc2 = 0
        t_tws1p = 0
        t_h1mm = [0, 0]
        t_o16 = [0, 0]

        def emit_cast(sidx):
            sl = slots[sidx]
            b.wait("dve", "pe", t_h1mm[sidx])
            t_o16[sidx] = b.op("dve", K.cp(sl["o16"], sl["og"]), inc="dve")

        def emit_round(sidx):
            nonlocal t_thr, t_cb, tc2, t_tws1p
            sl = slots[sidx]
            b.wait("pe", "dve", t_o16[sidx])
            for fc in range(8):
                if t_g[fc % 2]:
                    b.wait("pe", "act", t_g[fc % 2])
                tok = None
                for kc in range(8):
                    tok = b.op("pe", K.mm(
                        PSH[fc % 2],
                        self.cnw1[:, kc * 1024 + fc * 128:
                                  kc * 1024 + (fc + 1) * 128],
                        sl["o16"][:, kc * 512:(kc + 1) * 512],
                        start=(kc == 0), stop=(kc == 7)),
                        inc="pe" if kc == 7 else None)
                b.wait("act", "pe", tok)
                t_g[fc % 2] = b.op("act", K.act(
                    self.h1g[:, fc * 512:(fc + 1) * 512], PSH[fc % 2],
                    AF.Gelu, bias=self.col("cb1", fc, 1)), inc="act")
            t_h1mm[sidx] = self.cnt["pe"]
            t_h1 = self.cnt["act"]
            for fc in range(4):
                if t_g2[fc % 2]:
                    b.wait("pe", "act", t_g2[fc % 2])
                b.wait("pe", "act", t_h1)
                tok = None
                for kc in range(8):
                    tok = b.op("pe", K.mm(
                        PSH2[fc % 2],
                        self.cnw2[:, kc * 512 + fc * 128:
                                  kc * 512 + (fc + 1) * 128],
                        self.h1g[:, kc * 512:(kc + 1) * 512],
                        start=(kc == 0), stop=(kc == 7)),
                        inc="pe" if kc == 7 else None)
                b.wait("act", "pe", tok)
                t_g2[fc % 2] = b.op("act", K.act(
                    self.h2g[:, fc * 512:(fc + 1) * 512], PSH2[fc % 2],
                    AF.Gelu, bias=self.col("cb2", fc, 1)), inc="act")
            t_h2 = self.cnt["act"]
            for fc in range(2):
                if t_g3[(fc - 1) % 2]:
                    b.wait("pe", "act", t_g3[(fc - 1) % 2])
                b.wait("pe", "act", t_h2)
                tok = None
                for kc in range(4):
                    tok = b.op("pe", K.mm(
                        PSH3[fc % 2],
                        self.cnw3[:, kc * 256 + fc * 128:
                                  kc * 256 + (fc + 1) * 128],
                        self.h2g[:, kc * 512:(kc + 1) * 512],
                        start=(kc == 0), stop=(kc == 3)),
                        inc="pe" if kc == 3 else None)
                b.wait("act", "pe", tok)
                t_g3[fc % 2] = b.op("act", K.act(
                    self.h3g[:, fc * 512:(fc + 1) * 512], PSH3[fc % 2],
                    AF.Gelu, bias=self.col("cb3", fc, 1)), inc="act")
            t_h3 = self.cnt["act"]
            b.wait("pe", "act", t_h3)
            b.wait("pe", "act", t_thr)
            b.wait("pe", "dve", t_cb)
            tok = None
            for kc in range(2):
                tok = b.op("pe", K.mm(PSH4, self.cnw4[:, kc:kc + 1],
                                      self.h3g[:, kc * 512:(kc + 1) * 512],
                                      start=(kc == 0), stop=(kc == 1)),
                           inc="pe" if kc == 1 else None)
            b.wait("act", "pe", tok)
            b.wait("act", "dve", tc2)
            b.wait("act", "pool", t_tws1p)
            thr = self.rb_s[0:1, 0:512]
            ta = b.op("act", K.act(thr, PSH4, AF.Tanh, bias=self.sc32[0:1, 5:6],
                                   scale=0.5), inc="act")
            t_thr = ta
            b.wait("dve", "act", ta)
            b.op("dve", K.ts(thr, thr, 0.15, self.sc32[0:1, 8:9],
                             op0=OP.mult, op1=OP.add))
            tc = b.op("dve", K.ts(thr, thr, self.sc32[0:1, 4:5],
                                  self.sc32[0:1, 7:8], op0=OP.mult, op1=OP.add),
                      inc="dve")
            b.wait("pe", "dve", tc)
            b.wait("pe", "dve", t_cb)
            t_bc = b.op("pe", K.mm(PSCPL, self.ones32[0:1, 0:128], thr),
                        inc="pe")
            b.wait("dve", "pe", t_bc)
            t_cb = b.op("dve", K.cp(self.rb_s, PSCPL), inc="dve")
            tc2 = t_cb
            og = sl["og"]
            for pair in range(2):
                srcs = (2, 0) if pair == 0 else (3, 1)
                dsts = (0, 2) if pair == 0 else (1, 3)
                ops = (OP.add, OP.subtract) if pair == 0 else \
                    (OP.subtract, OP.add)
                b.wait("pool", "dve", tc2)
                tws = [0, 0]
                tws[0] = b.op("dve", K.tt(
                    self.twt[0], og[:, srcs[0] * 1024:(srcs[0] + 1) * 1024],
                    cbc, OP.mult), inc="dve")
                tws[1] = b.op("pool", K.tt(
                    self.twt[1], og[:, srcs[1] * 1024:(srcs[1] + 1) * 1024],
                    cbc, OP.mult), inc="pool")
                t_tws1p = tws[1]
                for j in range(2):
                    qq = dsts[j]
                    b.wait("dve", "pool", tws[1])
                    tc2 = b.op("dve", K.tt(
                        og[:, qq * 1024:(qq + 1) * 1024],
                        og[:, qq * 1024:(qq + 1) * 1024],
                        self.twt[j], ops[j]), inc="dve")

        for gp in range(4):
            b.wait("dve", "pe", self.cnt["pe"])
            b.wait("dve", "pool", self.cnt["pool"])
            for sidx in (0, 1):
                g = 2 * gp + sidx
                sl = slots[sidx]
                t_ev = 0
                for i in range(4):
                    T = g * 4 + i
                    slq = T % 3
                    tok = b.dld("dg%d" % slq, ("pe", t_dg[slq]),
                                K.dma(xld[slq],
                                      d["xn_sc"][T * 128:(T + 1) * 128, :]))
                    b.wait("pe", "dg%d" % slq, tok)
                    for r in range(2):
                        if t_pst:
                            b.wait("pe", "dve", t_pst)
                        tok2 = None
                        for c4 in range(4):
                            c = 4 * r + c4
                            tok2 = b.op("pe", K.tp(
                                self.PST[:, c4 * 128:(c4 + 1) * 128],
                                xld[slq][:, c * 128:(c + 1) * 128],
                                self.ident32[:, :]),
                                inc="pe" if c4 == 3 else None)
                        b.wait("dve", "pe", tok2)
                        dst = self.f32ap(sl["ar"], (4 * r) * 512 + i * 128,
                                         [[8192, 128], [512, 4], [1, 128]])
                        srcp = K.ap(self.PST, 0, [[512, 128], [128, 4], [1, 128]])
                        t_ev = b.op("dve", K.cp(dst, srcp), inc="dve")
                        t_pst = t_ev
                    t_dg[slq] = tok2
            for sidx in (0, 1):
                emit_cast(sidx)
            for r in range(3):
                for sidx in (0, 1):
                    emit_round(sidx)
                    if r < 2:
                        emit_cast(sidx)
            for sidx in (0, 1):
                g = 2 * gp + sidx
                sl = slots[sidx]
                th16 = sl["o16"]
                b.wait("act", "dve", self.cnt["dve"])
                b.wait("act", "pool", self.cnt["pool"])
                b.wait("act", "pe", t_h1mm[sidx])
                ta = b.op("act", K.act(th16, sl["og"], AF.Tanh, scale=0.5),
                          inc="act")
                b.wait("dve", "act", ta)
                b.op("dve", K.ts(th16, th16, 1.0, op0=OP.add))
                t_gg = b.op("dve", K.tt(sl["og"], sl["og"], th16, OP.mult),
                            inc="dve")
                b.wait("pe", "dve", t_gg)
                for i in range(4):
                    T = g * 4 + i
                    for r in range(2):
                        if t_pst:
                            b.wait("pe", "dve", t_pst)
                        tok2 = None
                        for c4 in range(4):
                            c = 4 * r + c4
                            tok2 = b.op("pe", K.tp(
                                self.PST[:, c4 * 128:(c4 + 1) * 128],
                                self.f32ap(sl["ar"], c * 512 + i * 128,
                                           [[8192, 128], [1, 128]]),
                                self.ident32[:, :]), inc="pe" if c4 == 3 else None)
                        b.wait("dve", "pe", tok2)
                        if t_st:
                            b.wait("dve", "sg", t_st)
                        dst = K.ap(self.arD, 3072 + (4 * r) * 128,
                                   [[6144, 128], [128, 4], [1, 128]])
                        srcp = K.ap(self.PST, 0, [[512, 128], [128, 4], [1, 128]])
                        t_ev2 = b.op("dve", K.cp(dst, srcp), inc="dve")
                        t_pst = t_ev2
                    b.wait("pool", "dve", t_ev2)
                    t_st = b.op("pool", K.dma(d["g_sc"][T * 128:(T + 1) * 128, :],
                                              gt), inc="sg", n=16)
        self.tG_gst = t_st

    def _phase_b(self):
        b, d = self, self.d
        psA_ = self.PSF[:, 0:1024]
        psB_ = [self.PSF[:, 1024:2048], self.PSF[:, 2048:3072]]
        psPV = self.PSF[:, 3072:3584]
        psS = self.PST[0:1, 0:512]
        psQK = self.PSF[:, 1024:1536]
        wst = [self.Dv(0, 1024), self.Dv(1024, 1024), self.Dv(2048, 1024)]
        wq16 = self.arW[:, 0:1024]
        wk16 = self.arW[:, 1024:2048]
        wv16 = self.arW[:, 2048:3072]
        vftmp = self.Dv(2048, 4096)
        sm = self.small32

        t_stt = [0] * 16
        t_tt = [0] * 16
        t_exp = [0] * 16
        t_pvs = [0] * 16
        t_sin = [0] * 16
        t_mmA = [0] * 16
        t_mmB = [0] * 16
        t_attn = [0, 0]
        t_recip = 0
        t_norm = 0
        t_sr1_roll = 0
        self.fpend = None

        t_pe_flash = 0
        for n in range(8):
            tw = []
            for wi, wd in enumerate([d["Wq"], d["Wk"], d["Wv"]]):
                tw.append(b.dld("dw%d" % wi, ("pe", t_pe_flash),
                                K.dma(wst[wi], K.ap(wd.tensor, n * 1024 * 128,
                                                    [[128, 128], [128 * 128, 8],
                                                     [1, 128]]))))
            b.wait("pool", "pe", t_pe_flash)
            for wi, w16 in enumerate([wq16, wk16, wv16]):
                b.wait("pool", "dw%d" % wi, tw[wi])
                b.op("pool", K.cp(w16, wst[wi]), inc="pool")
            t_w16 = self.cnt["pool"]
            b.wait("dve", "act", self.cnt["act"])
            b.op("dve", K.cp(self.bcqpat[0:1, 0:8],
                             self.bc2q16[0:1, n * 8:(n + 1) * 8]))
            b.op("dve", K.cp(self.bcqpat[0:1, 8:16],
                             self.bc2q16[0:1, n * 8:(n + 1) * 8]))
            w = 16
            while w < 512:
                b.op("dve", K.cp(self.bcqpat[0:1, w:2 * w], self.bcqpat[0:1, 0:w]))
                w *= 2
            t_pat = b.op("dve", lambda e: e.nop(), inc="dve")
            b.wait("pe", "pool", t_w16)
            b.wait("pe", "dve", self.cnt["dve"])
            t_ev = {}
            for wi in range(3):
                w16 = [wq16, wk16, wv16][wi]
                for sg in range(8):
                    ps = psB_[sg % 2]
                    prev = t_ev.get((wi, sg - 2), t_ev.get((wi - 1, 6 + sg), 0))
                    b.wait("pe", "dve", prev)
                    tok = None
                    for kc in range(8):
                        tok = b.op("pe", K.mm(
                            ps[:, 0:512], w16[:, kc * 128:(kc + 1) * 128],
                            self.xn_fm[:, kc * 4096 + sg * 512:
                                       kc * 4096 + (sg + 1) * 512],
                            start=(kc == 0), stop=(kc == 7)),
                            inc="pe" if kc == 7 else None)
                    b.wait("dve", "pe", tok)
                    if wi == 0:
                        fn = K.ts(self.qc_fm[:, sg * 512:(sg + 1) * 512],
                                  ps[:, 0:512], self.col("bq", n, 1), C1,
                                  op0=OP.add, op1=OP.mult)
                    elif wi == 1:
                        fn = K.ts(self.k_fm[:, sg * 512:(sg + 1) * 512],
                                  ps[:, 0:512], self.col("bk", n, 1), op0=OP.add)
                    else:
                        fn = K.ts(vftmp[:, sg * 512:(sg + 1) * 512], ps[:, 0:512],
                                  self.col("bv", n, 1), op0=OP.add)
                    t_ev[(wi, sg)] = b.op("dve", fn, inc="dve")
            t_qkv = self.cnt["dve"]
            b.wait("pe", "dve", t_qkv)
            t_vev = 0
            for rr in range(8):
                if t_vev:
                    b.wait("pe", "dve", t_vev)
                tok = None
                for j in range(4):
                    tau = rr * 4 + j
                    tok = b.op("pe", K.tp(self.PST[:, j * 128:(j + 1) * 128],
                                          vftmp[:, tau * 128:(tau + 1) * 128],
                                          self.ident32[:, :]),
                               inc="pe" if j == 3 else None)
                b.wait("dve", "pe", tok)
                t_vev = b.op("dve", K.cp(self.v_tm[:, rr * 512:(rr + 1) * 512],
                                         self.PST[:, 0:512]), inc="dve")
            b.wait("pe", "dve", t_vev)
            for tau in range(NT):
                b.op("pe", K.mm(psQK[:, 16 * tau:16 * tau + 8],
                                self.qc_fm[:, tau * 128:(tau + 1) * 128],
                                self.Wc2qc16[:, n * 8:(n + 1) * 8],
                                start=(tau == 0), stop=False))
                b.op("pe", K.mm(psQK[:, 16 * tau + 8:16 * tau + 16],
                                self.k_fm[:, tau * 128:(tau + 1) * 128],
                                self.Wc2q16[:, n * 8:(n + 1) * 8],
                                start=False, stop=False))
            b.wait("pe", "dve", t_pat)
            t_qk = b.op("pe", K.mm(psQK, self.ones16[0:1, 0:128], self.bcqpat,
                                   start=False, stop=True), inc="pe")
            b.wait("act", "pe", t_qk)
            ta = b.op("act", K.act(self.sqt[:, :], psQK, AF.Square), inc="act")
            b.wait("dve", "act", ta)
            td = b.op("dve", lambda e: e.reduce_sum(
                K.ap(self.small32, 64, [[192, 128], [2, 32], [1, 2]]),
                K.ap(self.sqt, 0, [[512, 128], [16, 32], [8, 2], [1, 8]]),
                axis=mybir.AxisListType.X), inc="dve")
            b.wait("act", "dve", td)
            b.op("act", K.act(self.nsq, self.nsq, AF.Ln, bias=self.cc[:, 1:2]))
            ta = b.op("act", K.act(self.nsq, self.nsq, AF.Exp, scale=-0.5),
                      inc="act")
            b.wait("dve", "act", ta)
            b.op("dve", K.ts(self.nsq, self.nsq, 1e4, op0=OP.min))
            b.op("dve", K.tt(self.rs,
                             self.nsq,
                             K.ap(self.cols32, self.CL["pat2"][0],
                                  [[160, 128], [0, 32], [1, 2]]),
                             OP.mult))
            tqk32 = b.op("dve", K.tt(
                self.qk32, psQK,
                K.ap(self.small32, 128, [[192, 128], [2, 32], [1, 2], [0, 8]]),
                OP.mult), inc="dve")
            b.wait("pe", "dve", tqk32)
            t_qev = 0
            for rr in range(16):
                if t_qev:
                    b.wait("pe", "act", t_qev)
                tok = None
                for j in range(4):
                    idx = rr * 4 + j
                    tau, side = idx // 2, idx % 2
                    tok = b.op("pe", K.tp(self.PST[0:8, j * 128:(j + 1) * 128],
                                          self.qk32[:, 16 * tau + 8 * side:
                                                    16 * tau + 8 * side + 8],
                                          self.ident32[:, :]),
                               inc="pe" if j == 3 else None)
                b.wait("act", "pe", tok)
                tau0 = rr * 2
                b.op("act", K.acp(self.qh[0:8, tau0 * 128:(tau0 + 2) * 128],
                                  K.ap(self.PST, 0, [[512, 8], [256, 2], [1, 128]])))
                t_qev = b.op("act", K.acp(
                    self.kl[0:8, tau0 * 128:(tau0 + 2) * 128],
                    K.ap(self.PST, 128, [[512, 8], [256, 2], [1, 128]])),
                    inc="act")
            t_prep_act = t_qev
            t_prep_pool = self.cnt["pool"]

            for i in range(9):
                has_F = i < 8
                has_B = i >= 1
                D0 = self.cnt["dve"]
                A0 = self.cnt["act"]
                has_norm = self.fpend is not None
                pv15_old = t_pvs[15]
                pvs_prev = list(t_pvs)
                tt_prev = list(t_tt)

                def pred_dquad(j):
                    return A0 + (8 if has_F else 0) + (j - 2) + 1

                def pred_stt(p):
                    return D0 + 2 * p + 1 + (2 if has_norm else 0)

                def pred_tt(p):
                    return pred_stt(p) + 1
                cp_tok = D0 + 1
                norm_tok = D0 + 2
                if has_norm:
                    b.wait("pe", "dve", t_recip)
                    b.wait("pe", "dve", t_tt[15])
                    t_bc = b.op("pe", K.mm(psA_[:, 0:512],
                                           self.ones32[0:1, 0:128],
                                           self.rb_s[0:1, 0:512]), inc="pe")
                for p in range(16):
                    if has_B:
                        b.wait("pe", "act",
                               t_exp[p] if p < 8 else pred_dquad(p // 4))
                        if p == 0:
                            b.wait("pe", "dve",
                                   norm_tok if has_norm else t_norm)
                            b.wait("pe", "dve", t_recip)
                        for half in range(2):
                            t_blk = 2 * p + half
                            b.op("pe", K.mm(
                                psPV,
                                self.v_tm[:, t_blk * 128:(t_blk + 1) * 128],
                                self.ein[:, p * 1024 + half * 512:
                                         p * 1024 + (half + 1) * 512],
                                start=(p == 0 and half == 0),
                                stop=(p == 15 and half == 1)))
                            t_pvs[p] = b.op("pe", K.mm(
                                psS, self.ones16[:, 0:1],
                                self.ein[:, p * 1024 + half * 512:
                                         p * 1024 + (half + 1) * 512],
                                start=(p == 0 and half == 0),
                                stop=(p == 15 and half == 1)),
                                inc="pe" if half == 1 else None)
                    if not has_F:
                        continue
                    pb = psB_[p % 2]
                    b.wait("pe", "act", t_prep_act)
                    b.wait("pe", "dve",
                           t_stt[(p - 2) % 16] if p < 2 else pred_stt(p - 2))
                    for half in range(2):
                        t_blk = 2 * p + half
                        t_mmB[p] = b.op("pe", K.mm(
                            pb[:, half * 512:(half + 1) * 512],
                            self.kl[0:8, t_blk * 128:(t_blk + 1) * 128],
                            self.qh[0:8, i * 512:(i + 1) * 512],
                            start=True, stop=True),
                            inc="pe" if half == 1 else None)
                    if p == 0:
                        b.wait("pe", "dve", t_tt[15])
                        b.wait("pe", "dve",
                               cp_tok if has_norm else t_norm)
                    else:
                        b.wait("pe", "dve", pred_tt(p - 1))
                    for half in range(2):
                        t_blk = 2 * p + half
                        t_mmA[p] = b.op("pe", K.mm(
                            psA_[:, half * 512:(half + 1) * 512],
                            self.k_fm[:, t_blk * 128:(t_blk + 1) * 128],
                            self.qc_fm[:, i * 512:(i + 1) * 512],
                            start=True, stop=True),
                            inc="pe" if half == 1 else None)
                def emit_sin(p):
                    b.wait("act", "pe", t_mmB[p])
                    b.wait("act", "dve",
                           t_stt[p + 10] if p < 6 else pred_stt(p - 6))
                    t_sin[p] = b.op("act", K.act(
                        self.s2bp[(p // 2) % 3][:, (p % 2) * 1024:
                                                (p % 2) * 1024 + 1024],
                        psB_[p % 2], AF.Sin, scale=SIN_SCALE), inc="act")

                def emit_equad(sl, j, gate_tt, gate_pe):
                    b.wait("act", "dve", gate_tt)
                    b.wait("act", "pe", gate_pe)
                    tj = b.op("act", K.act(
                        self.ein[:, j * 4096:(j + 1) * 4096],
                        self.ein[:, j * 4096:(j + 1) * 4096],
                        AF.Exp), inc="act")
                    for pp in range(4 * j, 4 * j + 4):
                        t_exp[pp] = tj
                for p in range(8 if has_F else 0):
                    emit_sin(p)
                if has_B:
                    for j in (2, 3):
                        emit_equad(i - 1, j, tt_prev[4 * j + 3],
                                   pvs_prev[4 * j + 3])
                for p in range(8, 16) if has_F else ():
                    emit_sin(p)
                if has_norm:
                    fn2, fss = self.fpend
                    b.wait("dve", "pe", t_bc)
                    t_cp = b.op("dve", K.cp(self.rb_s, psA_[:, 0:512]),
                                inc="dve")
                    assert t_cp == cp_tok
                    b.wait("dve", "pe", pv15_old)
                    b.wait("dve", "sa%d" % (fss % 2), t_attn[fss % 2])
                    t_norm = b.op("dve", K.tt(self.attn_t[fss % 2], psPV,
                                              self.rb_s, OP.mult), inc="dve")
                    assert t_norm == norm_tok
                    b.wait("pool", "dve", t_norm)
                    t_attn[fss % 2] = b.op("pool", K.dma(
                        d["attn_sc"][:, fn2 * 4096 + fss * 512:
                                     fn2 * 4096 + (fss + 1) * 512],
                        self.attn_t[fss % 2]), inc="sa%d" % (fss % 2), n=16)
                    self.fpend = None
                for p in range(16 if has_F else 0):
                    b.wait("dve", "act", t_sin[p])
                    t_stt[p] = b.op("dve", K.stt(
                        self.mq16[p % 3],
                        self.s2bp[(p // 2) % 3][:, (p % 2) * 1024:
                                                (p % 2) * 1024 + 1024],
                        10.0, psB_[p % 2], OP.add, OP.mult), inc="dve")
                    assert t_stt[p] == pred_stt(p)
                    b.wait("dve", "pe", t_mmA[p])
                    b.wait("dve", "pe", t_pvs[p])
                    b.wait("dve", "act", t_exp[p])
                    t_tt[p] = b.op("dve", K.tt(self.ein[:, p * 1024:(p + 1) * 1024],
                                               psA_, self.mq16[p % 3], OP.add),
                                   inc="dve")
                    assert t_tt[p] == pred_tt(p)
                for j in range(2) if has_F else ():
                    emit_equad(i, j, t_tt[4 * j + 3], t_pvs[4 * j + 3])
                if has_B:
                    b.wait("dve", "pe", t_pvs[15])
                    t_recip = b.op("dve",
                                   lambda e, o=self.rb_s[0:1, 0:512], i_=psS:
                                   e.reciprocal(o, i_), inc="dve")
                    self.fpend = (n, i - 1)
            fn2, fss = self.fpend
            b.wait("pe", "dve", t_recip)
            b.wait("pe", "dve", t_tt[15])
            t_bc = b.op("pe", K.mm(psA_[:, 0:512], self.ones32[0:1, 0:128],
                                   self.rb_s[0:1, 0:512]), inc="pe")
            b.wait("dve", "pe", t_bc)
            t_cp = b.op("dve", K.cp(self.rb_s, psA_[:, 0:512]), inc="dve")
            b.wait("dve", "pe", t_pvs[15])
            b.wait("dve", "sa%d" % (fss % 2), t_attn[fss % 2])
            t_norm = b.op("dve", K.tt(self.attn_t[fss % 2], psPV, self.rb_s,
                                      OP.mult), inc="dve")
            b.wait("pool", "dve", t_norm)
            t_attn[fss % 2] = b.op("pool", K.dma(
                d["attn_sc"][:, fn2 * 4096 + fss * 512:
                             fn2 * 4096 + (fss + 1) * 512],
                self.attn_t[fss % 2]), inc="sa%d" % (fss % 2), n=16)
            self.fpend = None
            t_pe_flash = self.cnt["pe"]

    def _cprep(self):
        b, d = self, self.d
        b.wait("syncq", "sa0", self.cnt["sa0"])
        b.wait("syncq", "sa1", self.cnt["sa1"])
        b.wait("syncq", "sg", self.cnt["sg"])
        b.wait("syncq", "sx0", self.cnt["sx0"])
        b.wait("syncq", "sx1", self.cnt["sx1"])
        t_wo = 0
        for c in range(8):
            tok = b.dld("dw0", ("pool", t_wo),
                        K.dma(self.Dv(0, 1024), d["Wo"][c * 128:(c + 1) * 128, :]))
            b.wait("pool", "dw0", tok)
            t_wo = b.op("pool", K.cp(self.arW[:, c * 1024:(c + 1) * 1024],
                                     self.Dv(0, 1024)), inc="pool")
        self.t_wo16 = self.cnt["pool"]
        rb = self.arD[0:1, 2048:3072]
        for nm, dst, half in [("g1", self.g1bc, False), ("g2", self.g2bc, True),
                              ("beta2", self.b2bc, True)]:
            tok = b.dil(K.dma(rb, d[nm][None, :]))
            b.wait("dve", "di", tok)
            if half:
                b.op("dve", K.ts(rb, rb, 0.5))
            trb = b.op("dve", lambda e: e.nop(), inc="dve")
            b.wait("pe", "dve", trb)
            tok2 = None
            for hh in range(2):
                tok2 = b.op("pe", K.mm(self.PSF[:, hh * 512:(hh + 1) * 512],
                                       self.ones32[0:1, 0:128],
                                       rb[0:1, hh * 512:(hh + 1) * 512]),
                            inc="pe" if hh == 1 else None)
            b.di_done("pe", tok2)
            b.wait("dve", "pe", tok2)
            b.op("dve", K.cp(dst, self.PSF[:, 0:1024]), inc="dve")
        for nm, dst in [("bo", self.boC), ("beta1", self.be1C)]:
            tok = b.dil(K.dma(rb, d[nm][None, :]))
            b.wait("dve", "di", tok)
            td = b.op("dve", K.cp(dst, rb), inc="dve")
            b.di_done("dve", td)
        for r in range(2):
            b.wait("pe", "dve", self.cnt["dve"])
            tok = None
            for j in range(4):
                cidx = r * 4 + j
                tok = b.op("pe", K.tp(self.PST[0:1, j * 128:(j + 1) * 128],
                                      self.col("bf2c", cidx, 1),
                                      self.ident32[:, :]),
                           inc="pe" if j == 3 else None)
            b.wait("dve", "pe", tok)
            b.op("dve", K.cp(self.bf2C[0:1, r * 512:(r + 1) * 512],
                             K.ap(self.PST, 0, [[512, 1], [128, 4], [1, 128]])),
                 inc="dve")
        for r in range(2):
            b.wait("pe", "dve", self.cnt["dve"])
            tok = None
            for j in range(4):
                fc = r * 4 + j
                tok = b.op("pe", K.tp(self.PST[0:8, j * 128:(j + 1) * 128],
                                      self.w2ftc[:, fc * 8:(fc + 1) * 8],
                                      self.ident32[:, :]),
                           inc="pe" if j == 3 else None)
            b.wait("dve", "pe", tok)
            b.op("dve", K.cp(self.W2f16[0:8, r * 512:(r + 1) * 512],
                             K.ap(self.PST, 0, [[512, 8], [128, 4], [1, 128]])),
                 inc="dve")
        self.tCP = b.op("dve", lambda e: e.nop(), inc="dve")

    def _phase_c(self):
        b, d = self, self.d
        psWO = [self.PSF[:, 0:1024], self.PSF[:, 1024:2048]]
        psQM = self.PSF[:, 2048:2056]
        psFF = self.PSF[:, 2560:3584]
        xnld = [self.Dv(3072, 1024), self.Dv(4096, 1024)]
        xld = [self.Dv(0, 1024), self.Dv(1024, 1024)]
        gld = [self.Dv(2048, 512), self.Dv(5120, 1024)]
        gld = [self.arD[:, 2048:3072], self.arD[:, 5120:6144]]
        sm = self.small32
        t_pre = [0, 0]
        t_sh = [0, 0]
        t_out = [0, 0]
        t_ld = [0, 0]
        t_wo = [0, 0]
        t_xn = [0, 0]
        t_x1 = [0] * 8
        t_x1T = [0] * 8
        t_qeT = [0] * 8
        self.t_ff_last = 0
        for gi in range(4):
            b.wait("dve", "pool", self.cnt["pool"])
            for t in range(8):
                T = gi * 8 + t
                xb = t % 2
                tok_a = b.dld("da%d" % xb, ("pe", t_wo[xb]),
                              K.dma(self.astg[xb],
                                    K.ap(d["attn_sc"].tensor, T * 128,
                                         [[32768, 128], [4096, 8], [1, 128]])))
                tok_x = b.dld("dn%d" % xb, ("dve", t_xn[xb]),
                              K.dma(xnld[xb],
                                    d["xn_sc"][T * 128:(T + 1) * 128, :]))
                b.wait("pe", "da%d" % xb, tok_a)
                if t_pre[xb]:
                    b.wait("pe", "dve", t_pre[xb])
                tok = None
                for c in range(8):
                    for hh in range(2):
                        b.op("pe", K.mm(
                            psWO[xb][:, hh * 512:(hh + 1) * 512],
                            self.astg[xb][:, c * 128:(c + 1) * 128],
                            self.arW[:, c * 1024 + hh * 512:
                                     c * 1024 + (hh + 1) * 512],
                            start=(c == 0), stop=False))
                for hh in range(2):
                    tok = b.op("pe", K.mm(psWO[xb][:, hh * 512:(hh + 1) * 512],
                                          self.ones16[0:1, 0:128],
                                          self.boC[0:1, hh * 512:(hh + 1) * 512],
                                          start=False, stop=True),
                               inc="pe" if hh == 1 else None)
                t_wo[xb] = tok
                b.wait("dve", "pe", tok)
                b.wait("dve", "dn%d" % xb, tok_x)
                t_xn[xb] = b.op("dve", K.tt(self.preb[xb], psWO[xb], xnld[xb],
                                            OP.add), inc="dve")
                b.wait("act", "dve", t_xn[xb])
                b.op("act", K.act(self.trashC, self.preb[xb], AF.Square,
                                  accum=sm[:, 24 + xb:25 + xb]))
                ta = b.op("act", K.act(self.trashC, self.preb[xb], AF.Copy,
                                       accum=sm[:, 26 + xb:27 + xb]), inc="act")
                b.wait("dve", "act", ta)
                b.op("dve", K.ts(sm[:, 28:29], sm[:, 26 + xb:27 + xb], 1.0 / 1024.0))
                b.op("dve", K.ts(sm[:, 29:30], sm[:, 24 + xb:25 + xb], 1.0 / 1024.0))
                b.op("dve", K.tt(sm[:, 30:31], sm[:, 28:29], sm[:, 28:29], OP.mult))
                td = b.op("dve", K.tt(sm[:, 31:32], sm[:, 29:30], sm[:, 30:31],
                                      OP.subtract), inc="dve")
                b.wait("act", "dve", td)
                b.op("act", K.act(sm[:, 31:32], sm[:, 31:32], AF.Ln, bias=self.cc[:, 0:1]))
                ta2 = b.op("act", K.act(sm[:, 32:33], sm[:, 31:32], AF.Exp,
                                        scale=-0.5), inc="act")
                b.wait("dve", "act", ta2)
                t_x1[t] = b.op("dve", K.ts(self.x1a(t), self.preb[xb], sm[:, 28:29],
                                           sm[:, 32:33], op0=OP.subtract,
                                           op1=OP.mult), inc="dve")
                t_pre[xb] = t_x1[t]
                b.wait("pe", "dve", t_x1[t])
                for r in range(2):
                    tok = None
                    for c4 in range(4):
                        c = 4 * r + c4
                        tok = b.op("pe", K.tp(
                            self.PST[:, c4 * 128:(c4 + 1) * 128],
                            self.f32ap(self.arA, t * 1024 + c * 128,
                                       [[16384, 128], [1, 128]]),
                            self.ident32[:, :]), inc="pe" if c4 == 3 else None)
                    b.wait("dve", "pe", tok)
                    t_x1T[t] = b.op("dve", K.cp(
                        self.x1aT(t)[:, r * 512:(r + 1) * 512],
                        self.PST[:, 0:512]), inc="dve")
                    if r == 0:
                        b.wait("pe", "dve", t_x1T[t])
            b.wait("act", "pe", self.cnt["pe"])
            for t in range(8):
                b.wait("pe", "dve", t_x1T[t])
                for c in range(8):
                    b.op("pe", K.mm(psQM, self.x1aT(t)[:, c * 128:(c + 1) * 128],
                                    self.Wf16[:, c * 8:(c + 1) * 8],
                                    start=(c == 0), stop=False))
                tok = b.op("pe", K.mm(psQM, self.ones16[0:1, 0:128],
                                      self.rows16[0:1, 0:8],
                                      start=False, stop=True), inc="pe")
                b.wait("act", "pe", tok)
                b.op("act", K.act(sm[:, 33:41], psQM, AF.Tanh))
                ta = b.op("act", K.act(sm[:, 41:49], psQM, AF.Abs,
                                       scale=PI / 4.0), inc="act")
                b.wait("dve", "act", ta)
                td = b.op("dve", K.ts(sm[:, 41:49], sm[:, 41:49], 1.5 * PI,
                                      op0=OP.min), inc="dve")
                b.wait("act", "dve", td)
                ta = b.op("act", K.act(sm[:, 41:49], sm[:, 41:49], AF.Sin,
                                       bias=self.cc[:, 2:3], scale=-1.0), inc="act")
                b.wait("dve", "act", ta)
                tq = b.op("dve", K.tt(sm[:, 41:49], sm[:, 33:41], sm[:, 41:49],
                                      OP.mult), inc="dve")
                b.wait("pe", "dve", tq)
                if t > 0:
                    b.wait("pe", "act", t_qeT[t - 1])
                tok = b.op("pe", K.tp(self.PST[0:8, 0:128], sm[:, 41:49],
                                      self.ident32[:, :]), inc="pe")
                b.wait("act", "pe", tok)
                t_qeT[t] = b.op("act", K.acp(self.qeT[0:8, t * 128:(t + 1) * 128],
                                             self.PST[0:8, 0:128]), inc="act")
            for t in range(8):
                T = gi * 8 + t
                xb = t % 2
                b.wait("pe", "act", t_qeT[t])
                if self.t_ff_last:
                    b.wait("pe", "dve", self.t_ff_last)
                tok = None
                for hh in range(2):
                    b.op("pe", K.mm(psFF[:, hh * 512:(hh + 1) * 512],
                                    self.qeT[0:8, t * 128:(t + 1) * 128],
                                    self.W2f16[0:8, hh * 512:(hh + 1) * 512],
                                    start=True, stop=False))
                    b.op("pe", K.mm(psFF[:, hh * 512:(hh + 1) * 512],
                                    self.ones16[0:1, 0:128],
                                    self.be1C[0:1, hh * 512:(hh + 1) * 512],
                                    start=False, stop=False))
                    tok = b.op("pe", K.mm(psFF[:, hh * 512:(hh + 1) * 512],
                                          self.ones16[0:1, 0:128],
                                          self.bf2C[0:1, hh * 512:(hh + 1) * 512],
                                          start=False, stop=True),
                               inc="pe" if hh == 1 else None)
                b.op("dve", K.tt(self.x1a(t), self.x1a(t), self.g1bc, OP.mult))
                b.wait("dve", "pe", tok)
                t_pre[xb] = b.op("dve", K.tt(self.preb[xb], psFF, self.x1a(t),
                                             OP.add), inc="dve")
                self.t_ff_last = t_pre[xb]
                b.wait("act", "dve", t_pre[xb])
                b.op("act", K.act(self.trashC, self.preb[xb], AF.Square,
                                  accum=sm[:, 24 + xb:25 + xb]))
                ta = b.op("act", K.act(self.trashC, self.preb[xb], AF.Copy,
                                       accum=sm[:, 26 + xb:27 + xb]), inc="act")
                b.wait("dve", "act", ta)
                b.op("dve", K.ts(sm[:, 28:29], sm[:, 26 + xb:27 + xb], 1.0 / 1024.0))
                b.op("dve", K.ts(sm[:, 29:30], sm[:, 24 + xb:25 + xb], 1.0 / 1024.0))
                b.op("dve", K.tt(sm[:, 30:31], sm[:, 28:29], sm[:, 28:29], OP.mult))
                td = b.op("dve", K.tt(sm[:, 31:32], sm[:, 29:30], sm[:, 30:31],
                                      OP.subtract), inc="dve")
                b.wait("act", "dve", td)
                b.op("act", K.act(sm[:, 31:32], sm[:, 31:32], AF.Ln, bias=self.cc[:, 0:1]))
                ta2 = b.op("act", K.act(sm[:, 32:33], sm[:, 31:32], AF.Exp,
                                        scale=-0.5), inc="act")
                b.wait("dve", "act", ta2)
                ts2 = b.op("dve", K.ts(self.x1a(t), self.preb[xb], sm[:, 28:29],
                                       sm[:, 32:33], op0=OP.subtract, op1=OP.mult),
                           inc="dve")
                b.wait("pool", "dve", ts2)
                if t_sh[xb]:
                    b.wait("pool", "dve", t_ld[xb])
                b.op("pool", K.tt(self.shb[xb], self.x1a(t), self.g2bc, OP.mult))
                t_sh[xb] = b.op("pool", K.tt(self.shb[xb], self.shb[xb], self.b2bc,
                                             OP.add), inc="pool")
                tok_x = b.dld("dxx%d" % xb, ("dve", t_ld[xb]),
                              K.dma(xld[xb], d["x"][T * 128:(T + 1) * 128, :]))
                tok_g = b.dld("dgl%d" % xb, ("dve", t_ld[xb]),
                              K.dma(gld[xb], d["g_sc"][T * 128:(T + 1) * 128, :]))
                b.wait("dve", "pool", t_sh[xb])
                b.wait("dve", "dgl%d" % xb, tok_g)
                if t_out[xb]:
                    b.wait("dve", "so%d" % xb, t_out[xb])
                b.op("dve", K.tt(self.outb[xb], self.shb[xb], gld[xb], OP.mult))
                b.wait("dve", "dxx%d" % xb, tok_x)
                t_ld[xb] = b.op("dve", K.tt(self.outb[xb], self.outb[xb], xld[xb],
                                            OP.add), inc="dve")
                b.wait("pool", "dve", t_ld[xb])
                t_out[xb] = b.op("pool", K.dma(d["out"][T * 128:(T + 1) * 128, :],
                                               self.outb[xb]),
                                 inc="so%d" % xb, n=16)
        b.wait("pool", "so0", self.cnt["so0"])
        b.wait("pool", "so1", self.cnt["so1"])


def build_kernel(dbg=()):
    k = K(dbg)
    nc = k.build()
    n = sum(len(k.q[e]) for e in ENGS)
    return nc, n


def kernel(**kw):
    x = np.ascontiguousarray(kw["L0_attn_x"], dtype=np.float32)
    nc, _ = build_kernel()
    const = {"c_ident": np.eye(128, dtype=np.float32)}
    in_maps = []
    for core in range(8):
        m = {k2: np.ascontiguousarray(v, dtype=np.float32)
             for k2, v in kw.items() if k2 != "L0_attn_x"}
        m["L0_attn_x"] = x[core]
        m.update(const)
        in_maps.append(m)
    res = run_bass_kernel_spmd(nc, in_maps, core_ids=list(range(8)))
    return np.stack([res.results[c]["out"] for c in range(8)], axis=0)
